# version 07
# baseline (speedup 1.0000x reference)
import numpy as np
import ml_dtypes
import concourse.bass as bass
import concourse.mybir as mybir
from concourse.bass_utils import run_bass_kernel_spmd
from contextlib import ExitStack

F32 = mybir.dt.float32
BF16 = mybir.dt.bfloat16
I32 = mybir.dt.int32
AF = mybir.ActivationFunctionType
ALU = mybir.AluOpType
AX = mybir.AxisListType
bfloat16 = ml_dtypes.bfloat16

B, S, D = 2, 8192, 1024
NCH, CW, NT = 4, 2048, 8
NH = 16
EPS_LN, EPS_RMS = 1e-5, 1e-6
DEBUG = False
STAGE = 6


class Ctx:
    def __init__(self, nc, stack):
        self.nc, self.stack = nc, stack
        self.sems, self.cnt = {}, {}

    def sem(self, name):
        h = self.stack.enter_context(self.nc.semaphore(name))
        self.sems[name] = h
        self.cnt[name] = 0
        return h

    def inc(self, name, by=1):
        self.cnt[name] += by
        return self.sems[name]

    def dma(self, eng, out, in_, sem="ld", **kw):
        i = eng.dma_start(out=out, in_=in_, **kw)
        i.then_inc(self.inc(sem, 16), 16)
        return i

    def wait(self, eng, *names):
        for name in names:
            if self.cnt[name]:
                eng.wait_ge(self.sems[name], self.cnt[name])

    def tick(self, inst, name):
        inst.then_inc(self.inc(name, 1), 1)
        return inst


def _newton_rsqrt(v, r_ap, var_ap, tmp_ap):
    v.tensor_tensor(tmp_ap, r_ap, r_ap, ALU.mult)
    v.tensor_tensor(tmp_ap, tmp_ap, var_ap, ALU.mult)
    v.tensor_scalar(tmp_ap, tmp_ap, -0.5, 1.5, ALU.mult, ALU.add)
    return v.tensor_tensor(r_ap, r_ap, tmp_ap, ALU.mult)


def build_program():
    nc = bass.Bass("TRN2", target_bir_lowering=False, debug=False)
    E = {}

    def din(name, shape, dt=F32):
        E[name] = nc.dram_tensor(name, shape, dt, kind="ExternalInput")
        return E[name]

    din("x_ch", [D, S])
    din("dd", [1, D])
    din("cfg", [1, 64], I32)
    din("phi", [8, 8])
    din("gwT", [D, 24])
    din("gb", [1, 24])
    din("ident", [128, 128])
    din("wqkvT", [D, 3072])
    din("bqkv", [1, 3072])
    din("ln1w", [1, D])
    din("maskm", [128, 2048], BF16)
    din("posmask", [128, 18])
    din("outwT", [D, D])
    din("outb", [1, D])
    din("w1T", [D, 4096])
    din("b1", [1, 4096])
    din("w2T", [4096, D])
    din("b2", [1, D])
    din("ln2w", [1, D])
    din("fln1w", [1, D])
    din("fln1b", [1, D])
    din("fln2w", [1, D])
    din("fln2b", [1, D])
    din("agw", [D, 8])
    din("qw", [8, D, 128])
    din("qb", [8, 128])
    din("kw", [D, 128])
    din("kb", [1, 128])
    din("vw", [D, 128])
    din("vb", [1, 128])
    din("ow", [8, 128, D])
    din("ob", [8, D])
    din("relT", [128, 129])
    din("indl", [2, S], BF16)
    din("indh", [2, S], BF16)
    din("fgw", [D, 16])
    din("fw1", [16, D, 512])
    din("fb1", [16, 512])
    din("fw2", [16, 512, D])
    din("fb2", [16, D])

    out_ch = nc.dram_tensor("out_ch", [D, 2048], F32, kind="ExternalOutput").ap()
    dbg = {}
    if DEBUG:
        for nm, shape, dt in [
            ("xd_own", [D, 2048], F32), ("rows", [6, S], F32), ("xstate", [128, 8], F32),
            ("sblk", [D, 2048], F32), ("kv", [1024, 512], BF16), ("q", [1024, 512], BF16),
            ("small", [16, 64], F32), ("vtok", [128, 1280], BF16),
            ("ctxn", [128, 2048], BF16), ("sums", [16, 2048], F32),
            ("acc", [4 * 65, 1024], F32), ("pstage", [128, 3072], BF16),
            ("x2", [D, 2048], F32), ("gT", [8, 2048], BF16), ("qt", [128, 2048], BF16),
            ("kmix", [128, 1024], BF16), ("negce", [128, 8], F32), ("gctx", [128, 1024], BF16),
            ("xn3", [128, 2048], BF16), ("fgT", [16, 2048], BF16),
            ("acc2", [65, 1024], F32), ("Pa", [128, 1024], BF16),
            ("ab0", [65, 1024], F32), ("ab1", [65, 1024], F32),
        ]:
            dbg[nm] = nc.dram_tensor("dbg_" + nm, shape, dt, kind="ExternalOutput").ap()

    sp_xn = nc.dram_tensor("sp_xn", [D, S], F32)
    sp_xd = nc.dram_tensor("sp_xd", [D, S], F32)
    rows_dram = nc.dram_tensor("rows_dram", [6, S], F32)
    small_dram = nc.dram_tensor("small_dram", [16, 64], F32)
    k_hbm = nc.dram_tensor("k_hbm", [1024, S], BF16)
    q_hbm = nc.dram_tensor("q_hbm", [1024, 2048], BF16)
    vtok_hbm = nc.dram_tensor("vtok_hbm", [64, 128, 16 * 80], BF16)
    sblk_dram = nc.dram_tensor("sblk_dram", [D, 2048], F32)
    sk_hbm = nc.dram_tensor("sk_hbm", [9, 1024, 1024], BF16)
    sq_hbm = nc.dram_tensor("sq_hbm", [9, 1024, 1024], BF16)
    sv_hbm = nc.dram_tensor("sv_hbm", [9, 8 * 128, 1280], BF16)
    ctxacc = nc.dram_tensor("ctxacc", [8, 2, 2, 65, 1024], F32)
    sums_dram = nc.dram_tensor("sums_dram", [16, 2048], F32)
    xdown_dram = nc.dram_tensor("xdown_dram", [D, 2048], F32)
    xnown_dram = nc.dram_tensor("xnown_dram", [D, 2048], F32)
    xown_dram = nc.dram_tensor("xown_dram", [D, 2048], F32)
    kraw_hbm = nc.dram_tensor("kraw_hbm", [128, S], BF16)
    kmix_hbm = nc.dram_tensor("kmix_hbm", [2, 128, 2560 + S], BF16)
    vmoa_hbm = nc.dram_tensor("vmoa_hbm", [84, 128, 144], BF16)
    vpad_hbm = nc.dram_tensor("vpad_hbm", [8, 8, 128, 512], BF16)
    sm6_dram = nc.dram_tensor("sm6_dram", [1, 4096], F32)
    gmoa_dram = nc.dram_tensor("gmoa_dram", [8, 2048], F32)
    fg_dram = nc.dram_tensor("fg_dram", [16, 2048], F32)
    E.update(sp_xn=sp_xn, sp_xd=sp_xd, rows_dram=rows_dram, small_dram=small_dram,
             k_hbm=k_hbm, q_hbm=q_hbm, vtok_hbm=vtok_hbm, sblk_dram=sblk_dram,
             sk_hbm=sk_hbm, sq_hbm=sq_hbm, sv_hbm=sv_hbm, ctxacc=ctxacc,
             sums_dram=sums_dram, xdown_dram=xdown_dram, xnown_dram=xnown_dram,
             xown_dram=xown_dram, kraw_hbm=kraw_hbm, kmix_hbm=kmix_hbm, vmoa_hbm=vmoa_hbm,
             vpad_hbm=vpad_hbm, sm6_dram=sm6_dram, gmoa_dram=gmoa_dram, fg_dram=fg_dram,
             out_ch=out_ch, dbg=dbg)

    with ExitStack() as stack:
        ctx = Ctx(nc, stack)
        for sname in ["ld", "gd", "sd", "pe", "act", "dve", "st"]:
            ctx.sem(sname)
        sb = lambda n, s, d: stack.enter_context(nc.sbuf_tensor(n, s, d))

        big_all = sb("big_all", [128, NT * (CW + 1)], F32)
        big = [big_all[:, i * (CW + 1):(i + 1) * (CW + 1)] for i in range(NT)]
        ones_col = sb("ones_col", [128, 2], F32)
        onesb = sb("onesb", [128, 2], BF16)
        xstate = sb("xstate", [128, 8], F32)
        cfg_sb = sb("cfg_sb", [1, 64], I32)
        smalls = sb("smalls", [128, 64], F32)
        negc_bc = sb("negc_bc", [128, 16], F32)
        identp = sb("identp", [128, 128], F32)
        E_identp_loaded = [False]
        SD, SBETA, SCAR = 0, 8, 16

        block = stack.enter_context(nc.Block())

        E["identp_sb"] = identp

        @block.sync
        def _(sync):
            for i in range(NT):
                ctx.dma(sync, smalls[:, SD + i:SD + i + 1], bass.AP(E["dd"], 128 * i, [[1, 128], [1, 1]]))
            ctx.dma(sync, cfg_sb[:, :], E["cfg"].ap())
            ctx.dma(sync, identp[:, :], E["ident"].ap())

        @block.vector
        def _(v):
            v.memset(xstate[:, :], 0.0)
            v.memset(smalls[:, SCAR:SCAR + 8], 0.0)
            v.memset(ones_col[:, :], 1.0)
            v.memset(onesb[:, :], 1.0)
            for i in range(NT - 1):
                v.memset(big[i][:, 0:1], 0.0)
            ctx.tick(v.memset(big[NT - 1][:, 0:1], 0.0), "dve")

        @block.scalar
        def _(sc):
            ctx.wait(sc, "ld")
            ctx.tick(sc.activation(smalls[:, SBETA:SBETA + 8], smalls[:, SD:SD + 8], AF.Sigmoid), "act")

        _prep(nc, ctx, block, E, big, ones_col, xstate, smalls, SBETA, SCAR)

        if STAGE >= 2:
            _gates_mhc(nc, ctx, block, E, big, big_all, xstate, cfg_sb)
            _qkv(nc, ctx, block, E, big, cfg_sb, onesb, negc_bc)
        if STAGE >= 3:
            with ExitStack() as s34:
                ctxn = [s34.enter_context(nc.sbuf_tensor(f"ctxn{i}", [128, CW], BF16))
                        for i in range(NT)]
                E["ctxn_tiles"] = ctxn
                _mha(nc, ctx, block, E, big, big_all, cfg_sb, negc_bc)
                if STAGE >= 4:
                    E["ones_col_ref"] = ones_col
                    _attnffn(nc, ctx, block, E, big, big_all, cfg_sb)
        if STAGE >= 5:
            _moa(nc, ctx, block, E, big, big_all, cfg_sb, ones_col)
        if STAGE >= 6:
            _moe_final(nc, ctx, block, E, big, big_all, cfg_sb, ones_col)

        @block.gpsimd
        def _(g):
            for sname in ["ld", "gd", "sd", "pe", "act", "dve", "st"]:
                ctx.wait(g, sname)
            if DEBUG:
                rA = g.alloc_register("rA")
                rB = g.alloc_register("rB")
                g.reg_load(rA, cfg_sb[0:1, 0:1])
                g.reg_load(rB, cfg_sb[0:1, 1:2])
                for k, rr in ((0, rA), (1024, rB)):
                    ctx.dma(g, dbg["xd_own"].rearrange("(i p) t -> p i t", i=NT, p=128)[:, :, k:k + 1024],
                            bass.AP(E["sp_xd"], rr, [[S, 128], [128 * S, NT], [1, 1024]]), sem="gd")
                g.wait_ge(ctx.sems["gd"], ctx.cnt["gd"])
                ctx.dma(g, dbg["rows"], E["rows_dram"].ap(), sem="gd")
                ctx.dma(g, dbg["xstate"], xstate[:, :], sem="gd")
                if STAGE >= 2:
                    ctx.dma(g, dbg["sblk"], E["sblk_dram"].ap(), sem="gd")
                    ctx.dma(g, dbg["kv"], E["k_hbm"].ap()[:, 1024:1536], sem="gd")
                    ctx.dma(g, dbg["q"], E["q_hbm"].ap()[:, 512:1024], sem="gd")
                    ctx.dma(g, dbg["small"], E["small_dram"].ap(), sem="gd")
                    ctx.dma(g, dbg["vtok"], E["vtok_hbm"].ap()[7], sem="gd")
                if STAGE >= 3:
                    ctx.dma(g, dbg["acc"], E["ctxacc"].ap().rearrange("a b c d e -> (a b c d) e")[0:4 * 65, :], sem="gd")
                    ctx.dma(g, dbg["pstage"][:, 0:1024], E["sk_hbm"].ap()[1][0:128, :], sem="gd")
                    ctx.dma(g, dbg["pstage"][:, 1024:2048], E["sq_hbm"].ap()[7][0:128, :], sem="gd")
                    ctx.dma(g, dbg["pstage"][:, 2048:3072],
                            bass.AP(E["sv_hbm"], 1 * 8 * 128 * 1280, [[1280, 128], [128 * 1280, 8], [1, 128]]), sem="gd")
                if STAGE >= 5:
                    ctx.dma(g, dbg["gT"], E["gmoa_dram"].ap(), sem="gd")
            if STAGE < 6:
                ctx.dma(g, E["out_ch"], E["sp_xd"].ap()[:, 0:2048], sem="gd")
            g.wait_ge(ctx.sems["gd"], ctx.cnt["gd"])

    return nc


def _prep(nc, ctx, block, E, big, ones_col, xstate, smalls, SBETA, SCAR):
    sp_xn, sp_xd, rows_dram = E["sp_xn"], E["sp_xd"], E["rows_dram"]
    x_ch_h = E["x_ch"]
    MU, RR = 0, CW
    with ExitStack() as ps:
        psb = lambda n, s, d=F32: ps.enter_context(nc.sbuf_tensor(n, s, d))
        rows = psb("rows", [1, 4 * CW])
        rowtmp = psb("rowtmp", [1, 2 * CW])
        m_bc = psb("m_bc", [128, CW])
        r_bc = psb("r_bc", [128, CW])
        xsq = [psb(f"xsq{i}", [128, CW]) for i in range(2)]
        dtile = psb("dtile", [128, CW])
        ytile = [psb(f"ytile{i}", [128, CW]) for i in range(2)]
        ycols = psb("ycols", [128, NCH * NT])
        p_s1 = ps.enter_context(nc.psum_tensor("p_s1", [1, CW], F32))
        p_s2 = ps.enter_context(nc.psum_tensor("p_s2", [1, CW], F32))

        for c in range(NCH):
            t0 = c * CW

            @block.sync
            def _(sync, t0=t0):
                ctx.wait(sync, "ld", "dve", "pe")
                for i in range(NT):
                    ctx.dma(sync, big[i][:, 1:CW + 1], x_ch_h.ap()[128 * i:128 * (i + 1), t0:t0 + CW])

            for i in range(NT):
                @block.scalar
                def _(sc, i=i):
                    if i == 0:
                        ctx.wait(sc, "ld", "dve")
                    ctx.wait(sc, "pe")
                    ctx.tick(sc.activation(xsq[i % 2][:, :], big[i][:, 1:CW + 1], AF.Square), "act")

                @block.tensor
                def _(t, i=i):
                    if i == 0:
                        ctx.wait(t, "ld", "dve")
                    for s4 in range(4):
                        t.matmul(p_s1[:, 512 * s4:512 * (s4 + 1)], ones_col[:, 0:1],
                                 big[i][:, 1 + 512 * s4:1 + 512 * (s4 + 1)],
                                 start=(i == 0), stop=(i == NT - 1))
                    ctx.wait(t, "act")
                    for s4 in range(4):
                        mm = t.matmul(p_s2[:, 512 * s4:512 * (s4 + 1)], ones_col[:, 0:1],
                                      xsq[i % 2][:, 512 * s4:512 * (s4 + 1)],
                                      start=(i == 0), stop=(i == NT - 1))
                    ctx.tick(mm, "pe")

            @block.vector
            def _(v):
                ctx.wait(v, "pe")
                v.tensor_scalar(rows[:, MU:MU + CW], p_s1[:, :], 1.0 / D, None, ALU.mult)
                v.tensor_scalar(rowtmp[:, 0:CW], p_s2[:, :], 1.0 / D, None, ALU.mult)
                v.drain()
                v.tensor_tensor(rowtmp[:, CW:2 * CW], rows[:, MU:MU + CW], rows[:, MU:MU + CW], ALU.mult)
                v.drain()
                v.tensor_tensor(rowtmp[:, 0:CW], rowtmp[:, 0:CW], rowtmp[:, CW:2 * CW], ALU.subtract)
                ctx.tick(v.tensor_scalar(rowtmp[:, 0:CW], rowtmp[:, 0:CW], EPS_LN, None, ALU.add), "dve")

            @block.scalar
            def _(sc):
                ctx.wait(sc, "dve")
                ctx.tick(sc.activation(rowtmp[:, CW:2 * CW], rowtmp[:, 0:CW], AF.Sqrt), "act")

            @block.vector
            def _(v):
                ctx.wait(v, "act")
                v.reciprocal(rows[:, RR:RR + CW], rowtmp[:, CW:2 * CW])
                v.drain()
                ctx.tick(_newton_rsqrt(v, rows[:, RR:RR + CW], rowtmp[:, 0:CW], rowtmp[:, CW:2 * CW]), "dve")

            @block.sync
            def _(sync, t0=t0):
                ctx.wait(sync, "dve")
                ctx.dma(sync, bass.AP(rows_dram, 0 * S + t0, [[S, 1], [1, CW]]), rows[:, MU:MU + CW])
                ctx.dma(sync, bass.AP(rows_dram, 1 * S + t0, [[S, 1], [1, CW]]), rows[:, RR:RR + CW])
                sync.wait_ge(ctx.sems["ld"], ctx.cnt["ld"])
                ctx.dma(sync, m_bc[:, :], bass.AP(rows_dram, 0 * S + t0, [[0, 128], [1, CW]]))
                ctx.dma(sync, r_bc[:, :], bass.AP(rows_dram, 1 * S + t0, [[0, 128], [1, CW]]))

            for i in range(NT):
                @block.vector
                def _(v, c=c, i=i):
                    ctx.wait(v, "ld")
                    v.tensor_tensor(big[i][:, 1:CW + 1], big[i][:, 1:CW + 1], m_bc[:, :], ALU.subtract)
                    ctx.tick(v.tensor_tensor(big[i][:, 1:CW + 1], big[i][:, 1:CW + 1], r_bc[:, :], ALU.mult), "dve")
                    v.drain()
                    v.tensor_tensor(dtile[:, :], big[i][:, 1:CW + 1], big[i][:, 0:CW], ALU.subtract)
                    v.drain()
                    v.tensor_tensor_scan(ytile[i % 2][:, :],
                                         smalls[:, SBETA + i:SBETA + i + 1].broadcast_to([128, CW]),
                                         dtile[:, :], smalls[:, SCAR + i:SCAR + i + 1], ALU.mult, ALU.add)
                    v.drain()
                    v.tensor_copy(smalls[:, SCAR + i:SCAR + i + 1], ytile[i % 2][:, CW - 1:CW])
                    ctx.tick(v.reduce_sum(ycols[:, c * NT + i:c * NT + i + 1], ytile[i % 2][:, :], AX.X), "dve")

                @block.scalar
                def _(sc, i=i):
                    ctx.wait(sc, "dve", "pe")
                    ctx.tick(sc.activation(xsq[0][:, :], big[i][:, 1:CW + 1], AF.Square), "act")
                    ctx.tick(sc.activation(xsq[1][:, :], ytile[i % 2][:, :], AF.Square), "act")

                @block.tensor
                def _(t, i=i):
                    ctx.wait(t, "act")
                    for s4 in range(4):
                        t.matmul(p_s2[:, 512 * s4:512 * (s4 + 1)], ones_col[:, 0:1],
                                 xsq[0][:, 512 * s4:512 * (s4 + 1)],
                                 start=(i == 0), stop=(i == NT - 1))
                    for s4 in range(4):
                        mm = t.matmul(p_s1[:, 512 * s4:512 * (s4 + 1)], ones_col[:, 0:1],
                                      xsq[1][:, 512 * s4:512 * (s4 + 1)],
                                      start=(i == 0), stop=(i == NT - 1))
                    ctx.tick(mm, "pe")

                @block.sync
                def _(sync, i=i, t0=t0):
                    ctx.wait(sync, "dve")
                    ctx.dma(sync, bass.AP(sp_xn, i * 128 * S + t0, [[S, 128], [1, CW]]), big[i][:, 1:CW + 1])
                    ctx.dma(sync, bass.AP(sp_xd, i * 128 * S + t0, [[S, 128], [1, CW]]), ytile[i % 2][:, :])

            @block.vector
            def _(v):
                ctx.wait(v, "ld")
                for i in range(NT - 1):
                    v.tensor_copy(big[i][:, 0:1], big[i][:, CW:CW + 1])
                ctx.tick(v.tensor_copy(big[NT - 1][:, 0:1], big[NT - 1][:, CW:CW + 1]), "dve")

            for which in (0, 1):
                @block.vector
                def _(v, which=which):
                    if which == 0:
                        ctx.wait(v, "pe")
                        v.tensor_scalar(rowtmp[:, 0:CW], p_s1[:, :], 1.0 / D, None, ALU.mult)
                        v.drain()
                        ctx.tick(v.tensor_scalar(rowtmp[:, 0:CW], rowtmp[:, 0:CW],
                                                 EPS_RMS, None, ALU.add), "dve")
                    else:
                        v.tensor_scalar(rowtmp[:, 0:CW], p_s2[:, :], 1.0 / D, None, ALU.mult)
                        v.drain()
                        ctx.tick(v.tensor_scalar(rowtmp[:, 0:CW], rowtmp[:, 0:CW],
                                                 EPS_LN, None, ALU.add), "dve")

                @block.scalar
                def _(sc, which=which):
                    ctx.wait(sc, "dve")
                    ctx.tick(sc.activation(rowtmp[:, CW:2 * CW], rowtmp[:, 0:CW], AF.Sqrt), "act")

                @block.vector
                def _(v, which=which):
                    ctx.wait(v, "act")
                    RX = 2 * CW if which == 0 else 3 * CW
                    v.reciprocal(rows[:, RX:RX + CW], rowtmp[:, CW:2 * CW])
                    v.drain()
                    ctx.tick(_newton_rsqrt(v, rows[:, RX:RX + CW], rowtmp[:, 0:CW],
                                           rowtmp[:, CW:2 * CW]), "dve")

            @block.sync
            def _(sync, t0=t0):
                ctx.wait(sync, "dve")
                ctx.dma(sync, bass.AP(rows_dram, 2 * S + t0, [[S, 1], [1, CW]]), rows[:, 2 * CW:3 * CW])
                ctx.dma(sync, bass.AP(rows_dram, 3 * S + t0, [[S, 1], [1, CW]]), rows[:, 3 * CW:4 * CW])

        @block.vector
        def _(v):
            for c2 in range(NCH):
                v.drain()
                v.tensor_tensor(xstate[:, 0:NT], xstate[:, 0:NT], ycols[:, c2 * NT:(c2 + 1) * NT], ALU.add)
            v.drain()
            ctx.tick(v.tensor_copy(xstate[:, 0:NT], xstate[:, 0:NT]), "dve")


def _gates_mhc(nc, ctx, block, E, big, big_all, xstate, cfg_sb):
    sp_xd, small_dram, sblk_dram = E["sp_xd"], E["small_dram"], E["sblk_dram"]
    with ExitStack() as s2:
        sb2 = lambda n, s, d=F32: s2.enter_context(nc.sbuf_tensor(n, s, d))
        kH = sb2("kH", [8, 8])
        kHT = sb2("kHT", [8, 8])
        hrow = sb2("hrow", [8, 1])
        ident = sb2("identt", [128, 128])
        gwT_sb = sb2("gwT_sb", [128, 8 * 24])
        gcol = sb2("gcol", [24, 1])
        gbcol = sb2("gbcol", [24, 1])
        gres = sb2("gres", [8, 2])
        a_bc = sb2("a_bc", [128, 64])
        mt = [sb2(f"mt{n}", [128, 1024], BF16) for n in range(NT)]
        xdb_all = sb2("xdb_all", [128, NT * CW], BF16)
        xdb = [xdb_all[:, i * CW:(i + 1) * CW] for i in range(NT)]
        evt = sb2("evt", [128, 512])
        p_sk = s2.enter_context(nc.psum_tensor("p_sk", [8, 8], F32))
        p_g = s2.enter_context(nc.psum_tensor("p_g", [24, 1], F32))
        p_m = [s2.enter_context(nc.psum_tensor(f"p_m{i}", [128, 512], F32)) for i in range(2)]

        @block.sync
        def _(sync):
            ctx.dma(sync, kH[:, :], E["phi"].ap())
            ctx.dma(sync, ident[:, :], E["ident"].ap())
            ctx.dma(sync, gwT_sb[:, :], bass.AP(E["gwT"], 0, [[24, 128], [128 * 24, 8], [1, 24]]))
            ctx.dma(sync, gbcol[:, :], bass.AP(E["gb"], 0, [[1, 24], [1, 1]]))

        @block.scalar
        def _(sc):
            ctx.wait(sc, "ld")
            ctx.tick(sc.activation(kH[:, :], kH[:, :], AF.Exp), "act")

        for it in range(15):
            @block.vector
            def _(v, it=it):
                if it == 0:
                    ctx.wait(v, "act")
                else:
                    ctx.wait(v, "pe")
                    v.tensor_copy(kH[:, :], p_sk[:, :])
                    v.drain()
                v.reduce_sum(hrow[:, :], kH[:, :], AX.X)
                v.drain()
                v.reciprocal(hrow[:, :], hrow[:, :])
                v.drain()
                ctx.tick(v.tensor_scalar(kH[:, :], kH[:, :], hrow[:, :], None, ALU.mult), "dve")

            @block.tensor
            def _(t):
                ctx.wait(t, "dve")
                ctx.tick(t.matmul(p_sk[:, :], kH[:, :], ident[0:8, 0:8], is_transpose=True,
                                  start=True, stop=True), "pe")

            @block.vector
            def _(v):
                ctx.wait(v, "pe")
                v.tensor_copy(kHT[:, :], p_sk[:, :])
                v.drain()
                v.reduce_sum(hrow[:, :], kHT[:, :], AX.X)
                v.drain()
                v.reciprocal(hrow[:, :], hrow[:, :])
                v.drain()
                ctx.tick(v.tensor_scalar(kHT[:, :], kHT[:, :], hrow[:, :], None, ALU.mult), "dve")

        @block.tensor
        def _(t):
            ctx.wait(t, "dve")
            for i in range(NT):
                mm = t.matmul(p_g[:, :], gwT_sb[:, 24 * i:24 * (i + 1)], xstate[:, i:i + 1],
                              start=(i == 0), stop=(i == NT - 1))
            ctx.tick(mm, "pe")

        @block.vector
        def _(v):
            ctx.wait(v, "pe")
            ctx.tick(v.tensor_scalar(gcol[:, :], p_g[:, :], 1.0 / S, gbcol[:, :], ALU.mult, ALU.add), "dve")

        @block.sync
        def _(sync):
            ctx.wait(sync, "dve")
            ctx.dma(sync, bass.AP(small_dram, 10 * 64, [[1, 24], [1, 1]]), gcol[:, :])
            sync.wait_ge(ctx.sems["ld"], ctx.cnt["ld"])
            ctx.dma(sync, gres[:, 0:1], bass.AP(small_dram, 10 * 64 + 16, [[1, 8], [1, 1]]))

        @block.scalar
        def _(sc):
            ctx.wait(sc, "ld")
            ctx.tick(sc.activation(gres[:, 1:2], gres[:, 0:1], AF.Sigmoid), "act")

        @block.vector
        def _(v):
            ctx.wait(v, "act")
            ctx.tick(v.tensor_scalar(kHT[:, :], kHT[:, :], gres[:, 1:2], None, ALU.mult), "dve")

        @block.sync
        def _(sync):
            ctx.wait(sync, "dve")
            ctx.dma(sync, bass.AP(small_dram, 0, [[64, 8], [1, 8]]), kHT[:, :])
            sync.wait_ge(ctx.sems["ld"], ctx.cnt["ld"])
            ctx.dma(sync, a_bc[:, :], bass.AP(small_dram, 0, [[0, 128], [64, 8], [1, 8]]))

        @block.gpsimd
        def _(g):
            ctx.wait(g, "ld", "dve", "pe", "act")
            rA = g.alloc_register("r2A")
            rB = g.alloc_register("r2B")
            g.reg_load(rA, cfg_sb[0:1, 0:1])
            g.reg_load(rB, cfg_sb[0:1, 1:2])
            for k, rr in ((0, rA), (1024, rB)):
                ctx.dma(g, big_all[:, :].rearrange("p (i w) -> p i w", i=NT, w=CW + 1)[:, :, k:k + 1024],
                        bass.AP(sp_xd, rr, [[S, 128], [128 * S, NT], [1, 1024]]), sem="gd")
                ctx.dma(g, xdb_all[:, :].rearrange("p (i w) -> p i w", i=NT, w=CW)[:, :, k:k + 1024],
                        bass.AP(sp_xd, rr, [[S, 128], [128 * S, NT], [1, 1024]]), sem="gd")
            g.wait_ge(ctx.sems["gd"], ctx.cnt["gd"])

        @block.vector
        def _(v):
            ctx.wait(v, "ld")
            for n in range(NT):
                for m in range(NT):
                    mm = v.tensor_scalar(mt[n][:, 128 * m:128 * (m + 1)], ident[:, :],
                                         a_bc[:, 8 * n + m:8 * n + m + 1], None, ALU.mult)
            ctx.tick(mm, "dve")

        for m in range(NT):
            for ch in range(4):
                pp = (m * 4 + ch) % 2

                @block.tensor
                def _(t, m=m, ch=ch, pp=pp):
                    ctx.wait(t, "dve", "gd")
                    if m * 4 + ch >= 2:
                        ctx.wait(t, "dve")
                    for n in range(NT):
                        mm = t.matmul(p_m[pp][:, :], mt[n][:, 128 * m:128 * (m + 1)],
                                      xdb[n][:, 512 * ch:512 * (ch + 1)],
                                      start=(n == 0), stop=(n == NT - 1))
                    ctx.tick(mm, "pe")

                @block.vector
                def _(v, m=m, ch=ch, pp=pp):
                    ctx.wait(v, "pe", "ld")
                    ctx.tick(v.tensor_tensor(evt[:, :], p_m[pp][:, :],
                                             big[m][:, 512 * ch:512 * (ch + 1)], ALU.add), "dve")

                @block.sync
                def _(sync, m=m, ch=ch):
                    ctx.wait(sync, "dve")
                    ctx.dma(sync, bass.AP(sblk_dram, m * 128 * 2048 + 512 * ch, [[2048, 128], [1, 512]]),
                            evt[:, :])

        @block.sync
        def _(sync):
            sync.wait_ge(ctx.sems["ld"], ctx.cnt["ld"])


def _qkv(nc, ctx, block, E, big, cfg_sb, onesb, negc_bc):
    sp_xd, rows_dram, small_dram = E["sp_xd"], E["rows_dram"], E["small_dram"]
    k_hbm, q_hbm, vtok_hbm = E["k_hbm"], E["q_hbm"], E["vtok_hbm"]
    with ExitStack() as s3:
        sb3 = lambda n, s, d=F32: s3.enter_context(nc.sbuf_tensor(n, s, d))
        wq = [sb3(f"wq{i}", [128, 3072], BF16) for i in range(NT)]
        rhs = [sb3(f"rhs{i}", [128, 512], BF16) for i in range(NT)]
        xdf_all = sb3("xdf_all", [128, NT * 512])
        xdf = [xdf_all[:, i * 512:(i + 1) * 512] for i in range(NT)]
        r1bc = sb3("r1bc", [128, 512])
        stg = [sb3(f"stg{i}", [128, 512], BF16) for i in range(2)]
        sq = sb3("sq", [128, 512], BF16)
        vstg = [sb3(f"vstg{i}", [128, 1280], BF16) for i in range(2)]
        vb_bc = sb3("vb_bc", [128, 1024])
        ln1c = sb3("ln1c", [128, 8])
        bcol = sb3("bcol", [128, 24])
        h2 = sb3("h2", [128, 2], BF16)
        knq = sb3("knq", [2, 32])
        p_kv = [s3.enter_context(nc.psum_tensor(f"p_kv{i}", [128, 512], F32)) for i in range(2)]
        p_vt = [s3.enter_context(nc.psum_tensor(f"p_vt{i}", [128, 512], F32)) for i in range(2)]
        p_n = s3.enter_context(nc.psum_tensor("p_n", [2, 512], F32))

        @block.sync
        def _(sync):
            for i in range(NT):
                ctx.dma(sync, ln1c[:, i:i + 1], bass.AP(E["ln1w"], 128 * i, [[1, 128], [1, 1]]))
            for o in range(24):
                ctx.dma(sync, bcol[:, o:o + 1], bass.AP(E["bqkv"], 128 * o, [[1, 128], [1, 1]]))
            ctx.dma(sync, vb_bc[:, :], bass.AP(E["bqkv"], 2048, [[0, 128], [1, 1024]]))

        @block.gpsimd
        def _(g):
            ctx.wait(g, "ld")
            for i in range(NT):
                ctx.dma(g, wq[i][:, :], E["wqkvT"].ap()[128 * i:128 * (i + 1), :], sem="gd")

        @block.vector
        def _(v):
            ctx.wait(v, "ld", "gd")
            v.memset(h2[:, :], 0.0)
            v.memset(h2[0:64, 0:1], 1.0)
            v.memset(h2[64:128, 1:2], 1.0)
            v.memset(knq[:, 0:16], 0.0)
            v.memset(vstg[0][:, :], 1.0)
            v.memset(vstg[1][:, :], 1.0)
            v.tensor_scalar(bcol[:, 0:8], bcol[:, 0:8], 0.125, None, ALU.mult)
            for i in range(NT):
                mm = v.tensor_scalar(wq[i][:, :], wq[i][:, :], ln1c[:, i:i + 1], None, ALU.mult)
            ctx.tick(mm, "dve")

        nloop = [("kv", ch) for ch in range(16)] + [("q", ch) for ch in range(4)]

        for kind, ch in nloop:
            if kind == "kv":
                @block.sync
                def _(sync, ch=ch):
                    ctx.wait(sync, "pe", "dve")
                    ctx.dma(sync, r1bc[:, :], bass.AP(rows_dram, 2 * S + 512 * ch, [[0, 128], [1, 512]]))
                    for i in range(NT):
                        ctx.dma(sync, xdf[i][:, :],
                                bass.AP(sp_xd, i * 128 * S + 512 * ch, [[S, 128], [1, 512]]))
            else:
                @block.gpsimd
                def _(g, ch=ch):
                    ctx.wait(g, "pe", "dve")
                    rr = g.alloc_register(f"rq{ch}")
                    g.reg_load(rr, cfg_sb[0:1, 0:1] if ch < 2 else cfg_sb[0:1, 1:2])
                    ro = g.alloc_register(f"rqo{ch}")
                    g.reg_alu(ro, rr, (ch % 2) * 512, ALU.add)
                    rbase = g.alloc_register(f"rqb{ch}")
                    g.reg_alu(rbase, ro, 2 * S, ALU.add)
                    ctx.dma(g, r1bc[:, :], bass.AP(rows_dram, rbase, [[0, 128], [1, 512]]), sem="gd")
                    ctx.dma(g, xdf_all[:, :].rearrange("p (i w) -> p i w", i=NT, w=512),
                            bass.AP(sp_xd, ro, [[S, 128], [128 * S, NT], [1, 512]]), sem="gd")
                    g.wait_ge(ctx.sems["gd"], ctx.cnt["gd"])

            @block.vector
            def _(v, kind=kind):
                ctx.wait(v, "ld", "gd")
                for i in range(NT):
                    mm = v.tensor_tensor(rhs[i][:, :], xdf[i][:, :], r1bc[:, :], ALU.mult)
                ctx.tick(mm, "dve")

            ots = range(8, 16) if kind == "kv" else range(8)
            for ot in ots:
                pp = ot % 2

                @block.tensor
                def _(t, ot=ot, ch=ch, pp=pp, kind=kind):
                    ctx.wait(t, "dve", "act")
                    for i in range(NT):
                        mm = t.matmul(p_kv[pp][:, :], wq[i][:, 128 * ot:128 * (ot + 1)],
                                      rhs[i][:, :], start=(i == 0), stop=(i == NT - 1))
                    ctx.tick(mm, "pe")

                @block.vector
                def _(v, ot=ot, pp=pp, kind=kind):
                    ctx.wait(v, "pe", "ld")
                    ctx.tick(v.tensor_scalar(stg[pp][:, :], p_kv[pp][:, :],
                                             (0.125 if kind == "q" else 1.0), bcol[:, ot:ot + 1],
                                             ALU.mult, ALU.add), "dve")

                @block.scalar
                def _(sc, ot=ot, pp=pp, kind=kind):
                    ctx.wait(sc, "dve")
                    ctx.tick(sc.activation(sq[:, :], stg[pp][:, :], AF.Square), "act")

                @block.tensor
                def _(t):
                    ctx.wait(t, "act", "dve")
                    ctx.tick(t.matmul(p_n[:, :], h2[:, :], sq[:, :], start=True, stop=True), "pe")

                @block.vector
                def _(v, ot=ot, kind=kind):
                    ctx.wait(v, "pe")
                    col = (ot - 8) if kind == "kv" else (8 + ot)
                    v.reduce_max(knq[:, 16:17], p_n[:, :], AX.X)
                    v.drain()
                    ctx.tick(v.tensor_tensor(knq[:, col:col + 1], knq[:, col:col + 1],
                                             knq[:, 16:17], ALU.max), "dve")

                @block.sync
                def _(sync, ot=ot, ch=ch, pp=pp, kind=kind):
                    ctx.wait(sync, "dve")
                    if kind == "kv":
                        ctx.dma(sync, k_hbm.ap()[128 * (ot - 8):128 * (ot - 7), 512 * ch:512 * (ch + 1)],
                                stg[pp][:, :])
                    else:
                        ctx.dma(sync, q_hbm.ap()[128 * ot:128 * (ot + 1), 512 * ch:512 * (ch + 1)],
                                stg[pp][:, :])

            if kind == "kv":
                for tt in range(4):
                    kt = 4 * ch + tt
                    vs = vstg[kt % 2]
                    for hf in range(2):
                        @block.tensor
                        def _(t, tt=tt, hf=hf):
                            ctx.wait(t, "dve", "act")
                            for i in range(NT):
                                mm = t.matmul(p_vt[hf][:, :], rhs[i][:, 128 * tt:128 * (tt + 1)],
                                              wq[i][:, 2048 + 512 * hf:2048 + 512 * (hf + 1)],
                                              start=(i == 0), stop=(i == NT - 1))
                            ctx.tick(mm, "pe")

                        @block.vector
                        def _(v, hf=hf, vs=vs):
                            ctx.wait(v, "pe", "ld")
                            ctx.tick(v.tensor_tensor(
                                vs[:, :].rearrange("p (h w) -> p h w", h=16, w=80)[:, 8 * hf:8 * (hf + 1), 0:64],
                                p_vt[hf][:, :].rearrange("p (h d) -> p h d", h=8, d=64),
                                vb_bc[:, 512 * hf:512 * (hf + 1)].rearrange("p (h d) -> p h d", h=8, d=64),
                                ALU.add), "act")

                    @block.sync
                    def _(sync, kt=kt, vs=vs):
                        ctx.wait(sync, "act")
                        ctx.dma(sync, bass.AP(vtok_hbm, kt * 128 * 1280, [[1280, 128], [1, 1280]]),
                                vs[:, :])

        @block.scalar
        def _(sc):
            ctx.wait(sc, "dve")
            ctx.tick(sc.activation(knq[:, 16:32], knq[:, 0:16], AF.Sqrt), "act")

        @block.vector
        def _(v):
            ctx.wait(v, "act")
            v.tensor_tensor(knq[:, 0:8], knq[:, 16:24], knq[:, 24:32], ALU.mult)
            v.drain()
            ctx.tick(v.tensor_scalar(knq[:, 0:8], knq[:, 0:8], -1.0, None, ALU.mult), "dve")

        @block.sync
        def _(sync):
            ctx.wait(sync, "dve")
            ctx.dma(sync, bass.AP(small_dram, 8 * 64, [[1, 2], [2, 8]]), knq[:, 0:8],
                    allow_slow_non_contiguous=True)
            sync.wait_ge(ctx.sems["ld"], ctx.cnt["ld"])
            ctx.dma(sync, negc_bc[:, :], bass.AP(small_dram, 8 * 64, [[0, 128], [1, 16]]))
            sync.wait_ge(ctx.sems["ld"], ctx.cnt["ld"])


def _mha(nc, ctx, block, E, big, big_all, cfg_sb, negc_bc):
    sk_hbm, sq_hbm, sv_hbm = E["sk_hbm"], E["sq_hbm"], E["sv_hbm"]
    ctxacc, sums_dram = E["ctxacc"], E["sums_dram"]
    k_hbm, q_hbm, vtok_hbm = E["k_hbm"], E["q_hbm"], E["vtok_hbm"]
    with ExitStack() as s4:
        sb4 = lambda n, s, d=F32: s4.enter_context(nc.sbuf_tensor(n, s, d))
        negcp = sb4("negcp", [128, 8], F32)
        skt = [sb4(f"skt{i}", [128, 1024], BF16) for i in range(2)]
        sqt = [sb4(f"sqt{i}", [128, 1024], BF16) for i in range(2)]
        svt = [sb4(f"svt{i}", [128, 1280], BF16) for i in range(2)]
        Pa = sb4("Pa", [128, 1024], BF16)
        Pb = sb4("Pb", [128, 1024], BF16)
        mskm = sb4("mskm", [128, 2048], BF16)
        pmsk = sb4("pmsk", [128, 18])
        abuf = [sb4(f"abuf{i}", [65, 1024], F32) for i in range(4)]
        zeros = sb4("zeros", [128, 1024], F32)
        p_sc = [s4.enter_context(nc.psum_tensor(f"p_sc{i}", [128, 1024], F32)) for i in range(2)]
        p_cx = [s4.enter_context(nc.psum_tensor(f"p_cx{i}", [128, 1024], F32)) for i in range(2)]

        @block.sync
        def _(sync):
            ctx.wait(sync, "ld", "act", "dve")
            ctx.dma(sync, mskm[:, :], E["maskm"].ap())
            ctx.dma(sync, pmsk[:, :], E["posmask"].ap())
            r = sync.alloc_register("skr")
            for pos in range(9):
                sync.reg_load(r, cfg_sb[0:1, 8 + pos:9 + pos])
                ctx.dma(sync, sk_hbm.ap()[pos], bass.AP(k_hbm, r, [[S, 1024], [1, 1024]]))
            for pos in range(9):
                sync.reg_load(r, cfg_sb[0:1, 26 + pos:27 + pos])
                ctx.dma(sync, bass.AP(sv_hbm, pos * 8 * 128 * 1280, [[1, 1], [1, 8 * 128 * 1280]]),
                    bass.AP(vtok_hbm, r, [[1, 1], [1, 8 * 128 * 1280]]))

        @block.scalar
        def _(sc):
            ctx.wait(sc, "ld", "act", "dve")
            r = sc.alloc_register("sqr")
            for pos in range(9):
                sc.reg_load(r, cfg_sb[0:1, 17 + pos:18 + pos])
                ctx.dma(sc, sq_hbm.ap()[pos], bass.AP(q_hbm, r, [[2048, 1024], [1, 1024]]), sem="sd")

        @block.gpsimd
        def _(g):
            ctx.wait(g, "ld", "act", "dve", "sd")
            pass

        @block.vector
        def _(v):
            ctx.wait(v, "ld")
            for p8 in range(8):
                v.tensor_tensor(negcp[:, p8:p8 + 1], negc_bc[:, 2 * p8:2 * p8 + 1],
                                negc_bc[:, 2 * p8 + 1:2 * p8 + 2], ALU.min)
            ctx.tick(v.memset(zeros[:, :], 0.0), "dve")

        @block.gpsimd
        def _(g):
            ctx.wait(g, "dve")
            for t in range(32):
                ctx.dma(g, ctxacc.ap().rearrange("a b c d e -> (a b c) d e")[t][0:65, :],
                        zeros[0:65, :], sem="gd")
            g.wait_ge(ctx.sems["gd"], ctx.cnt["gd"])

        for pos in range(9):
            for pair in range(8):
                it = pos * 8 + pair
                db = it % 2

                @block.sync
                def _(sync, pos=pos, pair=pair, db=db, it=it):
                    if it < 2:
                        ctx.wait(sync, "gd", "sd", "ld")
                    else:
                        ctx.wait(sync, "pe")
                    ctx.dma(sync, skt[db][:, :], sk_hbm.ap()[pos][128 * pair:128 * (pair + 1), :])
                    ctx.dma(sync, sqt[db][:, :], sq_hbm.ap()[pos][128 * pair:128 * (pair + 1), :])
                    ctx.dma(sync, svt[db][:, :].rearrange("p (a w) -> p a w", a=8, w=160),
                            bass.AP(sv_hbm, pos * 8 * 128 * 1280 + 160 * pair,
                                    [[1280, 128], [128 * 1280, 8], [1, 160]]))

                for kt in range(8):
                    @block.tensor
                    def _(t, pos=pos, pair=pair, db=db, kt=kt, it=it):
                        if kt == 0:
                            ctx.wait(t, "ld")
                            ctx.wait(t, "act", "dve")
                        else:
                            ctx.wait(t, "act")
                            if pos >= 7:
                                ctx.wait(t, "dve")
                        for hf in range(2):
                            t.matmul(p_sc[0][:, 512 * hf:512 * (hf + 1)],
                                     skt[db][0:64, 128 * kt:128 * (kt + 1)],
                                     sqt[db][0:64, 512 * hf:512 * (hf + 1)],
                                     start=True, stop=True, tile_position=(0, 0))
                            mm = t.matmul(p_sc[1][:, 512 * hf:512 * (hf + 1)],
                                          skt[db][64:128, 128 * kt:128 * (kt + 1)],
                                          sqt[db][64:128, 512 * hf:512 * (hf + 1)],
                                          start=True, stop=True, tile_position=(64, 0))
                        ctx.tick(mm, "pe")

                    @block.scalar
                    def _(sc, pair=pair, kt=kt):
                        ctx.wait(sc, "pe")
                        sc.activation(Pa[:, :], p_sc[0][:, :], AF.Exp, bias=negcp[:, pair:pair + 1])
                        ctx.tick(sc.activation(Pb[:, :], p_sc[1][:, :], AF.Exp,
                                               bias=negcp[:, pair:pair + 1]), "act")

                    if pos >= 7:
                        @block.vector
                        def _(v, kt=kt):
                            ctx.wait(v, "act")
                            o = 1024 - 128 * kt
                            v.tensor_tensor(Pa[:, :], Pa[:, :], mskm[:, o:o + 1024], ALU.mult)
                            ctx.tick(v.tensor_tensor(Pb[:, :], Pb[:, :], mskm[:, o:o + 1024], ALU.mult), "dve")

                    @block.tensor
                    def _(t, pos=pos, db=db, kt=kt):
                        ctx.wait(t, "act")
                        if pos >= 7:
                            ctx.wait(t, "dve")
                        for hd in range(2):
                            P = (Pa, Pb)[hd]
                            for hf in range(2):
                                mm = t.matmul(p_cx[hd][0:65, 512 * hf:512 * (hf + 1)],
                                              svt[db][:, 160 * kt + 80 * hd:160 * kt + 80 * hd + 65],
                                              P[:, 512 * hf:512 * (hf + 1)],
                                              start=(kt == 0), stop=(kt == 7))
                        ctx.tick(mm, "pe")

                @block.vector
                def _(v, pos=pos, pair=pair):
                    ctx.wait(v, "pe", "gd")
                    v.tensor_scalar(abuf[0][:, :], p_cx[0][0:65, :], pmsk[0:65, pos:pos + 1], None, ALU.mult)
                    v.tensor_scalar(abuf[2][:, :], p_cx[1][0:65, :], pmsk[0:65, pos:pos + 1], None, ALU.mult)
                    v.tensor_scalar(abuf[1][:, :], p_cx[0][0:65, :], pmsk[0:65, 9 + pos:10 + pos], None, ALU.mult)
                    ctx.tick(v.tensor_scalar(abuf[3][:, :], p_cx[1][0:65, :], pmsk[0:65, 9 + pos:10 + pos],
                                             None, ALU.mult), "dve")

                if DEBUG and pos == 7 and pair == 0:
                    @block.sync
                    def _(sync):
                        ctx.wait(sync, "act", "dve")
                        ctx.dma(sync, E["dbg"]["Pa"], Pa[:, :])
                        ctx.dma(sync, E["dbg"]["ab0"], abuf[0][0:65, :])
                        ctx.dma(sync, E["dbg"]["ab1"], abuf[1][0:65, :])

                @block.gpsimd
                def _(g, pos=pos, pair=pair):
                    ctx.wait(g, "dve", "gd", "ld")
                    for hd in range(2):
                        for st in range(2):
                            ctx.dma(g, ctxacc.ap()[pair][hd][st][0:65, :], abuf[2 * hd + st][0:65, :],
                                    accum_op=ALU.add, sem="gd")

        ctxn = E["ctxn_tiles"]
        ctxf = sb4("ctxf", [128, 2048])
        rnbc = sb4("rnbc", [128, 2048])
        sums16 = sb4("sums16", [16, 2048])

        @block.gpsimd
        def _(g):
            ctx.wait(g, "gd")
            for st in range(2):
                ctx.dma(g, sums16[:, 1024 * st:1024 * (st + 1)],
                        bass.AP(ctxacc, 64 * 1024 + st * 65 * 1024, [[2 * 65 * 1024, 16], [1, 1024]]),
                        sem="gd")

        @block.vector
        def _(v):
            ctx.wait(v, "gd")
            ctx.tick(v.reciprocal(sums16[:, :], sums16[:, :]), "dve")

        @block.sync
        def _(sync):
            ctx.wait(sync, "dve")
            ctx.dma(sync, sums_dram.ap(), sums16[:, :])
            sync.wait_ge(ctx.sems["ld"], ctx.cnt["ld"])

        for pair in range(8):
            @block.sync
            def _(sync, pair=pair):
                ctx.wait(sync, "dve")
                ctx.dma(sync, rnbc[:, :],
                        bass.AP(sums_dram, 2 * pair * 2048, [[2048, 2], [0, 64], [1, 2048]]))

            @block.gpsimd
            def _(g, pair=pair):
                ctx.wait(g, "gd", "dve")
                for hd in range(2):
                    ctx.dma(g, ctxf[64 * hd:64 * (hd + 1), :].rearrange("p (s q) -> p s q", s=2, q=1024),
                            bass.AP(ctxacc, (pair * 2 + hd) * 2 * 65 * 1024,
                                    [[1024, 64], [65 * 1024, 2], [1, 1024]]), sem="gd")

            @block.vector
            def _(v, pair=pair):
                ctx.wait(v, "ld", "gd")
                ctx.tick(v.tensor_tensor(ctxn[pair][:, :], ctxf[:, :], rnbc[:, :], ALU.mult), "dve")

        if DEBUG:
            @block.sync
            def _(sync):
                ctx.wait(sync, "dve", "gd")
                ctx.dma(sync, E["dbg"]["acc2"], ctxacc.ap()[0][0][0][0:65, :])

        if DEBUG:
            @block.sync
            def _(sync):
                ctx.wait(sync, "dve")
                ctx.dma(sync, E["dbg"]["ctxn"], ctxn[0][:, :])
                ctx.dma(sync, E["dbg"]["sums"], sums16[:, :])


def _attnffn(nc, ctx, block, E, big, big_all, cfg_sb):
    sblk_dram, rows_dram = E["sblk_dram"], E["rows_dram"]
    ctxn = E["ctxn_tiles"]
    with ExitStack() as s5:
        sb5 = lambda n, s, d=F32: s5.enter_context(nc.sbuf_tensor(n, s, d))
        outw = [sb5(f"outw{i}", [128, 1024], BF16) for i in range(NT)]
        obcol = sb5("obcol", [128, 8])
        b2col = sb5("b2col", [128, 8])
        ln2c = sb5("ln2c", [128, 8])
        b1col = sb5("b1col", [128, 32])
        E["b1col_sb"] = b1col
        sbo = [sb5(f"sbo{i}", [128, 512]) for i in range(NT)]
        p_o = [s5.enter_context(nc.psum_tensor(f"p_o{i}", [128, 512], F32)) for i in range(2)]
        p_st = s5.enter_context(nc.psum_tensor("p_st", [1, 512], F32))

        @block.gpsimd
        def _(g):
            ctx.wait(g, "dve")
            for i in range(NT):
                ctx.dma(g, outw[i][:, :], E["outwT"].ap()[128 * i:128 * (i + 1), :], sem="gd")

        @block.sync
        def _(sync):
            for o in range(NT):
                ctx.dma(sync, obcol[:, o:o + 1], bass.AP(E["outb"], 128 * o, [[1, 128], [1, 1]]))
                ctx.dma(sync, b2col[:, o:o + 1], bass.AP(E["b2"], 128 * o, [[1, 128], [1, 1]]))
                ctx.dma(sync, ln2c[:, o:o + 1], bass.AP(E["ln2w"], 128 * o, [[1, 128], [1, 1]]))
            for o in range(32):
                ctx.dma(sync, b1col[:, o:o + 1], bass.AP(E["b1"], 128 * o, [[1, 128], [1, 1]]))

        for dt in range(NT):
            for ch in range(4):
                pp = (dt * 4 + ch) % 2

                @block.sync
                def _(sync, dt=dt, ch=ch):
                    if dt == 0 and ch == 0:
                        pass
                    ctx.wait(sync, "dve", "ld")
                    ctx.dma(sync, sbo[dt][:, :],
                            bass.AP(sblk_dram, dt * 128 * 2048 + 512 * ch, [[2048, 128], [1, 512]]))

                @block.tensor
                def _(t, dt=dt, ch=ch, pp=pp):
                    if dt == 0 and ch == 0:
                        ctx.wait(t, "gd", "dve")
                    else:
                        ctx.wait(t, "act", "dve")
                    for i in range(NT):
                        mm = t.matmul(p_o[pp][:, :], outw[i][:, 128 * dt:128 * (dt + 1)],
                                      ctxn[i][:, 512 * ch:512 * (ch + 1)],
                                      start=(i == 0), stop=(i == NT - 1))
                    ctx.tick(mm, "pe")

                @block.vector
                def _(v, dt=dt, ch=ch, pp=pp):
                    ctx.wait(v, "pe", "ld")
                    sl = slice(512 * ch, 512 * (ch + 1))
                    v.scalar_tensor_tensor(sbo[dt][:, :], p_o[pp][:, :], obcol[:, dt:dt + 1],
                                           sbo[dt][:, :], ALU.add, ALU.add)
                    v.drain()
                    ctx.tick(v.scalar_tensor_tensor(big[dt][:, sl], p_o[pp][:, :], obcol[:, dt:dt + 1],
                                                    big[dt][:, sl], ALU.add, ALU.add), "dve")

                @block.sync
                def _(sync, dt=dt, ch=ch):
                    ctx.wait(sync, "dve")
                    ctx.dma(sync, bass.AP(sblk_dram, dt * 128 * 2048 + 512 * ch, [[2048, 128], [1, 512]]),
                            sbo[dt][:, :])

        nrm = [sb5(f"nrm{i}", [128, 512], BF16) for i in range(NT)]
        htile = sb5("htile", [128, 32 * 512], BF16)
        w1g = [sb5(f"w1g{i}", [128, 8 * 128], BF16) for i in range(NT)]
        w2g = sb5("w2g", [128, 4096], BF16)
        rowf = sb5("rowf", [1, 3 * 512])
        r2bc = sb5("r2bc", [128, 512])
        sq5 = sb5("sq5", [128, 512])

        for ch in range(4):
            for i in range(NT):
                @block.scalar
                def _(sc, i=i, ch=ch):
                    ctx.wait(sc, "dve") if i == 0 else None
                    ctx.wait(sc, "pe")
                    ctx.tick(sc.activation(sq5[:, :], big[i][:, 512 * ch:512 * (ch + 1)], AF.Square), "act")

                @block.tensor
                def _(t, i=i):
                    ctx.wait(t, "act")
                    mm = t.matmul(p_st[:, :], E["ones_col_ref"][:, 0:1], sq5[:, :],
                                  start=(i == 0), stop=(i == NT - 1))
                    ctx.tick(mm, "pe")

            @block.vector
            def _(v):
                ctx.wait(v, "pe")
                v.tensor_scalar(rowf[:, 0:512], p_st[:, :], 1.0 / D, EPS_RMS, ALU.mult, ALU.add)
                ctx.tick(v.tensor_copy(rowf[:, 512:1024], rowf[:, 0:512]), "dve")

            @block.scalar
            def _(sc):
                ctx.wait(sc, "dve")
                ctx.tick(sc.activation(rowf[:, 1024:1536], rowf[:, 0:512], AF.Sqrt), "act")

            @block.vector
            def _(v):
                ctx.wait(v, "act")
                v.reciprocal(rowf[:, 512:1024], rowf[:, 1024:1536])
                v.drain()
                ctx.tick(_newton_rsqrt(v, rowf[:, 512:1024], rowf[:, 0:512], rowf[:, 1024:1536]), "dve")

            @block.sync
            def _(sync, ch=ch):
                ctx.wait(sync, "dve")
                ctx.dma(sync, bass.AP(rows_dram, 4 * S, [[S, 1], [1, 512]]), rowf[:, 512:1024])
                sync.wait_ge(ctx.sems["ld"], ctx.cnt["ld"])
                ctx.dma(sync, r2bc[:, :], bass.AP(rows_dram, 4 * S, [[0, 128], [1, 512]]))

            for i in range(NT):
                @block.vector
                def _(v, i=i, ch=ch):
                    if i == 0:
                        ctx.wait(v, "ld")
                    ctx.tick(v.scalar_tensor_tensor(nrm[i][:, :], big[i][:, 512 * ch:512 * (ch + 1)],
                                                    ln2c[:, i:i + 1], r2bc[:, :], ALU.mult, ALU.mult), "dve")

            for hg in range(4):
                @block.gpsimd
                def _(g, hg=hg, ch=ch):
                    ctx.wait(g, "pe")
                    for i in range(NT):
                        ctx.dma(g, w1g[i][:, :], E["w1T"].ap()[128 * i:128 * (i + 1),
                                                               1024 * hg:1024 * (hg + 1)], sem="gd")

                for hl in range(8):
                    ht = hg * 8 + hl
                    pp = ht % 2

                    @block.tensor
                    def _(t, hg=hg, hl=hl, pp=pp, ht=ht):
                        ctx.wait(t, "gd", "dve")
                        if ht >= 2:
                            ctx.wait(t, "act")
                        for i in range(NT):
                            mm = t.matmul(p_o[pp][:, :], w1g[i][:, 128 * hl:128 * (hl + 1)],
                                          nrm[i][:, :], start=(i == 0), stop=(i == NT - 1))
                        ctx.tick(mm, "pe")

                    @block.scalar
                    def _(sc, ht=ht, pp=pp):
                        ctx.wait(sc, "pe")
                        ctx.tick(sc.activation(htile[:, 512 * ht:512 * (ht + 1)], p_o[pp][:, :],
                                               AF.Gelu, bias=b1col[:, ht:ht + 1]), "act")

            for dt in range(NT):
                pp = dt % 2

                @block.gpsimd
                def _(g, dt=dt, ch=ch):
                    ctx.wait(g, "pe")
                    ctx.dma(g, w2g[:, :],
                            bass.AP(E["w2T"], 128 * dt, [[1024, 128], [128 * 1024, 32], [1, 128]]),
                            sem="gd")

                @block.sync
                def _(sync, dt=dt, ch=ch):
                    ctx.wait(sync, "dve", "ld")
                    ctx.dma(sync, sbo[dt][:, :],
                            bass.AP(sblk_dram, dt * 128 * 2048 + 512 * ch, [[2048, 128], [1, 512]]))

                @block.tensor
                def _(t, dt=dt, pp=pp):
                    ctx.wait(t, "gd", "act", "dve")
                    for ht in range(32):
                        mm = t.matmul(p_o[pp][:, :], w2g[:, 128 * ht:128 * (ht + 1)],
                                      htile[:, 512 * ht:512 * (ht + 1)],
                                      start=(ht == 0), stop=(ht == 31))
                    ctx.tick(mm, "pe")

                @block.vector
                def _(v, dt=dt, ch=ch, pp=pp):
                    ctx.wait(v, "pe", "ld")
                    ctx.tick(v.scalar_tensor_tensor(sbo[dt][:, :], p_o[pp][:, :], b2col[:, dt:dt + 1],
                                                    sbo[dt][:, :], ALU.add, ALU.add), "dve")

                @block.sync
                def _(sync, dt=dt, ch=ch):
                    ctx.wait(sync, "dve")
                    ctx.dma(sync, bass.AP(sblk_dram, dt * 128 * 2048 + 512 * ch, [[2048, 128], [1, 512]]),
                            sbo[dt][:, :])



QS = 0.08838834764831845


def _moa(nc, ctx, block, E, big, big_all, cfg_sb, ones_col):
    sp_xn, rows_dram, sm6 = E["sp_xn"], E["rows_dram"], E["sm6_dram"]
    kraw_hbm, kmix_hbm, vmoa_hbm = E["kraw_hbm"], E["kmix_hbm"], E["vmoa_hbm"]
    with ExitStack() as s6:
        sb6 = lambda n, s, d=F32: s6.enter_context(nc.sbuf_tensor(n, s, d))
        qt = [sb6(f"qt{e}", [128, 2048], BF16) for e in range(NT)]
        gates_sb = sb6("gates_sb", [128, 128])
        gT = sb6("gT", [8, 2048], BF16)
        relTb = sb6("relTb", [128, 129], BF16)
        tmpA = sb6("tmpA", [128, 512])

        with ExitStack() as sp6:
            sbp = lambda n, s, d=F32: sp6.enter_context(nc.sbuf_tensor(n, s, d))
            kwt = [sbp(f"kwt{i}", [128, 128], BF16) for i in range(NT)]
            vwt = [sbp(f"vwt{i}", [128, 128], BF16) for i in range(NT)]
            agwt = [sbp(f"agwt{i}", [128, 8], BF16) for i in range(NT)]
            qwt = [sbp(f"qwt{e}", [128, 8 * 128], BF16) for e in range(NT)]
            xnb = [sbp(f"xnb{i}", [128, 512], BF16) for i in range(NT)]
            relT_sb = sbp("relT_sb", [128, 129])
            cols6 = sbp("cols6", [128, 32])
            fl1b_bf = sbp("fl1b_bf", [128, 8], BF16)
            ucol = sbp("ucol", [128, 8])
            u0q_c = sbp("u0q_c", [128, 8])
            u0v_bc = sbp("u0v_bc", [128, 129], BF16)
            u0g_bc = sbp("u0g_bc", [128, 8])
            qbcol = sbp("qbcol", [128, 8])
            r2bc = sbp("r2bc6", [128, 512])
            r2col = sbp("r2col", [128, 4])
            kstg = sbp("kstg", [128, 512], BF16)
            vstg = sbp("vstg", [128, 144], BF16)
            zero6 = sbp("zero6", [128, 1280], BF16)
            lgt = sbp("lgt", [128, 32])
            p_a = [sp6.enter_context(nc.psum_tensor(f"p_a{i}", [128, 512], F32)) for i in range(2)]
            p_b = sp6.enter_context(nc.psum_tensor("p_b", [128, 144], F32))

            CFL1W, CFL1B, CKB, CVB, CFL1W8 = 0, 8, 16, 17, 18

            @block.sync
            def _(sync):
                ctx.wait(sync, "dve", "act", "pe")
                for i in range(NT):
                    ctx.dma(sync, cols6[:, CFL1W + i:CFL1W + i + 1], bass.AP(E["fln1w"], 128 * i, [[1, 128], [1, 1]]))
                    ctx.dma(sync, cols6[:, CFL1B + i:CFL1B + i + 1], bass.AP(E["fln1b"], 128 * i, [[1, 128], [1, 1]]))
                ctx.dma(sync, cols6[:, CKB:CKB + 1], bass.AP(E["kb"], 0, [[1, 128], [1, 1]]))
                ctx.dma(sync, cols6[:, CVB:CVB + 1], bass.AP(E["vb"], 0, [[1, 128], [1, 1]]))
                for e in range(NT):
                    ctx.dma(sync, qbcol[:, e:e + 1], bass.AP(E["qb"], 128 * e, [[1, 128], [1, 1]]))
                ctx.dma(sync, relT_sb[:, 0:129], E["relT"].ap())

            @block.gpsimd
            def _(g):
                ctx.wait(g, "ld", "dve", "act", "pe")
                for i in range(NT):
                    ctx.dma(g, kwt[i][:, :], E["kw"].ap()[128 * i:128 * (i + 1), :], sem="gd")
                    ctx.dma(g, vwt[i][:, :], E["vw"].ap()[128 * i:128 * (i + 1), :], sem="gd")
                    ctx.dma(g, agwt[i][:, :], E["agw"].ap()[128 * i:128 * (i + 1), :], sem="gd")
                for e in range(NT):
                    for i in range(NT):
                        ctx.dma(g, qwt[e][:, 128 * i:128 * (i + 1)],
                                E["qw"].ap()[e][128 * i:128 * (i + 1), :], sem="gd")

            @block.vector
            def _(v):
                ctx.wait(v, "ld", "gd")
                v.memset(zero6[:, :], 0.0)
                v.memset(vstg[:, :], 0.0)
                v.tensor_scalar(cols6[:, CFL1W8:CFL1W8 + 8], cols6[:, CFL1W:CFL1W + 8], QS, None, ALU.mult)
                v.tensor_copy(fl1b_bf[:, :], cols6[:, CFL1B:CFL1B + 8])
                ctx.tick(v.tensor_copy(relTb[:, 0:129], relT_sb[:, 0:129]), "dve")

            @block.tensor
            def _(t):
                ctx.wait(t, "dve", "gd")
                for i in range(NT):
                    mk = t.matmul(p_a[0][:, 0:1], kwt[i][:, :], fl1b_bf[:, i:i + 1],
                                  start=(i == 0), stop=(i == NT - 1))
                ctx.tick(mk, "pe")
                for i in range(NT):
                    mv = t.matmul(p_a[1][:, 0:1], vwt[i][:, :], fl1b_bf[:, i:i + 1],
                                  start=(i == 0), stop=(i == NT - 1))
                ctx.tick(mv, "pe")
                for e in range(NT):
                    for i in range(NT):
                        mq = t.matmul(p_b[:, e:e + 1], qwt[e][:, 128 * i:128 * (i + 1)], fl1b_bf[:, i:i + 1],
                                      start=(i == 0), stop=(i == NT - 1))
                ctx.tick(mq, "pe")
                for i in range(NT):
                    mg = t.matmul(p_b[0:8, 128:129], agwt[i][:, :], fl1b_bf[:, i:i + 1],
                                  start=(i == 0), stop=(i == NT - 1))
                ctx.tick(mg, "pe")

            @block.vector
            def _(v):
                ctx.wait(v, "pe")
                v.tensor_scalar(ucol[:, 0:1], p_a[0][:, 0:1], cols6[:, CKB:CKB + 1], None, ALU.add)
                v.tensor_scalar(ucol[:, 1:2], p_a[1][:, 0:1], cols6[:, CVB:CVB + 1], None, ALU.add)
                v.memset(ucol[:, 4:5], 0.0)
                v.tensor_tensor(u0q_c[:, :], p_b[:, 0:8], qbcol[:, :], ALU.add)
                v.drain()
                v.tensor_scalar(u0q_c[:, :], u0q_c[:, :], QS, None, ALU.mult)
                ctx.tick(v.tensor_copy(ucol[0:8, 3:4], p_b[0:8, 128:129]), "dve")

            @block.sync
            def _(sync):
                ctx.wait(sync, "dve")
                ctx.dma(sync, bass.AP(sm6, 0, [[1, 64], [1, 1]]), ucol[0:64, 1:2])
                ctx.dma(sync, bass.AP(sm6, 65, [[1, 64], [1, 1]]), ucol[64:128, 1:2])
                ctx.dma(sync, bass.AP(sm6, 64, [[1, 1], [1, 1]]), ucol[0:1, 4:5])
                ctx.dma(sync, bass.AP(sm6, 192, [[1, 8], [1, 1]]), ucol[0:8, 3:4])
                ctx.dma(sync, bass.AP(sm6, 256, [[1, 128], [1, 1]]), relT_sb[:, 0:1])
                ctx.dma(sync, bass.AP(sm6, 384, [[1, 128], [1, 1]]), relT_sb[:, 128:129])
                sync.wait_ge(ctx.sems["ld"], ctx.cnt["ld"])
                ctx.dma(sync, u0g_bc[:, :], bass.AP(sm6, 192, [[0, 128], [1, 8]]))

            @block.gpsimd
            def _(g):
                ctx.wait(g, "ld")
                ctx.dma(g, u0v_bc[:, :], bass.AP(sm6, 0, [[0, 128], [1, 129]]), sem="gd")

            @block.vector
            def _(v):
                ctx.wait(v, "ld", "dve")
                for i in range(NT):
                    v.tensor_scalar(kwt[i][:, :], kwt[i][:, :], cols6[:, CFL1W + i:CFL1W + i + 1], None, ALU.mult)
                    v.tensor_scalar(vwt[i][:, :], vwt[i][:, :], cols6[:, CFL1W + i:CFL1W + i + 1], None, ALU.mult)
                    v.tensor_scalar(agwt[i][:, :], agwt[i][:, :], cols6[:, CFL1W + i:CFL1W + i + 1], None, ALU.mult)
                    for e in range(NT):
                        mm = v.tensor_scalar(qwt[e][:, 128 * i:128 * (i + 1)], qwt[e][:, 128 * i:128 * (i + 1)],
                                             cols6[:, CFL1W8 + i:CFL1W8 + i + 1], None, ALU.mult)
                ctx.tick(mm, "dve")

            for ch in range(16):
                @block.gpsimd
                def _(g, ch=ch):
                    ctx.wait(g, "pe", "dve")
                    for i in range(NT):
                        ctx.dma(g, xnb[i][:, :], bass.AP(sp_xn, i * 128 * S + 512 * ch, [[S, 128], [1, 512]]),
                                sem="gd")

                @block.sync
                def _(sync, ch=ch):
                    ctx.wait(sync, "dve")
                    ctx.dma(sync, r2bc[:, :], bass.AP(rows_dram, 3 * S + 512 * ch, [[0, 128], [1, 512]]))
                    for tt in range(4):
                        ctx.dma(sync, r2col[:, tt:tt + 1],
                                bass.AP(rows_dram, 3 * S + 512 * ch + 128 * tt, [[1, 128], [1, 1]]))

                @block.tensor
                def _(t, ch=ch):
                    ctx.wait(t, "gd", "ld", "dve", "act")
                    for i in range(NT):
                        mm = t.matmul(p_a[0][:, :], kwt[i][:, :], xnb[i][:, :],
                                      start=(i == 0), stop=(i == NT - 1))
                    ctx.tick(mm, "pe")

                @block.vector
                def _(v, ch=ch):
                    ctx.wait(v, "pe", "ld")
                    v.tensor_tensor(tmpA[:, :], p_a[0][:, :], r2bc[:, :], ALU.mult)
                    v.drain()
                    ctx.tick(v.tensor_scalar(kstg[:, :], tmpA[:, :], ucol[:, 0:1], None, ALU.add), "dve")

                @block.sync
                def _(sync, ch=ch):
                    ctx.wait(sync, "dve")
                    ctx.dma(sync, kraw_hbm.ap()[:, 512 * ch:512 * (ch + 1)], kstg[:, :])

                for tt in range(4):
                    kt = 4 * ch + tt

                    @block.tensor
                    def _(t, tt=tt, ch=ch):
                        ctx.wait(t, "act", "dve")
                        for i in range(NT):
                            mm = t.matmul(p_b[:, 0:128], xnb[i][:, 128 * tt:128 * (tt + 1)], vwt[i][:, :],
                                          start=(i == 0), stop=(i == NT - 1))
                        ctx.tick(mm, "pe")

                    @block.scalar
                    def _(sc, tt=tt, kt=kt):
                        ctx.wait(sc, "pe", "ld", "dve")
                        sc.activation(vstg[:, 0:64], p_b[:, 0:64], AF.Identity,
                                      scale=r2col[:, tt:tt + 1])
                        sc.activation(vstg[:, 65:129], p_b[:, 64:128], AF.Identity,
                                      scale=r2col[:, tt:tt + 1])
                        ctx.tick(sc.activation(vstg[:, 64:65], zero6[0:128, 0:1], AF.Identity,
                                               bias=1.0), "act")

                    @block.vector
                    def _(v, kt=kt):
                        ctx.wait(v, "act", "gd")
                        ctx.tick(v.tensor_tensor(vstg[:, 0:129], vstg[:, 0:129],
                                                 u0v_bc[:, :], ALU.add), "dve")

                    @block.sync
                    def _(sync, kt=kt):
                        ctx.wait(sync, "dve")
                        ctx.dma(sync, vmoa_hbm.ap()[10 + kt], vstg[:, :])

            @block.sync
            def _(sync):
                for sl in list(range(10)) + list(range(74, 84)):
                    ctx.dma(sync, vmoa_hbm.ap()[sl], zero6[:, 0:144])
                for qg in range(2):
                    ctx.dma(sync, bass.AP(kmix_hbm, qg * 128 * (2560 + S), [[2560 + S, 128], [1, 1280]]),
                            zero6[:, :])
                    ctx.dma(sync, bass.AP(kmix_hbm, qg * 128 * (2560 + S) + 1280 + S, [[2560 + S, 128], [1, 1280]]),
                            zero6[:, :])

            E["moa_env"] = dict(sb6=sbp, qt=qt, xnb=xnb, gates_sb=gates_sb, gT=gT,
                                u0q_c=u0q_c, u0g_bc=u0g_bc, tmpA=tmpA,
                                p_a=p_a, p_b=p_b, agwt=agwt, qwt=qwt, lgt=lgt)
            _moa_q_gates(nc, ctx, block, E, cfg_sb, ones_col)

        E["moa_env2"] = dict(qt=qt, gT=gT, gates_sb=gates_sb, relTb=relTb, tmpA=tmpA, sb6=sb6)
        _moa_attn(nc, ctx, block, E, cfg_sb, ones_col, big)


def _moa_q_gates(nc, ctx, block, E, cfg_sb, ones_col):
    M = E["moa_env"]
    rows_dram, xnown_dram = E["rows_dram"], E["xnown_dram"]
    sp_xn = E["sp_xn"]
    xnb, qt, qwt, agwt = M["xnb"], M["qt"], M["qwt"], M["agwt"]
    gates_sb, gT = M["gates_sb"], M["gT"]
    tmpA, p_a, p_b = M["tmpA"], M["p_a"], M["p_b"]
    u0q_c, u0g_bc, lgt = M["u0q_c"], M["u0g_bc"], M["lgt"]
    sb6 = M["sb6"]
    r2bco = sb6("r2bco", [128, 512])
    r2colo = sb6("r2colo", [128, 4])
    eqs = sb6("eqs", [128, 256])
    dcols = sb6("dcols", [128, 48])

    @block.scalar
    def _(sc):
        ctx.wait(sc, "ld", "dve", "act", "pe", "gd")
        rA = sc.alloc_register("moxA")
        rB = sc.alloc_register("moxB")
        sc.reg_load(rA, cfg_sb[0:1, 0:1])
        sc.reg_load(rB, cfg_sb[0:1, 1:2])
        for k, rr in ((0, rA), (1024, rB)):
            ctx.dma(sc, bass.AP(xnown_dram, k, [[2048, 1024], [1, 1024]]),
                    bass.AP(sp_xn, rr, [[S, 1024], [1, 1024]]), sem="sd")
        r3 = sc.alloc_register("mor3")
        for k, rr in ((0, rA), (1024, rB)):
            sc.reg_alu(r3, rr, 3 * S, ALU.add)
            ctx.dma(sc, bass.AP(rows_dram, 5 * S + k, [[S, 1], [1, 1024]]),
                    bass.AP(rows_dram, r3, [[S, 1], [1, 1024]]), sem="sd")
        sc.wait_ge(ctx.sems["sd"], ctx.cnt["sd"])

    for ch in range(4):
        @block.gpsimd
        def _(g, ch=ch):
            ctx.wait(g, "pe", "dve", "sd")
            for i in range(NT):
                ctx.dma(g, xnb[i][:, :], bass.AP(xnown_dram, i * 128 * 2048 + 512 * ch,
                                                 [[2048, 128], [1, 512]]), sem="gd")

        @block.sync
        def _(sync, ch=ch):
            ctx.wait(sync, "dve", "sd")
            ctx.dma(sync, r2bco[:, :], bass.AP(rows_dram, 5 * S + 512 * ch, [[0, 128], [1, 512]]))
            for tt4 in range(4):
                ctx.dma(sync, r2colo[:, tt4:tt4 + 1],
                        bass.AP(rows_dram, 5 * S + 512 * ch + 128 * tt4, [[1, 128], [1, 1]]))

        for e in range(NT):
            @block.tensor
            def _(t, e=e, ch=ch):
                ctx.wait(t, "gd", "ld", "dve", "act")
                for i in range(NT):
                    mm = t.matmul(p_a[e % 2][:, :], qwt[e][:, 128 * i:128 * (i + 1)], xnb[i][:, :],
                                  start=(i == 0), stop=(i == NT - 1))
                ctx.tick(mm, "pe")

            @block.vector
            def _(v, e=e, ch=ch):
                ctx.wait(v, "pe")
                v.tensor_tensor(tmpA[:, :], p_a[e % 2][:, :], r2bco[:, :], ALU.mult)
                v.drain()
                ctx.tick(v.tensor_scalar(qt[e][:, 512 * ch:512 * (ch + 1)], tmpA[:, :],
                                         u0q_c[:, e:e + 1], None, ALU.add), "dve")

        for tt in range(4):
            ti = 4 * ch + tt

            @block.tensor
            def _(t, tt=tt, ti=ti):
                ctx.wait(t, "dve", "act")
                for i in range(NT):
                    mm = t.matmul(p_b[:, 0:8], xnb[i][:, 128 * tt:128 * (tt + 1)], agwt[i][:, :],
                                  start=(i == 0), stop=(i == NT - 1))
                ctx.tick(mm, "pe")

            @block.vector
            def _(v, tt=tt, ti=ti):
                ctx.wait(v, "pe")
                lg = lgt[:, 0:8]
                v.tensor_scalar(lg, p_b[:, 0:8], r2colo[:, tt:tt + 1], None, ALU.mult)
                v.drain()
                v.tensor_tensor(lg, lg, u0g_bc[:, :], ALU.add)
                v.drain()
                m1, m2, msk = lgt[:, 8:9], lgt[:, 9:10], lgt[:, 10:18]
                v.reduce_max(m1, lg, AX.X)
                v.drain()
                v.tensor_scalar(eqs[:, 16 * ti:16 * ti + 8], lg, m1, None, ALU.is_equal)
                v.drain()
                v.scalar_tensor_tensor(msk, eqs[:, 16 * ti:16 * ti + 8], -1e30, lg, ALU.mult, ALU.add)
                v.drain()
                v.reduce_max(m2, msk, AX.X)
                v.drain()
                v.tensor_scalar(eqs[:, 16 * ti + 8:16 * ti + 16], lg, m2, None, ALU.is_equal)
                ctx.tick(v.tensor_tensor(dcols[:, ti:ti + 1], m2, m1, ALU.subtract), "dve")

    @block.scalar
    def _(sc):
        ctx.wait(sc, "dve")
        ctx.tick(sc.activation(dcols[:, 16:32], dcols[:, 0:16], AF.Exp), "act")

    @block.vector
    def _(v):
        ctx.wait(v, "act")
        v.tensor_scalar(dcols[:, 32:48], dcols[:, 16:32], 1.0, None, ALU.add)
        v.drain()
        v.reciprocal(dcols[:, 32:48], dcols[:, 32:48])
        v.drain()
        v.tensor_tensor(dcols[:, 16:32], dcols[:, 16:32], dcols[:, 32:48], ALU.mult)
        v.drain()
        for ti in range(16):
            v.tensor_scalar(gates_sb[:, 8 * ti:8 * (ti + 1)], eqs[:, 16 * ti:16 * ti + 8],
                            dcols[:, 32 + ti:33 + ti], None, ALU.mult)
            v.drain()
            mm = v.scalar_tensor_tensor(gates_sb[:, 8 * ti:8 * (ti + 1)],
                                        eqs[:, 16 * ti + 8:16 * ti + 16],
                                        dcols[:, 16 + ti:17 + ti],
                                        gates_sb[:, 8 * ti:8 * (ti + 1)], ALU.mult, ALU.add)
            v.drain()
        ctx.tick(mm, "dve")

    gTf = sb6("gTf", [8, 2048])
    for ti in range(16):
        @block.tensor
        def _(t, ti=ti):
            ctx.wait(t, "dve")
            ctx.tick(t.matmul(p_b[0:8, 0:128], gates_sb[:, 8 * ti:8 * (ti + 1)],
                              E["identp_sb"][:, :], is_transpose=True, start=True, stop=True), "pe")

        @block.vector
        def _(v, ti=ti):
            ctx.wait(v, "pe")
            v.tensor_copy(gT[0:8, 128 * ti:128 * (ti + 1)], p_b[0:8, 0:128])
            ctx.tick(v.tensor_copy(gTf[0:8, 128 * ti:128 * (ti + 1)], p_b[0:8, 0:128]), "dve")

    @block.sync
    def _(sync):
        ctx.wait(sync, "dve")
        ctx.dma(sync, E["gmoa_dram"].ap(), gTf[:, :])


def _moa_attn(nc, ctx, block, E, cfg_sb, ones_col, big):
    M = E["moa_env2"]
    qt, gT, relTb, tmpA = M["qt"], M["gT"], M["relTb"], M["tmpA"]
    sb6 = M["sb6"]
    sm6, rows_dram = E["sm6_dram"], E["rows_dram"]
    kraw_hbm, kmix_hbm, vmoa_hbm, vpad_hbm = E["kraw_hbm"], E["kmix_hbm"], E["vmoa_hbm"], E["vpad_hbm"]
    xnown_dram, gmoa_dram, dbg = E["xnown_dram"], E["gmoa_dram"], E["dbg"]
    xdown_dram = E["xdown_dram"]
    identp = E["identp_sb"]

    kmix = sb6("kmix", [128, S], BF16)
    vrot = sb6("vrot", [128, 64 * 129], BF16)
    relw = sb6("relw", [128, 512], BF16)
    band = sb6("band", [128, 8 * 384], BF16)
    kband = sb6("kband", [128, 1280], BF16)
    vband = sb6("vband", [128, 10 * 129], BF16)
    rel0r = sb6("rel0r", [1, 128], BF16)
    rel128r = sb6("rel128r", [1, 128], BF16)
    indl_c = sb6("indl_c", [1, 512], BF16)
    indh_c = sb6("indh_c", [1, 512], BF16)
    onesrow = sb6("onesrow", [1, 128], BF16)
    qrel0 = sb6("qrel0", [1, 1024], BF16)
    qrel128 = sb6("qrel128", [1, 1024], BF16)
    Pm = [sb6(f"Pm{i}", [128, 1024], BF16) for i in range(2)]
    fwork = sb6("fwork", [128, 1024])
    rngbc = sb6("rngbc", [128, 1024])
    cAe = sb6("cAe", [65, 1024])
    cBe = sb6("cBe", [65, 1024])
    rrow = sb6("rrow", [1, 1024])
    grow = sb6("grow", [1, 1024])
    negce = sb6("negce", [128, 8])
    nrow = sb6("nrow", [1, 64])
    krf = sb6("krf", [128, 512])
    bnd32 = sb6("bnd32", [128, 384])
    identb = sb6("identb", [128, 128], BF16)
    gc = [sb6(f"gc{e}", [128, 1024], BF16) for e in range(NT)]
    owt = sb6("owt", [128, 1024], BF16)
    obT = sb6("obT", [8, 1024], BF16)

    with ExitStack() as s7:
        p_s = [s7.enter_context(nc.psum_tensor(f"p_s6{i}", [128, 1024], F32)) for i in range(2)]
        p_cA = s7.enter_context(nc.psum_tensor("p_cA", [65, 1024], F32))
        p_cB = s7.enter_context(nc.psum_tensor("p_cB", [65, 1024], F32))

        @block.sync
        def _(sync):
            ctx.wait(sync, "dve", "act", "pe", "gd", "sd")
            sync.wait_ge(ctx.sems["ld"], ctx.cnt["ld"])
            ctx.dma(sync, vrot[:, :].rearrange("p (a w) -> p a w", a=64, w=129),
                    bass.AP(vmoa_hbm, 10 * 128 * 144, [[144, 128], [128 * 144, 64], [1, 129]]))

        @block.gpsimd
        def _(g):
            ctx.wait(g, "dve", "act", "pe", "sd")
            g.wait_ge(ctx.sems["ld"], ctx.cnt["ld"])
            ctx.dma(g, rel0r[:, :], bass.AP(sm6, 256, [[1, 1], [1, 128]]), sem="gd")
            ctx.dma(g, rel128r[:, :], bass.AP(sm6, 384, [[1, 1], [1, 128]]), sem="gd")
            ctx.dma(g, obT[:, :], E["ob"].ap(), sem="gd")

        @block.vector
        def _(v):
            ctx.wait(v, "ld", "pe", "act", "gd", "sd")
            v.memset(onesrow[:, :], 1.0)
            v.tensor_copy(identb[:, :], identp[:, :])
            v.tensor_copy(relw[:, 192:321], relTb[:, 0:129])
            v.tensor_copy(relw[:, 0:192], relTb[:, 0:1].broadcast_to([128, 192]))
            ctx.tick(v.tensor_copy(relw[:, 321:512], relTb[:, 128:129].broadcast_to([128, 191])), "dve")

        def _sq_chunk(src_getter, col, first, width=512):
            @block.scalar
            def _(sc):
                ctx.wait(sc, "pe", "ld", "dve")
                ctx.tick(sc.activation(tmpA[:, 0:width], src_getter(), AF.Square), "act")

            @block.tensor
            def _(t):
                ctx.wait(t, "act")
                ctx.tick(t.matmul(p_s[0][0:1, 0:width], ones_col[:, 0:1], tmpA[:, 0:width],
                                  start=True, stop=True), "pe")

            @block.vector
            def _(v):
                ctx.wait(v, "pe")
                v.reduce_max(nrow[0:1, 63:64], p_s[0][0:1, 0:width], AX.X)
                v.drain()
                if first:
                    mm = v.tensor_copy(nrow[0:1, col:col + 1], nrow[0:1, 63:64])
                else:
                    mm = v.tensor_tensor(nrow[0:1, col:col + 1], nrow[0:1, col:col + 1],
                                         nrow[0:1, 63:64], ALU.max)
                ctx.tick(mm, "dve")

        for e in range(NT):
            for chq in range(4):
                _sq_chunk(lambda e=e, chq=chq: qt[e][:, 512 * chq:512 * (chq + 1)], e, chq == 0)
        for ch in range(16):
            @block.sync
            def _(sync, ch=ch):
                ctx.wait(sync, "act")
                ctx.dma(sync, Pm[1][:, 0:512], kraw_hbm.ap()[:, 512 * ch:512 * (ch + 1)])
            _sq_chunk(lambda: Pm[1][:, 0:512], 16, ch == 0)
        _sq_chunk(lambda: relTb[:, 0:129], 17, True, width=129)

        @block.scalar
        def _(sc):
            ctx.wait(sc, "dve")
            ctx.tick(sc.activation(nrow[0:1, 32:50], nrow[0:1, 0:18], AF.Sqrt), "act")

        @block.vector
        def _(v):
            ctx.wait(v, "act")
            v.tensor_tensor(nrow[0:1, 20:21], nrow[0:1, 48:49], nrow[0:1, 49:50], ALU.add)
            v.drain()
            v.tensor_scalar(nrow[0:1, 24:32], nrow[0:1, 32:40], nrow[0:1, 20:21], None, ALU.mult)
            v.drain()
            ctx.tick(v.tensor_scalar(nrow[0:1, 24:32], nrow[0:1, 24:32], -1.0, None, ALU.mult), "dve")

        @block.sync
        def _(sync):
            ctx.wait(sync, "dve")
            ctx.dma(sync, bass.AP(sm6, 512, [[1, 1], [1, 8]]), nrow[0:1, 24:32])
            sync.wait_ge(ctx.sems["ld"], ctx.cnt["ld"])
            ctx.dma(sync, negce[:, :], bass.AP(sm6, 512, [[0, 128], [1, 8]]))

        for qg in range(2):
            qgs = qg * 1024

            for ch in range(16):
                @block.gpsimd
                def _(g, ch=ch):
                    ctx.wait(g, "dve")
                    ctx.dma(g, krf[:, :], kraw_hbm.ap()[:, 512 * ch:512 * (ch + 1)], sem="gd")

                @block.sync
                def _(sync, ch=ch, qg=qg):
                    ctx.wait(sync, "pe")
                    ctx.dma(sync, indl_c[:, :], bass.AP(E["indl"], qg * S + 512 * ch, [[S, 1], [1, 512]]))
                    ctx.dma(sync, indh_c[:, :], bass.AP(E["indh"], qg * S + 512 * ch, [[S, 1], [1, 512]]))

                @block.tensor
                def _(t, ch=ch, qg=qg):
                    ctx.wait(t, "dve", "act", "gd", "ld")
                    t.matmul(p_s[0][:, 0:512], rel0r[0:1, :], indl_c[0:1, :],
                             start=True, stop=False)
                    ctx.tick(t.matmul(p_s[0][:, 0:512], rel128r[0:1, :], indh_c[0:1, :],
                                      start=False, stop=True), "pe")

                @block.vector
                def _(v, ch=ch):
                    ctx.wait(v, "pe", "gd")
                    ctx.tick(v.tensor_tensor(kmix[:, 512 * ch:512 * (ch + 1)], p_s[0][:, 0:512],
                                             krf[:, :], ALU.add), "dve")

            @block.sync
            def _(sync, qg=qg):
                ctx.wait(sync, "dve")
                ctx.dma(sync, bass.AP(kmix_hbm, qg * 128 * (2560 + S) + 1280, [[2560 + S, 128], [1, S]]),
                        kmix[:, :])

            @block.scalar
            def _(sc, qg=qg):
                ctx.wait(sc, "ld", "pe")
                rk = sc.alloc_register(f"mbk{qg}")
                sc.reg_load(rk, cfg_sb[0:1, 37 + qg:38 + qg])
                ctx.dma(sc, kband[:, :], bass.AP(kmix_hbm, rk, [[2560 + S, 128], [1, 1280]]), sem="sd")
                rv = sc.alloc_register(f"mbv{qg}")
                sc.reg_load(rv, cfg_sb[0:1, 35 + qg:36 + qg])
                ctx.dma(sc, vband[:, :].rearrange("p (a w) -> p a w", a=10, w=129),
                        bass.AP(vmoa_hbm, rv, [[144, 128], [128 * 144, 10], [1, 129]]), sem="sd")
                sc.wait_ge(ctx.sems["sd"], ctx.cnt["sd"])

            vp_pe, vp_dve, vp_ld = [], [], []
            for e in range(NT):
                for qtl in range(8):
                    i = len(vp_pe)

                    @block.tensor
                    def _(t, e=e, qtl=qtl, qg=qg, i=i):
                        if i == 0:
                            ctx.wait(t, "dve", "ld")
                        if i >= 2:
                            t.wait_ge(ctx.sems["dve"], vp_dve[i - 2])
                        ctx.tick(t.matmul(p_s[i % 2][:, 0:512],
                                          qt[e][:, qgs + 128 * qtl:qgs + 128 * (qtl + 1)],
                                          relw[:, :], start=True, stop=True), "pe")
                        vp_pe.append(ctx.cnt["pe"])

                    @block.vector
                    def _(v, e=e, qtl=qtl, i=i):
                        v.wait_ge(ctx.sems["pe"], vp_pe[i])
                        if i >= 2:
                            v.wait_ge(ctx.sems["ld"], vp_ld[i - 2])
                        ctx.tick(v.tensor_copy(Pm[i % 2][:, 0:512], p_s[i % 2][:, 0:512]), "dve")
                        vp_dve.append(ctx.cnt["dve"])

                    @block.sync
                    def _(sync, e=e, qtl=qtl, i=i):
                        sync.wait_ge(ctx.sems["dve"], vp_dve[i])
                        ctx.dma(sync, vpad_hbm.ap()[e][qtl], Pm[i % 2][:, 0:512])
                        vp_ld.append(ctx.cnt["ld"])

            for e in range(NT):
                @block.tensor
                def _(t, e=e, qg=qg):
                    ctx.wait(t, "dve", "ld", "act")
                    for hf in range(2):
                        t.matmul(p_s[0][0:1, 512 * hf:512 * (hf + 1)], relTb[:, 0:1],
                                 qt[e][:, qgs + 512 * hf:qgs + 512 * (hf + 1)], start=True, stop=True)
                        mm = t.matmul(p_s[1][0:1, 512 * hf:512 * (hf + 1)], relTb[:, 128:129],
                                      qt[e][:, qgs + 512 * hf:qgs + 512 * (hf + 1)], start=True, stop=True)
                    ctx.tick(mm, "pe")

                @block.vector
                def _(v, e=e):
                    ctx.wait(v, "pe")
                    v.tensor_copy(qrel0[:, :], p_s[0][0:1, 0:1024])
                    ctx.tick(v.tensor_copy(qrel128[:, :], p_s[1][0:1, 0:1024]), "dve")

                @block.sync
                def _(sync, e=e, qg=qg):
                    ctx.wait(sync, "pe")
                    for qtl in range(8):
                        ctx.dma(sync, band[:, 384 * qtl:384 * (qtl + 1)],
                                bass.AP(vpad_hbm, (e * 8 + qtl) * 65536 + 128, [[511, 128], [1, 384]]))

                acts_l, pes_l = [], []

                def _exp_ctx(j, e=e, qg=qg, acts_l=acts_l, pes_l=pes_l):
                    @block.scalar
                    def _(sc, j=j, e=e):
                        sc.wait_ge(ctx.sems["pe"], pes_l[j])
                        if j == 0:
                            ctx.wait(sc, "dve")
                        ctx.tick(sc.activation(Pm[j % 2][:, :], p_s[j % 2][:, :], AF.Exp,
                                               bias=negce[:, e:e + 1]), "act")
                        acts_l.append(ctx.cnt["act"])

                    @block.tensor
                    def _(t, j=j):
                        t.wait_ge(ctx.sems["act"], acts_l[j])
                        for hf in range(2):
                            t.matmul(p_cA[0:65, 512 * hf:512 * (hf + 1)],
                                     vrot[:, 129 * j:129 * j + 65],
                                     Pm[j % 2][:, 512 * hf:512 * (hf + 1)],
                                     start=(j == 0), stop=False)
                            mm = t.matmul(p_cB[0:65, 512 * hf:512 * (hf + 1)],
                                          vrot[:, 129 * j + 64:129 * j + 129],
                                          Pm[j % 2][:, 512 * hf:512 * (hf + 1)],
                                          start=(j == 0), stop=False)
                        ctx.tick(mm, "pe")

                for kt in range(64):
                    @block.tensor
                    def _(t, e=e, kt=kt, qg=qg):
                        if kt == 0:
                            ctx.wait(t, "dve", "ld", "act", "sd")
                        if kt >= 2:
                            t.wait_ge(ctx.sems["act"], acts_l[kt - 2])
                        t.matmul(p_s[kt % 2][:, 0:512], kmix[:, 128 * kt:128 * (kt + 1)],
                                 qt[e][:, qgs:qgs + 512], start=True, stop=True)
                        ctx.tick(t.matmul(p_s[kt % 2][:, 512:1024], kmix[:, 128 * kt:128 * (kt + 1)],
                                          qt[e][:, qgs + 512:qgs + 1024], start=True, stop=True), "pe")
                        pes_l.append(ctx.cnt["pe"])

                    if kt >= 1:
                        _exp_ctx(kt - 1)
                _exp_ctx(63)

                for bi in range(10):
                    qtls = list(range(max(0, bi - 2), min(8, bi + 1)))

                    @block.vector
                    def _(v, e=e, bi=bi, qtls=qtls):
                        ctx.wait(v, "pe")
                        if bi == 0:
                            ctx.wait(v, "ld", "sd")
                        for ii, qtl in enumerate(qtls):
                            off = 128 * (bi - qtl)
                            mm = v.tensor_copy(bnd32[:, 128 * ii:128 * (ii + 1)],
                                               band[:, 384 * qtl + off:384 * qtl + off + 128])
                        ctx.tick(mm, "dve")

                    @block.tensor
                    def _(t, e=e, bi=bi, qg=qg, qtls=qtls):
                        ctx.wait(t, "act", "dve")
                        if bi == 0:
                            ctx.wait(t, "ld", "sd")
                        t.matmul(p_s[0][:, 0:512], kband[:, 128 * bi:128 * (bi + 1)],
                                 qt[e][:, qgs:qgs + 512], start=True, stop=True)
                        last = t.matmul(p_s[0][:, 512:1024], kband[:, 128 * bi:128 * (bi + 1)],
                                        qt[e][:, qgs + 512:qgs + 1024], start=True, stop=True)
                        for ii, qtl in enumerate(qtls):
                            last = t.matmul(p_s[1][:, 128 * qtl:128 * (qtl + 1)],
                                            bnd32[:, 128 * ii:128 * (ii + 1)],
                                            identp[:, :], is_transpose=True, start=True, stop=True)
                        if bi <= 6:
                            lo = 128 * (bi + 1)
                            while lo < 1024:
                                hi = min(1024, lo + (512 - lo % 512))
                                last = t.matmul(p_s[1][0:128, lo:hi], onesrow[0:1, 0:128],
                                                qrel0[0:1, lo:hi], start=True, stop=True)
                                lo = hi
                        if bi >= 3:
                            lo = 0
                            c1 = 128 * (bi - 2)
                            while lo < c1:
                                hi = min(c1, lo + 512)
                                last = t.matmul(p_s[1][0:128, lo:hi], onesrow[0:1, 0:128],
                                                qrel128[0:1, lo:hi], start=True, stop=True)
                                lo = hi
                        ctx.tick(last, "pe")

                    @block.vector
                    def _(v, e=e, bi=bi):
                        ctx.wait(v, "pe")
                        v.tensor_copy(krf[:, :], p_s[1][:, 0:512])
                        v.tensor_copy(tmpA[:, :], p_s[1][:, 512:1024])
                        v.drain()
                        v.tensor_tensor(fwork[:, 0:512], p_s[0][:, 0:512], krf[:, :], ALU.add)
                        ctx.tick(v.tensor_tensor(fwork[:, 512:1024], p_s[0][:, 512:1024],
                                                 tmpA[:, :], ALU.add), "dve")

                    @block.scalar
                    def _(sc, e=e, bi=bi):
                        ctx.wait(sc, "dve", "pe")
                        sc.activation(Pm[0][:, :], fwork[:, :], AF.Exp, bias=negce[:, e:e + 1])
                        ctx.tick(sc.activation(Pm[1][:, :], p_s[0][:, :], AF.Exp,
                                               bias=negce[:, e:e + 1]), "act")

                    @block.vector
                    def _(v, e=e, bi=bi):
                        ctx.wait(v, "act")
                        ctx.tick(v.tensor_tensor(Pm[0][:, :], Pm[0][:, :], Pm[1][:, :], ALU.subtract), "dve")

                    @block.tensor
                    def _(t, e=e, bi=bi):
                        ctx.wait(t, "dve")
                        for hf in range(2):
                            t.matmul(p_cA[0:65, 512 * hf:512 * (hf + 1)],
                                     vband[:, 129 * bi:129 * bi + 65],
                                     Pm[0][:, 512 * hf:512 * (hf + 1)],
                                     start=False, stop=(bi == 9))
                            mm = t.matmul(p_cB[0:65, 512 * hf:512 * (hf + 1)],
                                          vband[:, 129 * bi + 64:129 * bi + 129],
                                          Pm[0][:, 512 * hf:512 * (hf + 1)],
                                          start=False, stop=(bi == 9))
                        ctx.tick(mm, "pe")

                @block.vector
                def _(v, e=e, qg=qg):
                    ctx.wait(v, "pe", "ld")
                    v.tensor_copy(cAe[:, :], p_cA[0:65, :])
                    v.tensor_copy(cBe[:, :], p_cB[0:65, :])
                    v.drain()
                    ctx.tick(v.reciprocal(rrow[:, :], cBe[0:1, 0:1024]), "dve")

                @block.sync
                def _(sync, e=e, qg=qg):
                    ctx.wait(sync, "dve")
                    ctx.dma(sync, bass.AP(xdown_dram, 0, [[2048, 64], [1, 1024]]), cAe[0:64, :])
                    ctx.dma(sync, bass.AP(xdown_dram, 64 * 2048, [[2048, 64], [1, 1024]]), cBe[1:65, :])
                    ctx.dma(sync, grow[:, :], bass.AP(gmoa_dram, 2048 * e + qgs, [[1, 1], [1, 1024]]))

                @block.vector
                def _(v, e=e):
                    ctx.wait(v, "ld")
                    ctx.tick(v.tensor_tensor(rrow[:, :], rrow[:, :], grow[:, :], ALU.mult), "dve")

                @block.sync
                def _(sync, e=e):
                    ctx.wait(sync, "dve")
                    ctx.dma(sync, bass.AP(sm6, 2048, [[1, 1], [1, 1024]]), rrow[:, :])
                    sync.wait_ge(ctx.sems["ld"], ctx.cnt["ld"])
                    ctx.dma(sync, rngbc[:, :], bass.AP(sm6, 2048, [[0, 128], [1, 1024]]))
                    ctx.dma(sync, fwork[:, :], bass.AP(xdown_dram, 0, [[2048, 128], [1, 1024]]))

                @block.vector
                def _(v, e=e):
                    ctx.wait(v, "ld")
                    ctx.tick(v.tensor_tensor(gc[e][:, :], fwork[:, :], rngbc[:, :], ALU.mult), "dve")

            for dt in range(NT):
                @block.gpsimd
                def _(g, dt=dt, qg=qg):
                    ctx.wait(g, "pe", "dve")
                    ctx.dma(g, owt[:, :].rearrange("p (a w) -> p a w", a=8, w=128),
                            bass.AP(E["ow"], 128 * dt, [[1024, 128], [128 * 1024, 8], [1, 128]]),
                            sem="gd")

                @block.tensor
                def _(t, dt=dt, qg=qg):
                    ctx.wait(t, "gd", "dve")
                    if dt == 0:
                        ctx.wait(t, "act")
                    last = None
                    for hf in range(2):
                        for e in range(NT):
                            t.matmul(p_s[0][:, 512 * hf:512 * (hf + 1)],
                                     owt[:, 128 * e:128 * (e + 1)],
                                     gc[e][:, 512 * hf:512 * (hf + 1)],
                                     start=(e == 0), stop=False)
                        last = t.matmul(p_s[0][:, 512 * hf:512 * (hf + 1)],
                                        obT[0:8, 128 * dt:128 * (dt + 1)],
                                        gT[0:8, qgs + 512 * hf:qgs + 512 * (hf + 1)],
                                        start=False, stop=True)
                    ctx.tick(last, "pe")

                @block.sync
                def _(sync, dt=dt, qg=qg):
                    ctx.wait(sync, "pe", "dve")
                    ctx.dma(sync, krf[:, :], bass.AP(xnown_dram, 2048 * 128 * dt + qgs, [[2048, 128], [1, 512]]))
                    ctx.dma(sync, tmpA[:, :], bass.AP(xnown_dram, 2048 * 128 * dt + qgs + 512,
                                                      [[2048, 128], [1, 512]]))

                @block.vector
                def _(v, dt=dt, qg=qg):
                    ctx.wait(v, "ld", "pe")
                    v.tensor_tensor(big[dt][:, qgs:qgs + 512], p_s[0][:, 0:512], krf[:, :], ALU.add)
                    ctx.tick(v.tensor_tensor(big[dt][:, qgs + 512:qgs + 1024], p_s[0][:, 512:1024],
                                             tmpA[:, :], ALU.add), "dve")

        if DEBUG:
            @block.sync
            def _(sync):
                ctx.wait(sync, "dve", "pe", "act")
                for i in range(NT):
                    ctx.dma(sync, dbg["x2"][128 * i:128 * (i + 1), :], big[i][:, 0:2048])
                ctx.dma(sync, dbg["qt"], qt[0][:, :])
                ctx.dma(sync, dbg["kmix"], kmix[:, 2048:3072])
                ctx.dma(sync, dbg["negce"], negce[:, :])
                ctx.dma(sync, dbg["gctx"], gc[0][:, :])


def _moe_final(nc, ctx, block, E, big, big_all, cfg_sb, ones_col):
    rows_dram, sm6 = E["rows_dram"], E["sm6_dram"]
    fg_dram, sblk_dram, xown_dram = E["fg_dram"], E["sblk_dram"], E["xown_dram"]
    dbg = E["dbg"]
    with ExitStack() as s8:
        sb8 = lambda n, s, d=F32: s8.enter_context(nc.sbuf_tensor(n, s, d))
        xn3 = [sb8(f"xn3_{i}", [128, CW], BF16) for i in range(NT)]
        w1s = [sb8(f"w1s{i}", [128, 4096], BF16) for i in range(2)]
        w2s = [sb8(f"w2s{i}", [128, 4096], BF16) for i in range(2)]
        fgT = sb8("fgT", [16, CW], BF16)
        fgr = sb8("fgr", [16, CW])
        fgwb = sb8("fgwb", [128, 128], BF16)
        fb2b = sb8("fb2b", [16, 1024], BF16)
        fcols = sb8("fcols", [128, 16])
        fb1c = sb8("fb1c", [128, 64])
        hgb = sb8("hgb", [128, 4 * 512], BF16)
        gbc = sb8("gbc", [128, 512])
        mu_bc = sb8("mu_bc", [128, 512])
        r3_bc = sb8("r3_bc", [128, 512])
        rowm = sb8("rowm", [1, CW])
        rowv = sb8("rowv", [1, CW])
        rowr = sb8("rowr", [1, CW])
        rowt = sb8("rowt", [1, CW])
        sq8 = sb8("sq8", [128, 512])
        hq = sb8("hq", [128, 512])
        eqs8 = sb8("eqs8", [128, 512])
        dcol8 = sb8("dcol8", [128, 80])
        outs = sb8("outs", [128, 512])
        sfin_t = sb8("sfin_t", [128, 512])
        xow_t = sb8("xow_t", [128, 512])

        p_r1 = s8.enter_context(nc.psum_tensor("p_r1", [1, 512], F32))
        p_r2 = s8.enter_context(nc.psum_tensor("p_r2", [1, 512], F32))
        p_g8 = s8.enter_context(nc.psum_tensor("p_g8", [128, 128], F32))
        p_h8 = [s8.enter_context(nc.psum_tensor(f"p_h8{i}", [128, 512], F32)) for i in range(2)]
        p_o8 = [s8.enter_context(nc.psum_tensor(f"p_o8{i}", [128, 512], F32)) for i in range(2)]

        for cc in range(4):
            for i in range(NT):
                @block.scalar
                def _(sc, i=i, cc=cc):
                    ctx.wait(sc, "pe", "dve")
                    ctx.tick(sc.activation(sq8[:, :], big[i][:, 512 * cc:512 * (cc + 1)], AF.Square), "act")

                @block.tensor
                def _(t, i=i, cc=cc):
                    ctx.wait(t, "act", "dve")
                    t.matmul(p_r1[:, :], ones_col[:, 0:1], big[i][:, 512 * cc:512 * (cc + 1)],
                             start=(i == 0), stop=(i == NT - 1))
                    ctx.tick(t.matmul(p_r2[:, :], ones_col[:, 0:1], sq8[:, :],
                                      start=(i == 0), stop=(i == NT - 1)), "pe")

            @block.vector
            def _(v, cc=cc):
                ctx.wait(v, "pe")
                sl = slice(512 * cc, 512 * (cc + 1))
                v.tensor_scalar(rowm[0:1, sl], p_r1[:, :], 1.0 / D, None, ALU.mult)
                v.drain()
                v.tensor_tensor(rowt[0:1, 0:512], rowm[0:1, sl], rowm[0:1, sl], ALU.mult)
                v.drain()
                v.tensor_scalar(rowv[0:1, sl], p_r2[:, :], 1.0 / D, EPS_LN, ALU.mult, ALU.add)
                v.drain()
                ctx.tick(v.tensor_tensor(rowv[0:1, sl], rowv[0:1, sl], rowt[0:1, 0:512],
                                         ALU.subtract), "dve")

            @block.scalar
            def _(sc, cc=cc):
                ctx.wait(sc, "dve")
                sl = slice(512 * cc, 512 * (cc + 1))
                ctx.tick(sc.activation(rowt[0:1, sl], rowv[0:1, sl], AF.Sqrt), "act")

            @block.vector
            def _(v, cc=cc):
                ctx.wait(v, "act")
                sl = slice(512 * cc, 512 * (cc + 1))
                v.reciprocal(rowr[0:1, sl], rowt[0:1, sl])
                v.drain()
                ctx.tick(_newton_rsqrt(v, rowr[0:1, sl], rowv[0:1, sl], rowt[0:1, sl]), "dve")

        @block.sync
        def _(sync):
            ctx.wait(sync, "dve")
            ctx.dma(sync, bass.AP(rows_dram, 4 * S, [[S, 1], [1, 2048]]), rowm[0:1, :])
            ctx.dma(sync, bass.AP(rows_dram, 4 * S + 2048, [[S, 1], [1, 2048]]), rowr[0:1, :])

        @block.scalar
        def _(sc):
            ctx.wait(sc, "ld", "dve", "act", "pe", "gd")
            rA = sc.alloc_register("xoA")
            rB = sc.alloc_register("xoB")
            sc.reg_load(rA, cfg_sb[0:1, 0:1])
            sc.reg_load(rB, cfg_sb[0:1, 1:2])
            for k, rr in ((0, rA), (1024, rB)):
                ctx.dma(sc, bass.AP(xown_dram, k, [[2048, 1024], [1, 1024]]),
                        bass.AP(E["x_ch"], rr, [[S, 1024], [1, 1024]]), sem="sd")
            sc.wait_ge(ctx.sems["sd"], ctx.cnt["sd"])

        @block.sync
        def _(sync):
            ctx.wait(sync, "dve", "act")
            for e16 in range(16):
                for ft4 in range(4):
                    ctx.dma(sync, fb1c[:, 4 * e16 + ft4:4 * e16 + ft4 + 1],
                            bass.AP(E["fb1"], 512 * e16 + 128 * ft4, [[1, 128], [1, 1]]))
            for i in range(NT):
                ctx.dma(sync, fcols[:, i:i + 1], bass.AP(E["fln2w"], 128 * i, [[1, 128], [1, 1]]))
                ctx.dma(sync, fcols[:, 8 + i:9 + i], bass.AP(E["fln2b"], 128 * i, [[1, 128], [1, 1]]))

        @block.gpsimd
        def _(g):
            ctx.wait(g, "dve", "act")
            ctx.dma(g, fgwb[:, :].rearrange("p (a w) -> p a w", a=8, w=16),
                    bass.AP(E["fgw"], 0, [[16, 128], [2048, 8], [1, 16]]), sem="gd")
            ctx.dma(g, fb2b[:, :], E["fb2"].ap(), sem="gd")

        for cc in range(4):
            @block.sync
            def _(sync, cc=cc):
                ctx.wait(sync, "dve")
                sync.wait_ge(ctx.sems["ld"], ctx.cnt["ld"])
                ctx.dma(sync, mu_bc[:, :], bass.AP(rows_dram, 4 * S + 512 * cc, [[0, 128], [1, 512]]))
                ctx.dma(sync, r3_bc[:, :], bass.AP(rows_dram, 4 * S + 2048 + 512 * cc, [[0, 128], [1, 512]]))

            for i in range(NT):
                @block.vector
                def _(v, i=i, cc=cc):
                    ctx.wait(v, "ld")
                    v.tensor_tensor(sq8[:, :], big[i][:, 512 * cc:512 * (cc + 1)], mu_bc[:, :], ALU.subtract)
                    v.drain()
                    v.tensor_tensor(sq8[:, :], sq8[:, :], r3_bc[:, :], ALU.mult)
                    v.drain()
                    ctx.tick(v.tensor_scalar(xn3[i][:, 512 * cc:512 * (cc + 1)], sq8[:, :],
                                             fcols[:, i:i + 1], fcols[:, 8 + i:9 + i],
                                             ALU.mult, ALU.add), "dve")

        for tt in range(16):
            @block.tensor
            def _(t, tt=tt):
                ctx.wait(t, "dve", "ld", "gd")
                for i in range(NT):
                    mm = t.matmul(p_g8[:, 0:16], xn3[i][:, 128 * tt:128 * (tt + 1)],
                                  fgwb[:, 16 * i:16 * (i + 1)], start=(i == 0), stop=(i == NT - 1))
                ctx.tick(mm, "pe")

            @block.vector
            def _(v, tt=tt):
                ctx.wait(v, "pe")
                m1, m2, msk = dcol8[:, 48:49], dcol8[:, 49:50], dcol8[:, 64:80]
                v.reduce_max(m1, p_g8[:, 0:16], AX.X)
                v.drain()
                v.tensor_scalar(eqs8[:, 32 * tt:32 * tt + 16], p_g8[:, 0:16], m1, None, ALU.is_equal)
                v.drain()
                v.scalar_tensor_tensor(msk, eqs8[:, 32 * tt:32 * tt + 16], -1e30, p_g8[:, 0:16],
                                       ALU.mult, ALU.add)
                v.drain()
                v.reduce_max(m2, msk, AX.X)
                v.drain()
                v.tensor_scalar(eqs8[:, 32 * tt + 16:32 * tt + 32], p_g8[:, 0:16], m2, None, ALU.is_equal)
                ctx.tick(v.tensor_tensor(dcol8[:, tt:tt + 1], m2, m1, ALU.subtract), "dve")

        @block.scalar
        def _(sc):
            ctx.wait(sc, "dve")
            ctx.tick(sc.activation(dcol8[:, 16:32], dcol8[:, 0:16], AF.Exp), "act")

        @block.vector
        def _(v):
            ctx.wait(v, "act")
            v.tensor_scalar(dcol8[:, 32:48], dcol8[:, 16:32], 1.0, None, ALU.add)
            v.drain()
            v.reciprocal(dcol8[:, 32:48], dcol8[:, 32:48])
            v.drain()
            v.tensor_tensor(dcol8[:, 16:32], dcol8[:, 16:32], dcol8[:, 32:48], ALU.mult)
            v.drain()
            for tt in range(16):
                v.tensor_scalar(eqs8[:, 32 * tt:32 * tt + 16], eqs8[:, 32 * tt:32 * tt + 16],
                                dcol8[:, 32 + tt:33 + tt], None, ALU.mult)
                v.drain()
                mm = v.scalar_tensor_tensor(eqs8[:, 32 * tt:32 * tt + 16],
                                            eqs8[:, 32 * tt + 16:32 * tt + 32],
                                            dcol8[:, 16 + tt:17 + tt],
                                            eqs8[:, 32 * tt:32 * tt + 16], ALU.mult, ALU.add)
                v.drain()
            ctx.tick(mm, "dve")

        for tt in range(16):
            @block.tensor
            def _(t, tt=tt):
                ctx.wait(t, "dve")
                ctx.tick(t.matmul(p_g8[0:16, 0:128], eqs8[:, 32 * tt:32 * tt + 16],
                                  E["identp_sb"][:, :], is_transpose=True, start=True, stop=True), "pe")

            @block.vector
            def _(v, tt=tt):
                ctx.wait(v, "pe")
                v.tensor_copy(fgT[0:16, 128 * tt:128 * (tt + 1)], p_g8[0:16, 0:128])
                ctx.tick(v.tensor_copy(fgr[0:16, 128 * tt:128 * (tt + 1)], p_g8[0:16, 0:128]), "dve")

        @block.sync
        def _(sync):
            ctx.wait(sync, "dve")
            ctx.dma(sync, fg_dram.ap(), fgr[:, :])
            sync.wait_ge(ctx.sems["ld"], ctx.cnt["ld"])

        gds, pes8 = {}, {}
        h_pe, h_act, h_tt = [], [], []
        o_pe, o_add = [], []
        gbc_ld = {}
        for e in range(16):
            @block.gpsimd
            def _(g, e=e):
                if e >= 2:
                    g.wait_ge(ctx.sems["pe"], pes8[e - 2])
                else:
                    ctx.wait(g, "pe")
                for ft in range(4):
                    ctx.dma(g, w1s[e % 2][:, 1024 * ft:1024 * (ft + 1)].rearrange(
                        "p (a w) -> p a w", a=8, w=128),
                        bass.AP(E["fw1"], e * 524288 + 128 * ft, [[512, 128], [65536, 8], [1, 128]]),
                        sem="gd")
                    ctx.dma(g, w2s[e % 2][:, 1024 * ft:1024 * (ft + 1)].rearrange(
                        "p (a w) -> p a w", a=8, w=128),
                        bass.AP(E["fw2"], e * 524288 + 131072 * ft, [[1024, 128], [128, 8], [1, 128]]),
                        sem="gd")
                gds[e] = ctx.cnt["gd"]

            for cc in range(4):
                @block.sync
                def _(sync, e=e, cc=cc):
                    ctx.wait(sync, "dve")
                    ctx.dma(sync, gbc[:, :], bass.AP(fg_dram, 2048 * e + 512 * cc, [[0, 128], [1, 512]]))
                    gbc_ld[(e, cc)] = ctx.cnt["ld"]

                for ft in range(4):
                    idx = len(h_pe)

                    @block.tensor
                    def _(t, e=e, cc=cc, ft=ft, idx=idx):
                        t.wait_ge(ctx.sems["gd"], gds[e])
                        if idx == 0:
                            ctx.wait(t, "dve", "ld", "act")
                        if idx >= 2:
                            t.wait_ge(ctx.sems["act"], h_act[idx - 2])
                        for i in range(NT):
                            mm = t.matmul(p_h8[ft % 2][:, :],
                                          w1s[e % 2][:, 1024 * ft + 128 * i:1024 * ft + 128 * (i + 1)],
                                          xn3[i][:, 512 * cc:512 * (cc + 1)],
                                          start=(i == 0), stop=(i == NT - 1))
                        ctx.tick(mm, "pe")
                        h_pe.append(ctx.cnt["pe"])

                    @block.scalar
                    def _(sc, e=e, cc=cc, ft=ft, idx=idx):
                        sc.wait_ge(ctx.sems["pe"], h_pe[idx])
                        if idx >= 1:
                            sc.wait_ge(ctx.sems["dve"], h_tt[idx - 1])
                        ctx.tick(sc.activation(hq[:, :], p_h8[ft % 2][:, :], AF.Relu,
                                               bias=fb1c[:, 4 * e + ft:4 * e + ft + 1]), "act")
                        h_act.append(ctx.cnt["act"])

                    @block.vector
                    def _(v, e=e, cc=cc, ft=ft, idx=idx):
                        v.wait_ge(ctx.sems["act"], h_act[idx])
                        v.wait_ge(ctx.sems["ld"], gbc_ld[(e, cc)])
                        ctx.tick(v.tensor_tensor(hgb[:, 512 * ft:512 * (ft + 1)], hq[:, :], gbc[:, :],
                                                 ALU.mult), "dve")
                        h_tt.append(ctx.cnt["dve"])

                for dt in range(NT):
                    odx = len(o_pe)

                    @block.tensor
                    def _(t, e=e, cc=cc, dt=dt, odx=odx):
                        t.wait_ge(ctx.sems["dve"], h_tt[-1])
                        if odx >= 2:
                            t.wait_ge(ctx.sems["dve"], o_add[odx - 2])
                        for ft in range(4):
                            mm = t.matmul(p_o8[dt % 2][:, :],
                                          w2s[e % 2][:, 1024 * ft + 128 * dt:1024 * ft + 128 * (dt + 1)],
                                          hgb[:, 512 * ft:512 * (ft + 1)], start=(ft == 0), stop=(ft == 3))
                        ctx.tick(mm, "pe")
                        o_pe.append(ctx.cnt["pe"])

                    @block.vector
                    def _(v, e=e, cc=cc, dt=dt, odx=odx):
                        v.wait_ge(ctx.sems["pe"], o_pe[odx])
                        ctx.tick(v.tensor_tensor(big[dt][:, 512 * cc:512 * (cc + 1)],
                                                 big[dt][:, 512 * cc:512 * (cc + 1)],
                                                 p_o8[dt % 2][:, :], ALU.add), "dve")
                        o_add.append(ctx.cnt["dve"])
            pes8[e] = ctx.cnt["pe"]

        for cc in range(4):
            for dt in range(NT):
                @block.tensor
                def _(t, cc=cc, dt=dt):
                    ctx.wait(t, "dve", "gd")
                    ctx.tick(t.matmul(p_o8[dt % 2][:, :], fb2b[0:16, 128 * dt:128 * (dt + 1)],
                                      fgT[0:16, 512 * cc:512 * (cc + 1)], start=True, stop=True), "pe")

                @block.vector
                def _(v, cc=cc, dt=dt):
                    ctx.wait(v, "pe")
                    ctx.tick(v.tensor_tensor(big[dt][:, 512 * cc:512 * (cc + 1)],
                                             big[dt][:, 512 * cc:512 * (cc + 1)],
                                             p_o8[dt % 2][:, :], ALU.add), "dve")

        for dt in range(NT):
            for cc in range(4):
                @block.scalar
                def _(sc, dt=dt, cc=cc):
                    ctx.wait(sc, "dve")
                    ctx.tick(sc.activation(sq8[:, :], big[dt][:, 512 * cc:512 * (cc + 1)],
                                           AF.Sigmoid), "act")

                @block.sync
                def _(sync, dt=dt, cc=cc):
                    ctx.wait(sync, "dve", "sd")
                    ctx.dma(sync, sfin_t[:, :], bass.AP(sblk_dram, 128 * 2048 * dt + 512 * cc,
                                                        [[2048, 128], [1, 512]]))
                    ctx.dma(sync, xow_t[:, :], bass.AP(xown_dram, 128 * 2048 * dt + 512 * cc,
                                                       [[2048, 128], [1, 512]]))

                @block.vector
                def _(v, dt=dt, cc=cc):
                    ctx.wait(v, "act", "ld", "st")
                    v.tensor_tensor(outs[:, :], big[dt][:, 512 * cc:512 * (cc + 1)], sq8[:, :], ALU.mult)
                    v.drain()
                    v.tensor_tensor(outs[:, :], outs[:, :], sfin_t[:, :], ALU.mult)
                    v.drain()
                    ctx.tick(v.tensor_tensor(outs[:, :], outs[:, :], xow_t[:, :], ALU.add), "dve")

                @block.sync
                def _(sync, dt=dt, cc=cc):
                    ctx.wait(sync, "dve")
                    ctx.dma(sync, E["out_ch"][128 * dt:128 * (dt + 1), 512 * cc:512 * (cc + 1)],
                            outs[:, :], sem="st")

        if DEBUG:
            @block.sync
            def _(sync):
                ctx.wait(sync, "dve", "act", "pe")
                ctx.dma(sync, dbg["xn3"], xn3[0][:, :])
                ctx.dma(sync, dbg["fgT"], fgT[:, :])


def kernel(**inp):
    xs = np.ascontiguousarray(np.asarray(inp["L0_attn_x"], np.float32).transpose(0, 2, 1))
    kk = np.arange(128).reshape(128, 1)
    cc = np.arange(2048).reshape(1, 2048)
    maskm_np = (cc >= kk + 1024).astype(ml_dtypes.bfloat16)
    relT_np = np.ascontiguousarray(np.asarray(inp["L0_ffn_rel_table"], np.float32).T)
    ts = np.arange(S)
    nc = build_program()
    in_maps = []
    for c in range(8):
        b, j = c // 4, c % 4
        q0A, q0B = 1024 * j, 1024 * (7 - j)
        cfg = np.zeros((1, 64), np.int32)
        cfg[0, 0] = q0A
        cfg[0, 1] = q0B
        blocks = ([(0, 1024 * r) for r in range(j)] +
                  [(1, 1024 * r) for r in range(7 - j)] +
                  [(0, 1024 * j), (1, 1024 * (7 - j))])
        pmsk = np.zeros((2, 9), np.float32)
        for pos, (st, k0) in enumerate(blocks):
            cfg[0, 8 + pos] = k0
            cfg[0, 17 + pos] = st * 1024
            cfg[0, 26 + pos] = (k0 // 128) * 128 * 1280
            pmsk[st, pos] = 1.0
        pmsk = pmsk.reshape(-1)
        cfg[0, 35] = (q0A // 128 + 9) * (128 * 144)
        cfg[0, 36] = (q0B // 128 + 9) * (128 * 144)
        cfg[0, 37] = 0 * 128 * (2560 + S) + q0A + 1152
        cfg[0, 38] = 1 * 128 * (2560 + S) + q0B + 1152
        indl_np = np.stack([(ts <= q0A - 129), (ts <= q0B - 129)]).astype(ml_dtypes.bfloat16)
        indh_np = np.stack([(ts >= q0A + 1152), (ts >= q0B + 1152)]).astype(ml_dtypes.bfloat16)
        in_maps.append(dict(
            x_ch=xs[b],
            dd=np.asarray(inp["L0_attn_delta_decay"], np.float32).reshape(1, D),
            cfg=cfg,
            phi=np.asarray(inp["L0_attn_phi_res"], np.float32),
            gwT=np.ascontiguousarray(np.asarray(inp["L0_attn_gate_w"], np.float32).T),
            gb=np.asarray(inp["L0_attn_gate_b"], np.float32).reshape(1, 24),
            ident=np.eye(128, dtype=np.float32),
            wqkvT=np.ascontiguousarray(np.asarray(inp["L0_attn_in_proj_w"], np.float32).T),
            bqkv=np.asarray(inp["L0_attn_in_proj_b"], np.float32).reshape(1, 3072),
            ln1w=np.asarray(inp["L0_attn_ln1_w"], np.float32).reshape(1, D),
            maskm=maskm_np,
            posmask=np.broadcast_to(pmsk.reshape(1, 18), (128, 18)).copy(),
            outwT=np.ascontiguousarray(np.asarray(inp["L0_attn_out_w"], np.float32).T),
            outb=np.asarray(inp["L0_attn_out_b"], np.float32).reshape(1, D),
            w1T=np.ascontiguousarray(np.asarray(inp["L0_attn_w1"], np.float32).T),
            b1=np.asarray(inp["L0_attn_b1"], np.float32).reshape(1, 4096),
            w2T=np.ascontiguousarray(np.asarray(inp["L0_attn_w2"], np.float32).T),
            b2=np.asarray(inp["L0_attn_b2"], np.float32).reshape(1, D),
            ln2w=np.asarray(inp["L0_attn_ln2_w"], np.float32).reshape(1, D),
            fln1w=np.asarray(inp["L0_ffn_ln1_w"], np.float32).reshape(1, D),
            fln1b=np.asarray(inp["L0_ffn_ln1_b"], np.float32).reshape(1, D),
            fln2w=np.asarray(inp["L0_ffn_ln2_w"], np.float32).reshape(1, D),
            fln2b=np.asarray(inp["L0_ffn_ln2_b"], np.float32).reshape(1, D),
            agw=np.asarray(inp["L0_ffn_attn_gate_w"], np.float32),
            qw=np.asarray(inp["L0_ffn_q_w"], np.float32),
            qb=np.asarray(inp["L0_ffn_q_b"], np.float32),
            kw=np.asarray(inp["L0_ffn_k_w"], np.float32),
            kb=np.asarray(inp["L0_ffn_k_b"], np.float32).reshape(1, 128),
            vw=np.asarray(inp["L0_ffn_v_w"], np.float32),
            vb=np.asarray(inp["L0_ffn_v_b"], np.float32).reshape(1, 128),
            ow=np.asarray(inp["L0_ffn_o_w"], np.float32),
            ob=np.asarray(inp["L0_ffn_o_b"], np.float32),
            relT=relT_np,
            indl=indl_np,
            indh=indh_np,
            fgw=np.asarray(inp["L0_ffn_ff_gate_w"], np.float32),
            fw1=np.asarray(inp["L0_ffn_ff_w1"], np.float32),
            fb1=np.asarray(inp["L0_ffn_ff_b1"], np.float32),
            fw2=np.asarray(inp["L0_ffn_ff_w2"], np.float32),
            fb2=np.asarray(inp["L0_ffn_ff_b2"], np.float32),
        ))
    res = run_bass_kernel_spmd(nc, in_maps, core_ids=list(range(8)))
    out = np.zeros((B, S, D), np.float32)
    for c in range(8):
        b, j = c // 4, c % 4
        oc = res.results[c]["out_ch"]
        out[b, 1024 * j:1024 * (j + 1)] = oc[:, 0:1024].T
        out[b, 1024 * (7 - j):1024 * (8 - j)] = oc[:, 1024:2048].T
    kernel._res = res
    return out


# version 08
# speedup vs baseline: 1.1657x; 1.1657x over previous
import numpy as np
import ml_dtypes
import concourse.bass as bass
import concourse.mybir as mybir
from concourse.bass_utils import run_bass_kernel_spmd
from contextlib import ExitStack

F32 = mybir.dt.float32
BF16 = mybir.dt.bfloat16
I32 = mybir.dt.int32
AF = mybir.ActivationFunctionType
ALU = mybir.AluOpType
AX = mybir.AxisListType
bfloat16 = ml_dtypes.bfloat16

B, S, D = 2, 8192, 1024
NCH, CW, NT = 4, 2048, 8
NH = 16
EPS_LN, EPS_RMS = 1e-5, 1e-6
DEBUG = False
STAGE = 6


class Ctx:
    def __init__(self, nc, stack):
        self.nc, self.stack = nc, stack
        self.sems, self.cnt = {}, {}

    def sem(self, name):
        h = self.stack.enter_context(self.nc.semaphore(name))
        self.sems[name] = h
        self.cnt[name] = 0
        return h

    def inc(self, name, by=1):
        self.cnt[name] += by
        return self.sems[name]

    def dma(self, eng, out, in_, sem="ld", **kw):
        i = eng.dma_start(out=out, in_=in_, **kw)
        i.then_inc(self.inc(sem, 16), 16)
        return i

    def wait(self, eng, *names):
        for name in names:
            if self.cnt[name]:
                eng.wait_ge(self.sems[name], self.cnt[name])

    def tick(self, inst, name):
        inst.then_inc(self.inc(name, 1), 1)
        return inst


def _newton_rsqrt(v, r_ap, var_ap, tmp_ap):
    v.tensor_tensor(tmp_ap, r_ap, r_ap, ALU.mult)
    v.tensor_tensor(tmp_ap, tmp_ap, var_ap, ALU.mult)
    v.tensor_scalar(tmp_ap, tmp_ap, -0.5, 1.5, ALU.mult, ALU.add)
    return v.tensor_tensor(r_ap, r_ap, tmp_ap, ALU.mult)


def build_program():
    nc = bass.Bass("TRN2", target_bir_lowering=False, debug=False)
    E = {}

    def din(name, shape, dt=F32):
        E[name] = nc.dram_tensor(name, shape, dt, kind="ExternalInput")
        return E[name]

    din("x_ch", [D, S])
    din("dd", [1, D])
    din("cfg", [1, 64], I32)
    din("phi", [8, 8])
    din("gwT", [D, 24])
    din("gb", [1, 24])
    din("ident", [128, 128])
    din("wqkvT", [D, 3072])
    din("bqkv", [1, 3072])
    din("ln1w", [1, D])
    din("maskm", [128, 2048], BF16)
    din("posmask", [128, 18])
    din("outwT", [D, D])
    din("outb", [1, D])
    din("w1T", [D, 4096])
    din("b1", [1, 4096])
    din("w2T", [4096, D])
    din("b2", [1, D])
    din("ln2w", [1, D])
    din("fln1w", [1, D])
    din("fln1b", [1, D])
    din("fln2w", [1, D])
    din("fln2b", [1, D])
    din("agw", [D, 8])
    din("qw", [8, D, 128])
    din("qb", [8, 128])
    din("kw", [D, 128])
    din("kb", [1, 128])
    din("vw", [D, 128])
    din("vb", [1, 128])
    din("ow", [8, 128, D])
    din("ob", [8, D])
    din("relT", [128, 129])
    din("indl", [2, S], BF16)
    din("indh", [2, S], BF16)
    din("fgw", [D, 16])
    din("fw1", [16, D, 512])
    din("fb1", [16, 512])
    din("fw2", [16, 512, D])
    din("fb2", [16, D])

    out_ch = nc.dram_tensor("out_ch", [D, 2048], F32, kind="ExternalOutput").ap()
    dbg = {}
    if DEBUG:
        for nm, shape, dt in [
            ("xd_own", [D, 2048], F32), ("rows", [6, S], F32), ("xstate", [128, 8], F32),
            ("sblk", [D, 2048], F32), ("kv", [1024, 512], BF16), ("q", [1024, 512], BF16),
            ("small", [16, 64], F32), ("vtok", [128, 1280], BF16),
            ("ctxn", [128, 2048], BF16), ("sums", [16, 2048], F32),
            ("acc", [4 * 65, 1024], F32), ("pstage", [128, 3072], BF16),
            ("x2", [D, 2048], F32), ("gT", [8, 2048], BF16), ("qt", [128, 2048], BF16),
            ("kmix", [128, 1024], BF16), ("negce", [128, 8], F32), ("gctx", [128, 1024], BF16),
            ("xn3", [128, 2048], BF16), ("fgT", [16, 2048], BF16),
            ("acc2", [65, 1024], F32), ("Pa", [128, 1024], BF16),
            ("ab0", [65, 1024], F32), ("ab1", [65, 1024], F32),
        ]:
            dbg[nm] = nc.dram_tensor("dbg_" + nm, shape, dt, kind="ExternalOutput").ap()

    sp_xn = nc.dram_tensor("sp_xn", [D, S], F32)
    sp_xd = nc.dram_tensor("sp_xd", [D, S], F32)
    rows_dram = nc.dram_tensor("rows_dram", [6, S], F32)
    small_dram = nc.dram_tensor("small_dram", [16, 64], F32)
    k_hbm = nc.dram_tensor("k_hbm", [1024, S], BF16)
    q_hbm = nc.dram_tensor("q_hbm", [1024, 2048], BF16)
    vtok_hbm = nc.dram_tensor("vtok_hbm", [64, 128, 16 * 80], BF16)
    sblk_dram = nc.dram_tensor("sblk_dram", [D, 2048], F32)
    sk_hbm = nc.dram_tensor("sk_hbm", [9, 1024, 1024], BF16)
    sq_hbm = nc.dram_tensor("sq_hbm", [9, 1024, 1024], BF16)
    sv_hbm = nc.dram_tensor("sv_hbm", [9, 8 * 128, 1280], BF16)
    ctxacc = nc.dram_tensor("ctxacc", [8, 2, 2, 65, 1024], F32)
    sums_dram = nc.dram_tensor("sums_dram", [16, 2048], F32)
    xdown_dram = nc.dram_tensor("xdown_dram", [D, 2048], F32)
    xnown_dram = nc.dram_tensor("xnown_dram", [D, 2048], F32)
    xown_dram = nc.dram_tensor("xown_dram", [D, 2048], F32)
    kraw_hbm = nc.dram_tensor("kraw_hbm", [128, S], BF16)
    kmix_hbm = nc.dram_tensor("kmix_hbm", [2, 128, 2560 + S], BF16)
    vmoa_hbm = nc.dram_tensor("vmoa_hbm", [84, 128, 144], BF16)
    vpad_hbm = nc.dram_tensor("vpad_hbm", [8, 8, 128, 512], BF16)
    sm6_dram = nc.dram_tensor("sm6_dram", [1, 4096], F32)
    gmoa_dram = nc.dram_tensor("gmoa_dram", [8, 2048], F32)
    fg_dram = nc.dram_tensor("fg_dram", [16, 2048], F32)
    E.update(sp_xn=sp_xn, sp_xd=sp_xd, rows_dram=rows_dram, small_dram=small_dram,
             k_hbm=k_hbm, q_hbm=q_hbm, vtok_hbm=vtok_hbm, sblk_dram=sblk_dram,
             sk_hbm=sk_hbm, sq_hbm=sq_hbm, sv_hbm=sv_hbm, ctxacc=ctxacc,
             sums_dram=sums_dram, xdown_dram=xdown_dram, xnown_dram=xnown_dram,
             xown_dram=xown_dram, kraw_hbm=kraw_hbm, kmix_hbm=kmix_hbm, vmoa_hbm=vmoa_hbm,
             vpad_hbm=vpad_hbm, sm6_dram=sm6_dram, gmoa_dram=gmoa_dram, fg_dram=fg_dram,
             out_ch=out_ch, dbg=dbg)

    with ExitStack() as stack:
        ctx = Ctx(nc, stack)
        for sname in ["ld", "gd", "sd", "pe", "act", "dve", "st"]:
            ctx.sem(sname)
        sb = lambda n, s, d: stack.enter_context(nc.sbuf_tensor(n, s, d))

        big_all = sb("big_all", [128, NT * (CW + 1)], F32)
        big = [big_all[:, i * (CW + 1):(i + 1) * (CW + 1)] for i in range(NT)]
        ones_col = sb("ones_col", [128, 2], F32)
        onesb = sb("onesb", [128, 2], BF16)
        xstate = sb("xstate", [128, 8], F32)
        cfg_sb = sb("cfg_sb", [1, 64], I32)
        smalls = sb("smalls", [128, 64], F32)
        negc_bc = sb("negc_bc", [128, 16], F32)
        identp = sb("identp", [128, 128], F32)
        E_identp_loaded = [False]
        SD, SBETA, SCAR = 0, 8, 16

        block = stack.enter_context(nc.Block())

        E["identp_sb"] = identp

        @block.sync
        def _(sync):
            for i in range(NT):
                ctx.dma(sync, smalls[:, SD + i:SD + i + 1], bass.AP(E["dd"], 128 * i, [[1, 128], [1, 1]]))
            ctx.dma(sync, cfg_sb[:, :], E["cfg"].ap())
            ctx.dma(sync, identp[:, :], E["ident"].ap())

        @block.vector
        def _(v):
            v.memset(xstate[:, :], 0.0)
            v.memset(smalls[:, SCAR:SCAR + 8], 0.0)
            v.memset(ones_col[:, :], 1.0)
            v.memset(onesb[:, :], 1.0)
            for i in range(NT - 1):
                v.memset(big[i][:, 0:1], 0.0)
            ctx.tick(v.memset(big[NT - 1][:, 0:1], 0.0), "dve")

        @block.scalar
        def _(sc):
            ctx.wait(sc, "ld")
            ctx.tick(sc.activation(smalls[:, SBETA:SBETA + 8], smalls[:, SD:SD + 8], AF.Sigmoid), "act")

        _prep(nc, ctx, block, E, big, ones_col, xstate, smalls, SBETA, SCAR)

        if STAGE >= 2:
            _gates_mhc(nc, ctx, block, E, big, big_all, xstate, cfg_sb)
            _qkv(nc, ctx, block, E, big, cfg_sb, onesb, negc_bc)
        if STAGE >= 3:
            with ExitStack() as s34:
                ctxn = [s34.enter_context(nc.sbuf_tensor(f"ctxn{i}", [128, CW], BF16))
                        for i in range(NT)]
                E["ctxn_tiles"] = ctxn
                _mha(nc, ctx, block, E, big, big_all, cfg_sb, negc_bc)
                if STAGE >= 4:
                    E["ones_col_ref"] = ones_col
                    _attnffn(nc, ctx, block, E, big, big_all, cfg_sb)
        if STAGE >= 5:
            _moa(nc, ctx, block, E, big, big_all, cfg_sb, ones_col)
        if STAGE >= 6:
            _moe_final(nc, ctx, block, E, big, big_all, cfg_sb, ones_col)

        @block.gpsimd
        def _(g):
            for sname in ["ld", "gd", "sd", "pe", "act", "dve", "st"]:
                ctx.wait(g, sname)
            if DEBUG:
                rA = g.alloc_register("rA")
                rB = g.alloc_register("rB")
                g.reg_load(rA, cfg_sb[0:1, 0:1])
                g.reg_load(rB, cfg_sb[0:1, 1:2])
                for k, rr in ((0, rA), (1024, rB)):
                    ctx.dma(g, dbg["xd_own"].rearrange("(i p) t -> p i t", i=NT, p=128)[:, :, k:k + 1024],
                            bass.AP(E["sp_xd"], rr, [[S, 128], [128 * S, NT], [1, 1024]]), sem="gd")
                g.wait_ge(ctx.sems["gd"], ctx.cnt["gd"])
                ctx.dma(g, dbg["rows"], E["rows_dram"].ap(), sem="gd")
                ctx.dma(g, dbg["xstate"], xstate[:, :], sem="gd")
                if STAGE >= 2:
                    ctx.dma(g, dbg["sblk"], E["sblk_dram"].ap(), sem="gd")
                    ctx.dma(g, dbg["kv"], E["k_hbm"].ap()[:, 1024:1536], sem="gd")
                    ctx.dma(g, dbg["q"], E["q_hbm"].ap()[:, 512:1024], sem="gd")
                    ctx.dma(g, dbg["small"], E["small_dram"].ap(), sem="gd")
                    ctx.dma(g, dbg["vtok"], E["vtok_hbm"].ap()[7], sem="gd")
                if STAGE >= 3:
                    ctx.dma(g, dbg["acc"], E["ctxacc"].ap().rearrange("a b c d e -> (a b c d) e")[0:4 * 65, :], sem="gd")
                    ctx.dma(g, dbg["pstage"][:, 0:1024], E["sk_hbm"].ap()[1][0:128, :], sem="gd")
                    ctx.dma(g, dbg["pstage"][:, 1024:2048], E["sq_hbm"].ap()[7][0:128, :], sem="gd")
                    ctx.dma(g, dbg["pstage"][:, 2048:3072],
                            bass.AP(E["sv_hbm"], 1 * 8 * 128 * 1280, [[1280, 128], [128 * 1280, 8], [1, 128]]), sem="gd")
                if STAGE >= 5:
                    ctx.dma(g, dbg["gT"], E["gmoa_dram"].ap(), sem="gd")
            if STAGE < 6:
                ctx.dma(g, E["out_ch"], E["sp_xd"].ap()[:, 0:2048], sem="gd")
            g.wait_ge(ctx.sems["gd"], ctx.cnt["gd"])

    return nc


def _prep(nc, ctx, block, E, big, ones_col, xstate, smalls, SBETA, SCAR):
    sp_xn, sp_xd, rows_dram = E["sp_xn"], E["sp_xd"], E["rows_dram"]
    x_ch_h = E["x_ch"]
    MU, RR = 0, CW
    with ExitStack() as ps:
        psb = lambda n, s, d=F32: ps.enter_context(nc.sbuf_tensor(n, s, d))
        rows = psb("rows", [1, 4 * CW])
        rowtmp = psb("rowtmp", [1, 2 * CW])
        m_bc = psb("m_bc", [128, CW])
        r_bc = psb("r_bc", [128, CW])
        xsq = [psb(f"xsq{i}", [128, CW]) for i in range(2)]
        dtile = psb("dtile", [128, CW])
        ytile = [psb(f"ytile{i}", [128, CW]) for i in range(2)]
        ycols = psb("ycols", [128, NCH * NT])
        p_s1 = ps.enter_context(nc.psum_tensor("p_s1", [1, CW], F32))
        p_s2 = ps.enter_context(nc.psum_tensor("p_s2", [1, CW], F32))

        for c in range(NCH):
            t0 = c * CW

            @block.sync
            def _(sync, t0=t0):
                ctx.wait(sync, "ld", "dve", "pe")
                for i in range(NT):
                    ctx.dma(sync, big[i][:, 1:CW + 1], x_ch_h.ap()[128 * i:128 * (i + 1), t0:t0 + CW])

            for i in range(NT):
                @block.scalar
                def _(sc, i=i):
                    if i == 0:
                        ctx.wait(sc, "ld", "dve")
                    ctx.wait(sc, "pe")
                    ctx.tick(sc.activation(xsq[i % 2][:, :], big[i][:, 1:CW + 1], AF.Square), "act")

                @block.tensor
                def _(t, i=i):
                    if i == 0:
                        ctx.wait(t, "ld", "dve")
                    for s4 in range(4):
                        t.matmul(p_s1[:, 512 * s4:512 * (s4 + 1)], ones_col[:, 0:1],
                                 big[i][:, 1 + 512 * s4:1 + 512 * (s4 + 1)],
                                 start=(i == 0), stop=(i == NT - 1))
                    ctx.wait(t, "act")
                    for s4 in range(4):
                        mm = t.matmul(p_s2[:, 512 * s4:512 * (s4 + 1)], ones_col[:, 0:1],
                                      xsq[i % 2][:, 512 * s4:512 * (s4 + 1)],
                                      start=(i == 0), stop=(i == NT - 1))
                    ctx.tick(mm, "pe")

            @block.vector
            def _(v):
                ctx.wait(v, "pe")
                v.tensor_scalar(rows[:, MU:MU + CW], p_s1[:, :], 1.0 / D, None, ALU.mult)
                v.tensor_scalar(rowtmp[:, 0:CW], p_s2[:, :], 1.0 / D, None, ALU.mult)
                v.drain()
                v.tensor_tensor(rowtmp[:, CW:2 * CW], rows[:, MU:MU + CW], rows[:, MU:MU + CW], ALU.mult)
                v.drain()
                v.tensor_tensor(rowtmp[:, 0:CW], rowtmp[:, 0:CW], rowtmp[:, CW:2 * CW], ALU.subtract)
                ctx.tick(v.tensor_scalar(rowtmp[:, 0:CW], rowtmp[:, 0:CW], EPS_LN, None, ALU.add), "dve")

            @block.scalar
            def _(sc):
                ctx.wait(sc, "dve")
                ctx.tick(sc.activation(rowtmp[:, CW:2 * CW], rowtmp[:, 0:CW], AF.Sqrt), "act")

            @block.vector
            def _(v):
                ctx.wait(v, "act")
                v.reciprocal(rows[:, RR:RR + CW], rowtmp[:, CW:2 * CW])
                v.drain()
                ctx.tick(_newton_rsqrt(v, rows[:, RR:RR + CW], rowtmp[:, 0:CW], rowtmp[:, CW:2 * CW]), "dve")

            @block.sync
            def _(sync, t0=t0):
                ctx.wait(sync, "dve")
                ctx.dma(sync, bass.AP(rows_dram, 0 * S + t0, [[S, 1], [1, CW]]), rows[:, MU:MU + CW])
                ctx.dma(sync, bass.AP(rows_dram, 1 * S + t0, [[S, 1], [1, CW]]), rows[:, RR:RR + CW])
                sync.wait_ge(ctx.sems["ld"], ctx.cnt["ld"])
                ctx.dma(sync, m_bc[:, :], bass.AP(rows_dram, 0 * S + t0, [[0, 128], [1, CW]]))
                ctx.dma(sync, r_bc[:, :], bass.AP(rows_dram, 1 * S + t0, [[0, 128], [1, CW]]))

            for i in range(NT):
                @block.vector
                def _(v, c=c, i=i):
                    ctx.wait(v, "ld")
                    v.tensor_tensor(big[i][:, 1:CW + 1], big[i][:, 1:CW + 1], m_bc[:, :], ALU.subtract)
                    ctx.tick(v.tensor_tensor(big[i][:, 1:CW + 1], big[i][:, 1:CW + 1], r_bc[:, :], ALU.mult), "dve")
                    v.drain()
                    v.tensor_tensor(dtile[:, :], big[i][:, 1:CW + 1], big[i][:, 0:CW], ALU.subtract)
                    v.drain()
                    v.tensor_tensor_scan(ytile[i % 2][:, :],
                                         smalls[:, SBETA + i:SBETA + i + 1].broadcast_to([128, CW]),
                                         dtile[:, :], smalls[:, SCAR + i:SCAR + i + 1], ALU.mult, ALU.add)
                    v.drain()
                    v.tensor_copy(smalls[:, SCAR + i:SCAR + i + 1], ytile[i % 2][:, CW - 1:CW])
                    ctx.tick(v.reduce_sum(ycols[:, c * NT + i:c * NT + i + 1], ytile[i % 2][:, :], AX.X), "dve")

                @block.scalar
                def _(sc, i=i):
                    ctx.wait(sc, "dve", "pe")
                    ctx.tick(sc.activation(xsq[0][:, :], big[i][:, 1:CW + 1], AF.Square), "act")
                    ctx.tick(sc.activation(xsq[1][:, :], ytile[i % 2][:, :], AF.Square), "act")

                @block.tensor
                def _(t, i=i):
                    ctx.wait(t, "act")
                    for s4 in range(4):
                        t.matmul(p_s2[:, 512 * s4:512 * (s4 + 1)], ones_col[:, 0:1],
                                 xsq[0][:, 512 * s4:512 * (s4 + 1)],
                                 start=(i == 0), stop=(i == NT - 1))
                    for s4 in range(4):
                        mm = t.matmul(p_s1[:, 512 * s4:512 * (s4 + 1)], ones_col[:, 0:1],
                                      xsq[1][:, 512 * s4:512 * (s4 + 1)],
                                      start=(i == 0), stop=(i == NT - 1))
                    ctx.tick(mm, "pe")

                @block.sync
                def _(sync, i=i, t0=t0):
                    ctx.wait(sync, "dve")
                    ctx.dma(sync, bass.AP(sp_xn, i * 128 * S + t0, [[S, 128], [1, CW]]), big[i][:, 1:CW + 1])
                    ctx.dma(sync, bass.AP(sp_xd, i * 128 * S + t0, [[S, 128], [1, CW]]), ytile[i % 2][:, :])

            @block.vector
            def _(v):
                ctx.wait(v, "ld")
                for i in range(NT - 1):
                    v.tensor_copy(big[i][:, 0:1], big[i][:, CW:CW + 1])
                ctx.tick(v.tensor_copy(big[NT - 1][:, 0:1], big[NT - 1][:, CW:CW + 1]), "dve")

            for which in (0, 1):
                @block.vector
                def _(v, which=which):
                    if which == 0:
                        ctx.wait(v, "pe")
                        v.tensor_scalar(rowtmp[:, 0:CW], p_s1[:, :], 1.0 / D, None, ALU.mult)
                        v.drain()
                        ctx.tick(v.tensor_scalar(rowtmp[:, 0:CW], rowtmp[:, 0:CW],
                                                 EPS_RMS, None, ALU.add), "dve")
                    else:
                        v.tensor_scalar(rowtmp[:, 0:CW], p_s2[:, :], 1.0 / D, None, ALU.mult)
                        v.drain()
                        ctx.tick(v.tensor_scalar(rowtmp[:, 0:CW], rowtmp[:, 0:CW],
                                                 EPS_LN, None, ALU.add), "dve")

                @block.scalar
                def _(sc, which=which):
                    ctx.wait(sc, "dve")
                    ctx.tick(sc.activation(rowtmp[:, CW:2 * CW], rowtmp[:, 0:CW], AF.Sqrt), "act")

                @block.vector
                def _(v, which=which):
                    ctx.wait(v, "act")
                    RX = 2 * CW if which == 0 else 3 * CW
                    v.reciprocal(rows[:, RX:RX + CW], rowtmp[:, CW:2 * CW])
                    v.drain()
                    ctx.tick(_newton_rsqrt(v, rows[:, RX:RX + CW], rowtmp[:, 0:CW],
                                           rowtmp[:, CW:2 * CW]), "dve")

            @block.sync
            def _(sync, t0=t0):
                ctx.wait(sync, "dve")
                ctx.dma(sync, bass.AP(rows_dram, 2 * S + t0, [[S, 1], [1, CW]]), rows[:, 2 * CW:3 * CW])
                ctx.dma(sync, bass.AP(rows_dram, 3 * S + t0, [[S, 1], [1, CW]]), rows[:, 3 * CW:4 * CW])

        @block.vector
        def _(v):
            for c2 in range(NCH):
                v.drain()
                v.tensor_tensor(xstate[:, 0:NT], xstate[:, 0:NT], ycols[:, c2 * NT:(c2 + 1) * NT], ALU.add)
            v.drain()
            ctx.tick(v.tensor_copy(xstate[:, 0:NT], xstate[:, 0:NT]), "dve")


def _gates_mhc(nc, ctx, block, E, big, big_all, xstate, cfg_sb):
    sp_xd, small_dram, sblk_dram = E["sp_xd"], E["small_dram"], E["sblk_dram"]
    with ExitStack() as s2:
        sb2 = lambda n, s, d=F32: s2.enter_context(nc.sbuf_tensor(n, s, d))
        kH = sb2("kH", [8, 8])
        kHT = sb2("kHT", [8, 8])
        hrow = sb2("hrow", [8, 1])
        ident = sb2("identt", [128, 128])
        gwT_sb = sb2("gwT_sb", [128, 8 * 24])
        gcol = sb2("gcol", [24, 1])
        gbcol = sb2("gbcol", [24, 1])
        gres = sb2("gres", [8, 2])
        a_bc = sb2("a_bc", [128, 64])
        mt = [sb2(f"mt{n}", [128, 1024], BF16) for n in range(NT)]
        xdb_all = sb2("xdb_all", [128, NT * CW], BF16)
        xdb = [xdb_all[:, i * CW:(i + 1) * CW] for i in range(NT)]
        evt = sb2("evt", [128, 512])
        p_sk = s2.enter_context(nc.psum_tensor("p_sk", [8, 8], F32))
        p_g = s2.enter_context(nc.psum_tensor("p_g", [24, 1], F32))
        p_m = [s2.enter_context(nc.psum_tensor(f"p_m{i}", [128, 512], F32)) for i in range(2)]

        @block.sync
        def _(sync):
            ctx.dma(sync, kH[:, :], E["phi"].ap())
            ctx.dma(sync, ident[:, :], E["ident"].ap())
            ctx.dma(sync, gwT_sb[:, :], bass.AP(E["gwT"], 0, [[24, 128], [128 * 24, 8], [1, 24]]))
            ctx.dma(sync, gbcol[:, :], bass.AP(E["gb"], 0, [[1, 24], [1, 1]]))

        @block.scalar
        def _(sc):
            ctx.wait(sc, "ld")
            ctx.tick(sc.activation(kH[:, :], kH[:, :], AF.Exp), "act")

        for it in range(15):
            @block.vector
            def _(v, it=it):
                if it == 0:
                    ctx.wait(v, "act")
                else:
                    ctx.wait(v, "pe")
                    v.tensor_copy(kH[:, :], p_sk[:, :])
                    v.drain()
                v.reduce_sum(hrow[:, :], kH[:, :], AX.X)
                v.drain()
                v.reciprocal(hrow[:, :], hrow[:, :])
                v.drain()
                ctx.tick(v.tensor_scalar(kH[:, :], kH[:, :], hrow[:, :], None, ALU.mult), "dve")

            @block.tensor
            def _(t):
                ctx.wait(t, "dve")
                ctx.tick(t.matmul(p_sk[:, :], kH[:, :], ident[0:8, 0:8], is_transpose=True,
                                  start=True, stop=True), "pe")

            @block.vector
            def _(v):
                ctx.wait(v, "pe")
                v.tensor_copy(kHT[:, :], p_sk[:, :])
                v.drain()
                v.reduce_sum(hrow[:, :], kHT[:, :], AX.X)
                v.drain()
                v.reciprocal(hrow[:, :], hrow[:, :])
                v.drain()
                ctx.tick(v.tensor_scalar(kHT[:, :], kHT[:, :], hrow[:, :], None, ALU.mult), "dve")

        @block.tensor
        def _(t):
            ctx.wait(t, "dve")
            for i in range(NT):
                mm = t.matmul(p_g[:, :], gwT_sb[:, 24 * i:24 * (i + 1)], xstate[:, i:i + 1],
                              start=(i == 0), stop=(i == NT - 1))
            ctx.tick(mm, "pe")

        @block.vector
        def _(v):
            ctx.wait(v, "pe")
            ctx.tick(v.tensor_scalar(gcol[:, :], p_g[:, :], 1.0 / S, gbcol[:, :], ALU.mult, ALU.add), "dve")

        @block.sync
        def _(sync):
            ctx.wait(sync, "dve")
            ctx.dma(sync, bass.AP(small_dram, 10 * 64, [[1, 24], [1, 1]]), gcol[:, :])
            sync.wait_ge(ctx.sems["ld"], ctx.cnt["ld"])
            ctx.dma(sync, gres[:, 0:1], bass.AP(small_dram, 10 * 64 + 16, [[1, 8], [1, 1]]))

        @block.scalar
        def _(sc):
            ctx.wait(sc, "ld")
            ctx.tick(sc.activation(gres[:, 1:2], gres[:, 0:1], AF.Sigmoid), "act")

        @block.vector
        def _(v):
            ctx.wait(v, "act")
            ctx.tick(v.tensor_scalar(kHT[:, :], kHT[:, :], gres[:, 1:2], None, ALU.mult), "dve")

        @block.sync
        def _(sync):
            ctx.wait(sync, "dve")
            ctx.dma(sync, bass.AP(small_dram, 0, [[64, 8], [1, 8]]), kHT[:, :])
            sync.wait_ge(ctx.sems["ld"], ctx.cnt["ld"])
            ctx.dma(sync, a_bc[:, :], bass.AP(small_dram, 0, [[0, 128], [64, 8], [1, 8]]))

        @block.gpsimd
        def _(g):
            ctx.wait(g, "ld", "dve", "pe", "act")
            rA = g.alloc_register("r2A")
            rB = g.alloc_register("r2B")
            g.reg_load(rA, cfg_sb[0:1, 0:1])
            g.reg_load(rB, cfg_sb[0:1, 1:2])
            for k, rr in ((0, rA), (1024, rB)):
                ctx.dma(g, big_all[:, :].rearrange("p (i w) -> p i w", i=NT, w=CW + 1)[:, :, k:k + 1024],
                        bass.AP(sp_xd, rr, [[S, 128], [128 * S, NT], [1, 1024]]), sem="gd")
                ctx.dma(g, xdb_all[:, :].rearrange("p (i w) -> p i w", i=NT, w=CW)[:, :, k:k + 1024],
                        bass.AP(sp_xd, rr, [[S, 128], [128 * S, NT], [1, 1024]]), sem="gd")
            g.wait_ge(ctx.sems["gd"], ctx.cnt["gd"])

        @block.vector
        def _(v):
            ctx.wait(v, "ld")
            for n in range(NT):
                for m in range(NT):
                    mm = v.tensor_scalar(mt[n][:, 128 * m:128 * (m + 1)], ident[:, :],
                                         a_bc[:, 8 * n + m:8 * n + m + 1], None, ALU.mult)
            ctx.tick(mm, "dve")

        for m in range(NT):
            for ch in range(4):
                pp = (m * 4 + ch) % 2

                @block.tensor
                def _(t, m=m, ch=ch, pp=pp):
                    ctx.wait(t, "dve", "gd")
                    if m * 4 + ch >= 2:
                        ctx.wait(t, "dve")
                    for n in range(NT):
                        mm = t.matmul(p_m[pp][:, :], mt[n][:, 128 * m:128 * (m + 1)],
                                      xdb[n][:, 512 * ch:512 * (ch + 1)],
                                      start=(n == 0), stop=(n == NT - 1))
                    ctx.tick(mm, "pe")

                @block.vector
                def _(v, m=m, ch=ch, pp=pp):
                    ctx.wait(v, "pe", "ld")
                    ctx.tick(v.tensor_tensor(evt[:, :], p_m[pp][:, :],
                                             big[m][:, 512 * ch:512 * (ch + 1)], ALU.add), "dve")

                @block.sync
                def _(sync, m=m, ch=ch):
                    ctx.wait(sync, "dve")
                    ctx.dma(sync, bass.AP(sblk_dram, m * 128 * 2048 + 512 * ch, [[2048, 128], [1, 512]]),
                            evt[:, :])

        @block.sync
        def _(sync):
            sync.wait_ge(ctx.sems["ld"], ctx.cnt["ld"])


def _qkv(nc, ctx, block, E, big, cfg_sb, onesb, negc_bc):
    sp_xd, rows_dram, small_dram = E["sp_xd"], E["rows_dram"], E["small_dram"]
    k_hbm, q_hbm, vtok_hbm = E["k_hbm"], E["q_hbm"], E["vtok_hbm"]
    with ExitStack() as s3:
        sb3 = lambda n, s, d=F32: s3.enter_context(nc.sbuf_tensor(n, s, d))
        wq = [sb3(f"wq{i}", [128, 3072], BF16) for i in range(NT)]
        rhs = [sb3(f"rhs{i}", [128, 512], BF16) for i in range(NT)]
        xdf_all = sb3("xdf_all", [128, NT * 512])
        xdf = [xdf_all[:, i * 512:(i + 1) * 512] for i in range(NT)]
        r1bc = sb3("r1bc", [128, 512])
        stg = [sb3(f"stg{i}", [128, 512], BF16) for i in range(2)]
        sq = sb3("sq", [128, 512], BF16)
        vstg = [sb3(f"vstg{i}", [128, 1280], BF16) for i in range(2)]
        vb_bc = sb3("vb_bc", [128, 1024])
        ln1c = sb3("ln1c", [128, 8])
        bcol = sb3("bcol", [128, 24])
        h2 = sb3("h2", [128, 2], BF16)
        knq = sb3("knq", [2, 32])
        p_kv = [s3.enter_context(nc.psum_tensor(f"p_kv{i}", [128, 512], F32)) for i in range(2)]
        p_vt = [s3.enter_context(nc.psum_tensor(f"p_vt{i}", [128, 512], F32)) for i in range(2)]
        p_n = s3.enter_context(nc.psum_tensor("p_n", [2, 512], F32))

        @block.sync
        def _(sync):
            for i in range(NT):
                ctx.dma(sync, ln1c[:, i:i + 1], bass.AP(E["ln1w"], 128 * i, [[1, 128], [1, 1]]))
            for o in range(24):
                ctx.dma(sync, bcol[:, o:o + 1], bass.AP(E["bqkv"], 128 * o, [[1, 128], [1, 1]]))
            ctx.dma(sync, vb_bc[:, :], bass.AP(E["bqkv"], 2048, [[0, 128], [1, 1024]]))

        @block.gpsimd
        def _(g):
            ctx.wait(g, "ld")
            for i in range(NT):
                ctx.dma(g, wq[i][:, :], E["wqkvT"].ap()[128 * i:128 * (i + 1), :], sem="gd")

        @block.vector
        def _(v):
            ctx.wait(v, "ld", "gd")
            v.memset(h2[:, :], 0.0)
            v.memset(h2[0:64, 0:1], 1.0)
            v.memset(h2[64:128, 1:2], 1.0)
            v.memset(knq[:, 0:16], 0.0)
            v.memset(vstg[0][:, :], 1.0)
            v.memset(vstg[1][:, :], 1.0)
            v.tensor_scalar(bcol[:, 0:8], bcol[:, 0:8], 0.125, None, ALU.mult)
            for i in range(NT):
                mm = v.tensor_scalar(wq[i][:, :], wq[i][:, :], ln1c[:, i:i + 1], None, ALU.mult)
            ctx.tick(mm, "dve")

        nloop = [("kv", ch) for ch in range(16)] + [("q", ch) for ch in range(4)]

        for kind, ch in nloop:
            if kind == "kv":
                @block.sync
                def _(sync, ch=ch):
                    ctx.wait(sync, "pe", "dve")
                    ctx.dma(sync, r1bc[:, :], bass.AP(rows_dram, 2 * S + 512 * ch, [[0, 128], [1, 512]]))
                    for i in range(NT):
                        ctx.dma(sync, xdf[i][:, :],
                                bass.AP(sp_xd, i * 128 * S + 512 * ch, [[S, 128], [1, 512]]))
            else:
                @block.gpsimd
                def _(g, ch=ch):
                    ctx.wait(g, "pe", "dve")
                    rr = g.alloc_register(f"rq{ch}")
                    g.reg_load(rr, cfg_sb[0:1, 0:1] if ch < 2 else cfg_sb[0:1, 1:2])
                    ro = g.alloc_register(f"rqo{ch}")
                    g.reg_alu(ro, rr, (ch % 2) * 512, ALU.add)
                    rbase = g.alloc_register(f"rqb{ch}")
                    g.reg_alu(rbase, ro, 2 * S, ALU.add)
                    ctx.dma(g, r1bc[:, :], bass.AP(rows_dram, rbase, [[0, 128], [1, 512]]), sem="gd")
                    ctx.dma(g, xdf_all[:, :].rearrange("p (i w) -> p i w", i=NT, w=512),
                            bass.AP(sp_xd, ro, [[S, 128], [128 * S, NT], [1, 512]]), sem="gd")
                    g.wait_ge(ctx.sems["gd"], ctx.cnt["gd"])

            @block.vector
            def _(v, kind=kind):
                ctx.wait(v, "ld", "gd")
                for i in range(NT):
                    mm = v.tensor_tensor(rhs[i][:, :], xdf[i][:, :], r1bc[:, :], ALU.mult)
                ctx.tick(mm, "dve")

            ots = range(8, 16) if kind == "kv" else range(8)
            for ot in ots:
                pp = ot % 2

                @block.tensor
                def _(t, ot=ot, ch=ch, pp=pp, kind=kind):
                    ctx.wait(t, "dve", "act")
                    for i in range(NT):
                        mm = t.matmul(p_kv[pp][:, :], wq[i][:, 128 * ot:128 * (ot + 1)],
                                      rhs[i][:, :], start=(i == 0), stop=(i == NT - 1))
                    ctx.tick(mm, "pe")

                @block.vector
                def _(v, ot=ot, pp=pp, kind=kind):
                    ctx.wait(v, "pe", "ld")
                    ctx.tick(v.tensor_scalar(stg[pp][:, :], p_kv[pp][:, :],
                                             (0.125 if kind == "q" else 1.0), bcol[:, ot:ot + 1],
                                             ALU.mult, ALU.add), "dve")

                @block.scalar
                def _(sc, ot=ot, pp=pp, kind=kind):
                    ctx.wait(sc, "dve")
                    ctx.tick(sc.activation(sq[:, :], stg[pp][:, :], AF.Square), "act")

                @block.tensor
                def _(t):
                    ctx.wait(t, "act", "dve")
                    ctx.tick(t.matmul(p_n[:, :], h2[:, :], sq[:, :], start=True, stop=True), "pe")

                @block.vector
                def _(v, ot=ot, kind=kind):
                    ctx.wait(v, "pe")
                    col = (ot - 8) if kind == "kv" else (8 + ot)
                    v.reduce_max(knq[:, 16:17], p_n[:, :], AX.X)
                    v.drain()
                    ctx.tick(v.tensor_tensor(knq[:, col:col + 1], knq[:, col:col + 1],
                                             knq[:, 16:17], ALU.max), "dve")

                @block.sync
                def _(sync, ot=ot, ch=ch, pp=pp, kind=kind):
                    ctx.wait(sync, "dve")
                    if kind == "kv":
                        ctx.dma(sync, k_hbm.ap()[128 * (ot - 8):128 * (ot - 7), 512 * ch:512 * (ch + 1)],
                                stg[pp][:, :])
                    else:
                        ctx.dma(sync, q_hbm.ap()[128 * ot:128 * (ot + 1), 512 * ch:512 * (ch + 1)],
                                stg[pp][:, :])

            if kind == "kv":
                for tt in range(4):
                    kt = 4 * ch + tt
                    vs = vstg[kt % 2]
                    for hf in range(2):
                        @block.tensor
                        def _(t, tt=tt, hf=hf):
                            ctx.wait(t, "dve", "act")
                            for i in range(NT):
                                mm = t.matmul(p_vt[hf][:, :], rhs[i][:, 128 * tt:128 * (tt + 1)],
                                              wq[i][:, 2048 + 512 * hf:2048 + 512 * (hf + 1)],
                                              start=(i == 0), stop=(i == NT - 1))
                            ctx.tick(mm, "pe")

                        @block.vector
                        def _(v, hf=hf, vs=vs):
                            ctx.wait(v, "pe", "ld")
                            ctx.tick(v.tensor_tensor(
                                vs[:, :].rearrange("p (h w) -> p h w", h=16, w=80)[:, 8 * hf:8 * (hf + 1), 0:64],
                                p_vt[hf][:, :].rearrange("p (h d) -> p h d", h=8, d=64),
                                vb_bc[:, 512 * hf:512 * (hf + 1)].rearrange("p (h d) -> p h d", h=8, d=64),
                                ALU.add), "act")

                    @block.sync
                    def _(sync, kt=kt, vs=vs):
                        ctx.wait(sync, "act")
                        ctx.dma(sync, bass.AP(vtok_hbm, kt * 128 * 1280, [[1280, 128], [1, 1280]]),
                                vs[:, :])

        @block.scalar
        def _(sc):
            ctx.wait(sc, "dve")
            ctx.tick(sc.activation(knq[:, 16:32], knq[:, 0:16], AF.Sqrt), "act")

        @block.vector
        def _(v):
            ctx.wait(v, "act")
            v.tensor_tensor(knq[:, 0:8], knq[:, 16:24], knq[:, 24:32], ALU.mult)
            v.drain()
            ctx.tick(v.tensor_scalar(knq[:, 0:8], knq[:, 0:8], -1.0, None, ALU.mult), "dve")

        @block.sync
        def _(sync):
            ctx.wait(sync, "dve")
            ctx.dma(sync, bass.AP(small_dram, 8 * 64, [[1, 2], [2, 8]]), knq[:, 0:8],
                    allow_slow_non_contiguous=True)
            sync.wait_ge(ctx.sems["ld"], ctx.cnt["ld"])
            ctx.dma(sync, negc_bc[:, :], bass.AP(small_dram, 8 * 64, [[0, 128], [1, 16]]))
            sync.wait_ge(ctx.sems["ld"], ctx.cnt["ld"])


def _mha(nc, ctx, block, E, big, big_all, cfg_sb, negc_bc):
    sk_hbm, sq_hbm, sv_hbm = E["sk_hbm"], E["sq_hbm"], E["sv_hbm"]
    ctxacc, sums_dram = E["ctxacc"], E["sums_dram"]
    k_hbm, q_hbm, vtok_hbm = E["k_hbm"], E["q_hbm"], E["vtok_hbm"]
    with ExitStack() as s4:
        sb4 = lambda n, s, d=F32: s4.enter_context(nc.sbuf_tensor(n, s, d))
        negcp = sb4("negcp", [128, 8], F32)
        skt = [sb4(f"skt{i}", [128, 1024], BF16) for i in range(2)]
        sqt = [sb4(f"sqt{i}", [128, 1024], BF16) for i in range(2)]
        svt = [sb4(f"svt{i}", [128, 1280], BF16) for i in range(2)]
        Pab = sb4("Pab", [128, 2048], BF16)
        mskm = sb4("mskm", [128, 2048], BF16)
        pmsk = sb4("pmsk", [128, 18])
        abuf = [sb4(f"abuf{i}", [65, 1024], F32) for i in range(4)]
        zeros = sb4("zeros", [128, 1024], F32)
        p_scb = s4.enter_context(nc.psum_tensor("p_scb", [128, 2048], F32))
        p_cx = [s4.enter_context(nc.psum_tensor(f"p_cx{i}", [128, 1024], F32)) for i in range(2)]

        @block.sync
        def _(sync):
            ctx.wait(sync, "ld", "act", "dve")
            ctx.dma(sync, mskm[:, :], E["maskm"].ap())
            ctx.dma(sync, pmsk[:, :], E["posmask"].ap())
            r = sync.alloc_register("skr")
            for pos in range(9):
                sync.reg_load(r, cfg_sb[0:1, 8 + pos:9 + pos])
                ctx.dma(sync, sk_hbm.ap()[pos], bass.AP(k_hbm, r, [[S, 1024], [1, 1024]]))
            for pos in range(9):
                sync.reg_load(r, cfg_sb[0:1, 26 + pos:27 + pos])
                ctx.dma(sync, bass.AP(sv_hbm, pos * 8 * 128 * 1280, [[1, 1], [1, 8 * 128 * 1280]]),
                    bass.AP(vtok_hbm, r, [[1, 1], [1, 8 * 128 * 1280]]))

        @block.scalar
        def _(sc):
            ctx.wait(sc, "ld", "act", "dve")
            r = sc.alloc_register("sqr")
            for pos in range(9):
                sc.reg_load(r, cfg_sb[0:1, 17 + pos:18 + pos])
                ctx.dma(sc, sq_hbm.ap()[pos], bass.AP(q_hbm, r, [[2048, 1024], [1, 1024]]), sem="sd")

        @block.gpsimd
        def _(g):
            ctx.wait(g, "ld", "act", "dve", "sd")
            pass

        @block.vector
        def _(v):
            ctx.wait(v, "ld")
            for p8 in range(8):
                v.tensor_tensor(negcp[:, p8:p8 + 1], negc_bc[:, 2 * p8:2 * p8 + 1],
                                negc_bc[:, 2 * p8 + 1:2 * p8 + 2], ALU.min)
            ctx.tick(v.memset(zeros[:, :], 0.0), "dve")

        @block.gpsimd
        def _(g):
            ctx.wait(g, "dve")
            for t in range(32):
                ctx.dma(g, ctxacc.ap().rearrange("a b c d e -> (a b c) d e")[t][0:65, :],
                        zeros[0:65, :], sem="gd")
            g.wait_ge(ctx.sems["gd"], ctx.cnt["gd"])

        for pos in range(9):
            for pair in range(8):
                it = pos * 8 + pair
                db = it % 2

                @block.sync
                def _(sync, pos=pos, pair=pair, db=db, it=it):
                    if it < 2:
                        ctx.wait(sync, "gd", "sd", "ld")
                    else:
                        ctx.wait(sync, "pe")
                    ctx.dma(sync, skt[db][:, :], sk_hbm.ap()[pos][128 * pair:128 * (pair + 1), :])
                    ctx.dma(sync, sqt[db][:, :], sq_hbm.ap()[pos][128 * pair:128 * (pair + 1), :])
                    ctx.dma(sync, svt[db][:, :].rearrange("p (a w) -> p a w", a=8, w=160),
                            bass.AP(sv_hbm, pos * 8 * 128 * 1280 + 160 * pair,
                                    [[1280, 128], [128 * 1280, 8], [1, 160]]))

                for kt in range(8):
                    @block.tensor
                    def _(t, pos=pos, pair=pair, db=db, kt=kt, it=it):
                        if kt == 0:
                            ctx.wait(t, "ld")
                            ctx.wait(t, "act", "dve")
                        else:
                            ctx.wait(t, "act")
                            if pos >= 7:
                                ctx.wait(t, "dve")
                        for hf in range(2):
                            t.matmul(p_scb[:, 512 * hf:512 * (hf + 1)],
                                     skt[db][0:64, 128 * kt:128 * (kt + 1)],
                                     sqt[db][0:64, 512 * hf:512 * (hf + 1)],
                                     start=True, stop=True, tile_position=(0, 0))
                            mm = t.matmul(p_scb[:, 1024 + 512 * hf:1024 + 512 * (hf + 1)],
                                          skt[db][64:128, 128 * kt:128 * (kt + 1)],
                                          sqt[db][64:128, 512 * hf:512 * (hf + 1)],
                                          start=True, stop=True, tile_position=(64, 0))
                        ctx.tick(mm, "pe")

                    @block.scalar
                    def _(sc, pair=pair, kt=kt):
                        ctx.wait(sc, "pe")
                        ctx.tick(sc.activation(Pab[:, :], p_scb[:, :], AF.Exp,
                                               bias=negcp[:, pair:pair + 1]), "act")

                    if pos >= 7:
                        @block.vector
                        def _(v, kt=kt):
                            ctx.wait(v, "act")
                            o = 1024 - 128 * kt
                            v.tensor_tensor(Pab[:, 0:1024], Pab[:, 0:1024], mskm[:, o:o + 1024], ALU.mult)
                            ctx.tick(v.tensor_tensor(Pab[:, 1024:2048], Pab[:, 1024:2048],
                                                     mskm[:, o:o + 1024], ALU.mult), "dve")

                    @block.tensor
                    def _(t, pos=pos, db=db, kt=kt):
                        ctx.wait(t, "act")
                        if pos >= 7:
                            ctx.wait(t, "dve")
                        for hd in range(2):
                            for hf in range(2):
                                mm = t.matmul(p_cx[hd][0:65, 512 * hf:512 * (hf + 1)],
                                              svt[db][:, 160 * kt + 80 * hd:160 * kt + 80 * hd + 65],
                                              Pab[:, 1024 * hd + 512 * hf:1024 * hd + 512 * (hf + 1)],
                                              start=(kt == 0), stop=(kt == 7))
                        ctx.tick(mm, "pe")

                @block.vector
                def _(v, pos=pos, pair=pair):
                    ctx.wait(v, "pe", "gd")
                    v.tensor_scalar(abuf[0][:, :], p_cx[0][0:65, :], pmsk[0:65, pos:pos + 1], None, ALU.mult)
                    v.tensor_scalar(abuf[2][:, :], p_cx[1][0:65, :], pmsk[0:65, pos:pos + 1], None, ALU.mult)
                    v.tensor_scalar(abuf[1][:, :], p_cx[0][0:65, :], pmsk[0:65, 9 + pos:10 + pos], None, ALU.mult)
                    ctx.tick(v.tensor_scalar(abuf[3][:, :], p_cx[1][0:65, :], pmsk[0:65, 9 + pos:10 + pos],
                                             None, ALU.mult), "dve")

                if DEBUG and pos == 7 and pair == 0:
                    @block.sync
                    def _(sync):
                        ctx.wait(sync, "act", "dve")
                        ctx.dma(sync, E["dbg"]["Pa"], Pab[:, 0:1024])
                        ctx.dma(sync, E["dbg"]["ab0"], abuf[0][0:65, :])
                        ctx.dma(sync, E["dbg"]["ab1"], abuf[1][0:65, :])

                @block.gpsimd
                def _(g, pos=pos, pair=pair):
                    ctx.wait(g, "dve", "gd", "ld")
                    for hd in range(2):
                        for st in range(2):
                            ctx.dma(g, ctxacc.ap()[pair][hd][st][0:65, :], abuf[2 * hd + st][0:65, :],
                                    accum_op=ALU.add, sem="gd")

        ctxn = E["ctxn_tiles"]
        ctxf = sb4("ctxf", [128, 2048])
        rnbc = sb4("rnbc", [128, 2048])
        sums16 = sb4("sums16", [16, 2048])

        @block.gpsimd
        def _(g):
            ctx.wait(g, "gd")
            for st in range(2):
                ctx.dma(g, sums16[:, 1024 * st:1024 * (st + 1)],
                        bass.AP(ctxacc, 64 * 1024 + st * 65 * 1024, [[2 * 65 * 1024, 16], [1, 1024]]),
                        sem="gd")

        @block.vector
        def _(v):
            ctx.wait(v, "gd")
            ctx.tick(v.reciprocal(sums16[:, :], sums16[:, :]), "dve")

        @block.sync
        def _(sync):
            ctx.wait(sync, "dve")
            ctx.dma(sync, sums_dram.ap(), sums16[:, :])
            sync.wait_ge(ctx.sems["ld"], ctx.cnt["ld"])

        for pair in range(8):
            @block.sync
            def _(sync, pair=pair):
                ctx.wait(sync, "dve")
                ctx.dma(sync, rnbc[:, :],
                        bass.AP(sums_dram, 2 * pair * 2048, [[2048, 2], [0, 64], [1, 2048]]))

            @block.gpsimd
            def _(g, pair=pair):
                ctx.wait(g, "gd", "dve")
                for hd in range(2):
                    ctx.dma(g, ctxf[64 * hd:64 * (hd + 1), :].rearrange("p (s q) -> p s q", s=2, q=1024),
                            bass.AP(ctxacc, (pair * 2 + hd) * 2 * 65 * 1024,
                                    [[1024, 64], [65 * 1024, 2], [1, 1024]]), sem="gd")

            @block.vector
            def _(v, pair=pair):
                ctx.wait(v, "ld", "gd")
                ctx.tick(v.tensor_tensor(ctxn[pair][:, :], ctxf[:, :], rnbc[:, :], ALU.mult), "dve")

        if DEBUG:
            @block.sync
            def _(sync):
                ctx.wait(sync, "dve", "gd")
                ctx.dma(sync, E["dbg"]["acc2"], ctxacc.ap()[0][0][0][0:65, :])

        if DEBUG:
            @block.sync
            def _(sync):
                ctx.wait(sync, "dve")
                ctx.dma(sync, E["dbg"]["ctxn"], ctxn[0][:, :])
                ctx.dma(sync, E["dbg"]["sums"], sums16[:, :])


def _attnffn(nc, ctx, block, E, big, big_all, cfg_sb):
    sblk_dram, rows_dram = E["sblk_dram"], E["rows_dram"]
    ctxn = E["ctxn_tiles"]
    with ExitStack() as s5:
        sb5 = lambda n, s, d=F32: s5.enter_context(nc.sbuf_tensor(n, s, d))
        outw = [sb5(f"outw{i}", [128, 1024], BF16) for i in range(NT)]
        obcol = sb5("obcol", [128, 8])
        b2col = sb5("b2col", [128, 8])
        ln2c = sb5("ln2c", [128, 8])
        b1col = sb5("b1col", [128, 32])
        E["b1col_sb"] = b1col
        sbo = [sb5(f"sbo{i}", [128, 512]) for i in range(NT)]
        p_o = [s5.enter_context(nc.psum_tensor(f"p_o{i}", [128, 512], F32)) for i in range(2)]
        p_st = s5.enter_context(nc.psum_tensor("p_st", [1, 512], F32))

        @block.gpsimd
        def _(g):
            ctx.wait(g, "dve")
            for i in range(NT):
                ctx.dma(g, outw[i][:, :], E["outwT"].ap()[128 * i:128 * (i + 1), :], sem="gd")

        @block.sync
        def _(sync):
            for o in range(NT):
                ctx.dma(sync, obcol[:, o:o + 1], bass.AP(E["outb"], 128 * o, [[1, 128], [1, 1]]))
                ctx.dma(sync, b2col[:, o:o + 1], bass.AP(E["b2"], 128 * o, [[1, 128], [1, 1]]))
                ctx.dma(sync, ln2c[:, o:o + 1], bass.AP(E["ln2w"], 128 * o, [[1, 128], [1, 1]]))
            for o in range(32):
                ctx.dma(sync, b1col[:, o:o + 1], bass.AP(E["b1"], 128 * o, [[1, 128], [1, 1]]))

        for dt in range(NT):
            for ch in range(4):
                pp = (dt * 4 + ch) % 2

                @block.sync
                def _(sync, dt=dt, ch=ch):
                    if dt == 0 and ch == 0:
                        pass
                    ctx.wait(sync, "dve", "ld")
                    ctx.dma(sync, sbo[dt][:, :],
                            bass.AP(sblk_dram, dt * 128 * 2048 + 512 * ch, [[2048, 128], [1, 512]]))

                @block.tensor
                def _(t, dt=dt, ch=ch, pp=pp):
                    if dt == 0 and ch == 0:
                        ctx.wait(t, "gd", "dve")
                    else:
                        ctx.wait(t, "act", "dve")
                    for i in range(NT):
                        mm = t.matmul(p_o[pp][:, :], outw[i][:, 128 * dt:128 * (dt + 1)],
                                      ctxn[i][:, 512 * ch:512 * (ch + 1)],
                                      start=(i == 0), stop=(i == NT - 1))
                    ctx.tick(mm, "pe")

                @block.vector
                def _(v, dt=dt, ch=ch, pp=pp):
                    ctx.wait(v, "pe", "ld")
                    sl = slice(512 * ch, 512 * (ch + 1))
                    v.scalar_tensor_tensor(sbo[dt][:, :], p_o[pp][:, :], obcol[:, dt:dt + 1],
                                           sbo[dt][:, :], ALU.add, ALU.add)
                    v.drain()
                    ctx.tick(v.scalar_tensor_tensor(big[dt][:, sl], p_o[pp][:, :], obcol[:, dt:dt + 1],
                                                    big[dt][:, sl], ALU.add, ALU.add), "dve")

                @block.sync
                def _(sync, dt=dt, ch=ch):
                    ctx.wait(sync, "dve")
                    ctx.dma(sync, bass.AP(sblk_dram, dt * 128 * 2048 + 512 * ch, [[2048, 128], [1, 512]]),
                            sbo[dt][:, :])

        nrm = [sb5(f"nrm{i}", [128, 512], BF16) for i in range(NT)]
        htile = sb5("htile", [128, 32 * 512], BF16)
        w1g = [sb5(f"w1g{i}", [128, 8 * 128], BF16) for i in range(NT)]
        w2g = sb5("w2g", [128, 4096], BF16)
        rowf = sb5("rowf", [1, 3 * 512])
        r2bc = sb5("r2bc", [128, 512])
        sq5 = sb5("sq5", [128, 512])

        for ch in range(4):
            for i in range(NT):
                @block.scalar
                def _(sc, i=i, ch=ch):
                    ctx.wait(sc, "dve") if i == 0 else None
                    ctx.wait(sc, "pe")
                    ctx.tick(sc.activation(sq5[:, :], big[i][:, 512 * ch:512 * (ch + 1)], AF.Square), "act")

                @block.tensor
                def _(t, i=i):
                    ctx.wait(t, "act")
                    mm = t.matmul(p_st[:, :], E["ones_col_ref"][:, 0:1], sq5[:, :],
                                  start=(i == 0), stop=(i == NT - 1))
                    ctx.tick(mm, "pe")

            @block.vector
            def _(v):
                ctx.wait(v, "pe")
                v.tensor_scalar(rowf[:, 0:512], p_st[:, :], 1.0 / D, EPS_RMS, ALU.mult, ALU.add)
                ctx.tick(v.tensor_copy(rowf[:, 512:1024], rowf[:, 0:512]), "dve")

            @block.scalar
            def _(sc):
                ctx.wait(sc, "dve")
                ctx.tick(sc.activation(rowf[:, 1024:1536], rowf[:, 0:512], AF.Sqrt), "act")

            @block.vector
            def _(v):
                ctx.wait(v, "act")
                v.reciprocal(rowf[:, 512:1024], rowf[:, 1024:1536])
                v.drain()
                ctx.tick(_newton_rsqrt(v, rowf[:, 512:1024], rowf[:, 0:512], rowf[:, 1024:1536]), "dve")

            @block.sync
            def _(sync, ch=ch):
                ctx.wait(sync, "dve")
                ctx.dma(sync, bass.AP(rows_dram, 4 * S, [[S, 1], [1, 512]]), rowf[:, 512:1024])
                sync.wait_ge(ctx.sems["ld"], ctx.cnt["ld"])
                ctx.dma(sync, r2bc[:, :], bass.AP(rows_dram, 4 * S, [[0, 128], [1, 512]]))

            for i in range(NT):
                @block.vector
                def _(v, i=i, ch=ch):
                    if i == 0:
                        ctx.wait(v, "ld")
                    ctx.tick(v.scalar_tensor_tensor(nrm[i][:, :], big[i][:, 512 * ch:512 * (ch + 1)],
                                                    ln2c[:, i:i + 1], r2bc[:, :], ALU.mult, ALU.mult), "dve")

            for hg in range(4):
                @block.gpsimd
                def _(g, hg=hg, ch=ch):
                    ctx.wait(g, "pe")
                    for i in range(NT):
                        ctx.dma(g, w1g[i][:, :], E["w1T"].ap()[128 * i:128 * (i + 1),
                                                               1024 * hg:1024 * (hg + 1)], sem="gd")

                for hl in range(8):
                    ht = hg * 8 + hl
                    pp = ht % 2

                    @block.tensor
                    def _(t, hg=hg, hl=hl, pp=pp, ht=ht):
                        ctx.wait(t, "gd", "dve")
                        if ht >= 2:
                            ctx.wait(t, "act")
                        for i in range(NT):
                            mm = t.matmul(p_o[pp][:, :], w1g[i][:, 128 * hl:128 * (hl + 1)],
                                          nrm[i][:, :], start=(i == 0), stop=(i == NT - 1))
                        ctx.tick(mm, "pe")

                    @block.scalar
                    def _(sc, ht=ht, pp=pp):
                        ctx.wait(sc, "pe")
                        ctx.tick(sc.activation(htile[:, 512 * ht:512 * (ht + 1)], p_o[pp][:, :],
                                               AF.Gelu, bias=b1col[:, ht:ht + 1]), "act")

            for dt in range(NT):
                pp = dt % 2

                @block.gpsimd
                def _(g, dt=dt, ch=ch):
                    ctx.wait(g, "pe")
                    ctx.dma(g, w2g[:, :],
                            bass.AP(E["w2T"], 128 * dt, [[1024, 128], [128 * 1024, 32], [1, 128]]),
                            sem="gd")

                @block.sync
                def _(sync, dt=dt, ch=ch):
                    ctx.wait(sync, "dve", "ld")
                    ctx.dma(sync, sbo[dt][:, :],
                            bass.AP(sblk_dram, dt * 128 * 2048 + 512 * ch, [[2048, 128], [1, 512]]))

                @block.tensor
                def _(t, dt=dt, pp=pp):
                    ctx.wait(t, "gd", "act", "dve")
                    for ht in range(32):
                        mm = t.matmul(p_o[pp][:, :], w2g[:, 128 * ht:128 * (ht + 1)],
                                      htile[:, 512 * ht:512 * (ht + 1)],
                                      start=(ht == 0), stop=(ht == 31))
                    ctx.tick(mm, "pe")

                @block.vector
                def _(v, dt=dt, ch=ch, pp=pp):
                    ctx.wait(v, "pe", "ld")
                    ctx.tick(v.scalar_tensor_tensor(sbo[dt][:, :], p_o[pp][:, :], b2col[:, dt:dt + 1],
                                                    sbo[dt][:, :], ALU.add, ALU.add), "dve")

                @block.sync
                def _(sync, dt=dt, ch=ch):
                    ctx.wait(sync, "dve")
                    ctx.dma(sync, bass.AP(sblk_dram, dt * 128 * 2048 + 512 * ch, [[2048, 128], [1, 512]]),
                            sbo[dt][:, :])



QS = 0.08838834764831845


def _moa(nc, ctx, block, E, big, big_all, cfg_sb, ones_col):
    sp_xn, rows_dram, sm6 = E["sp_xn"], E["rows_dram"], E["sm6_dram"]
    kraw_hbm, kmix_hbm, vmoa_hbm = E["kraw_hbm"], E["kmix_hbm"], E["vmoa_hbm"]
    with ExitStack() as s6:
        sb6 = lambda n, s, d=F32: s6.enter_context(nc.sbuf_tensor(n, s, d))
        qt = [sb6(f"qt{e}", [128, 2048], BF16) for e in range(NT)]
        gates_sb = sb6("gates_sb", [128, 128])
        gT = sb6("gT", [8, 2048], BF16)
        relTb = sb6("relTb", [128, 129], BF16)
        tmpA = sb6("tmpA", [128, 512])

        with ExitStack() as sp6:
            sbp = lambda n, s, d=F32: sp6.enter_context(nc.sbuf_tensor(n, s, d))
            kwt = [sbp(f"kwt{i}", [128, 128], BF16) for i in range(NT)]
            vwt = [sbp(f"vwt{i}", [128, 128], BF16) for i in range(NT)]
            agwt = [sbp(f"agwt{i}", [128, 8], BF16) for i in range(NT)]
            qwt = [sbp(f"qwt{e}", [128, 8 * 128], BF16) for e in range(NT)]
            xnb = [sbp(f"xnb{i}", [128, 512], BF16) for i in range(NT)]
            relT_sb = sbp("relT_sb", [128, 129])
            cols6 = sbp("cols6", [128, 32])
            fl1b_bf = sbp("fl1b_bf", [128, 8], BF16)
            ucol = sbp("ucol", [128, 8])
            u0q_c = sbp("u0q_c", [128, 8])
            u0v_bc = sbp("u0v_bc", [128, 129], BF16)
            u0g_bc = sbp("u0g_bc", [128, 8])
            qbcol = sbp("qbcol", [128, 8])
            r2bc = sbp("r2bc6", [128, 512])
            r2col = sbp("r2col", [128, 4])
            kstg = sbp("kstg", [128, 512], BF16)
            vstg = sbp("vstg", [128, 144], BF16)
            zero6 = sbp("zero6", [128, 1280], BF16)
            lgt = sbp("lgt", [128, 32])
            p_a = [sp6.enter_context(nc.psum_tensor(f"p_a{i}", [128, 512], F32)) for i in range(2)]
            p_b = sp6.enter_context(nc.psum_tensor("p_b", [128, 144], F32))

            CFL1W, CFL1B, CKB, CVB, CFL1W8 = 0, 8, 16, 17, 18

            @block.sync
            def _(sync):
                ctx.wait(sync, "dve", "act", "pe")
                for i in range(NT):
                    ctx.dma(sync, cols6[:, CFL1W + i:CFL1W + i + 1], bass.AP(E["fln1w"], 128 * i, [[1, 128], [1, 1]]))
                    ctx.dma(sync, cols6[:, CFL1B + i:CFL1B + i + 1], bass.AP(E["fln1b"], 128 * i, [[1, 128], [1, 1]]))
                ctx.dma(sync, cols6[:, CKB:CKB + 1], bass.AP(E["kb"], 0, [[1, 128], [1, 1]]))
                ctx.dma(sync, cols6[:, CVB:CVB + 1], bass.AP(E["vb"], 0, [[1, 128], [1, 1]]))
                for e in range(NT):
                    ctx.dma(sync, qbcol[:, e:e + 1], bass.AP(E["qb"], 128 * e, [[1, 128], [1, 1]]))
                ctx.dma(sync, relT_sb[:, 0:129], E["relT"].ap())

            @block.gpsimd
            def _(g):
                ctx.wait(g, "ld", "dve", "act", "pe")
                for i in range(NT):
                    ctx.dma(g, kwt[i][:, :], E["kw"].ap()[128 * i:128 * (i + 1), :], sem="gd")
                    ctx.dma(g, vwt[i][:, :], E["vw"].ap()[128 * i:128 * (i + 1), :], sem="gd")
                    ctx.dma(g, agwt[i][:, :], E["agw"].ap()[128 * i:128 * (i + 1), :], sem="gd")
                for e in range(NT):
                    for i in range(NT):
                        ctx.dma(g, qwt[e][:, 128 * i:128 * (i + 1)],
                                E["qw"].ap()[e][128 * i:128 * (i + 1), :], sem="gd")

            @block.vector
            def _(v):
                ctx.wait(v, "ld", "gd")
                v.memset(zero6[:, :], 0.0)
                v.memset(vstg[:, :], 0.0)
                v.tensor_scalar(cols6[:, CFL1W8:CFL1W8 + 8], cols6[:, CFL1W:CFL1W + 8], QS, None, ALU.mult)
                v.tensor_copy(fl1b_bf[:, :], cols6[:, CFL1B:CFL1B + 8])
                ctx.tick(v.tensor_copy(relTb[:, 0:129], relT_sb[:, 0:129]), "dve")

            @block.tensor
            def _(t):
                ctx.wait(t, "dve", "gd")
                for i in range(NT):
                    mk = t.matmul(p_a[0][:, 0:1], kwt[i][:, :], fl1b_bf[:, i:i + 1],
                                  start=(i == 0), stop=(i == NT - 1))
                ctx.tick(mk, "pe")
                for i in range(NT):
                    mv = t.matmul(p_a[1][:, 0:1], vwt[i][:, :], fl1b_bf[:, i:i + 1],
                                  start=(i == 0), stop=(i == NT - 1))
                ctx.tick(mv, "pe")
                for e in range(NT):
                    for i in range(NT):
                        mq = t.matmul(p_b[:, e:e + 1], qwt[e][:, 128 * i:128 * (i + 1)], fl1b_bf[:, i:i + 1],
                                      start=(i == 0), stop=(i == NT - 1))
                ctx.tick(mq, "pe")
                for i in range(NT):
                    mg = t.matmul(p_b[0:8, 128:129], agwt[i][:, :], fl1b_bf[:, i:i + 1],
                                  start=(i == 0), stop=(i == NT - 1))
                ctx.tick(mg, "pe")

            @block.vector
            def _(v):
                ctx.wait(v, "pe")
                v.tensor_scalar(ucol[:, 0:1], p_a[0][:, 0:1], cols6[:, CKB:CKB + 1], None, ALU.add)
                v.tensor_scalar(ucol[:, 1:2], p_a[1][:, 0:1], cols6[:, CVB:CVB + 1], None, ALU.add)
                v.memset(ucol[:, 4:5], 0.0)
                v.tensor_tensor(u0q_c[:, :], p_b[:, 0:8], qbcol[:, :], ALU.add)
                v.drain()
                v.tensor_scalar(u0q_c[:, :], u0q_c[:, :], QS, None, ALU.mult)
                ctx.tick(v.tensor_copy(ucol[0:8, 3:4], p_b[0:8, 128:129]), "dve")

            @block.sync
            def _(sync):
                ctx.wait(sync, "dve")
                ctx.dma(sync, bass.AP(sm6, 0, [[1, 64], [1, 1]]), ucol[0:64, 1:2])
                ctx.dma(sync, bass.AP(sm6, 65, [[1, 64], [1, 1]]), ucol[64:128, 1:2])
                ctx.dma(sync, bass.AP(sm6, 64, [[1, 1], [1, 1]]), ucol[0:1, 4:5])
                ctx.dma(sync, bass.AP(sm6, 192, [[1, 8], [1, 1]]), ucol[0:8, 3:4])
                ctx.dma(sync, bass.AP(sm6, 256, [[1, 128], [1, 1]]), relT_sb[:, 0:1])
                ctx.dma(sync, bass.AP(sm6, 384, [[1, 128], [1, 1]]), relT_sb[:, 128:129])
                sync.wait_ge(ctx.sems["ld"], ctx.cnt["ld"])
                ctx.dma(sync, u0g_bc[:, :], bass.AP(sm6, 192, [[0, 128], [1, 8]]))

            @block.gpsimd
            def _(g):
                ctx.wait(g, "ld")
                ctx.dma(g, u0v_bc[:, :], bass.AP(sm6, 0, [[0, 128], [1, 129]]), sem="gd")

            @block.vector
            def _(v):
                ctx.wait(v, "ld", "dve")
                for i in range(NT):
                    v.tensor_scalar(kwt[i][:, :], kwt[i][:, :], cols6[:, CFL1W + i:CFL1W + i + 1], None, ALU.mult)
                    v.tensor_scalar(vwt[i][:, :], vwt[i][:, :], cols6[:, CFL1W + i:CFL1W + i + 1], None, ALU.mult)
                    v.tensor_scalar(agwt[i][:, :], agwt[i][:, :], cols6[:, CFL1W + i:CFL1W + i + 1], None, ALU.mult)
                    for e in range(NT):
                        mm = v.tensor_scalar(qwt[e][:, 128 * i:128 * (i + 1)], qwt[e][:, 128 * i:128 * (i + 1)],
                                             cols6[:, CFL1W8 + i:CFL1W8 + i + 1], None, ALU.mult)
                ctx.tick(mm, "dve")

            for ch in range(16):
                @block.gpsimd
                def _(g, ch=ch):
                    ctx.wait(g, "pe", "dve")
                    for i in range(NT):
                        ctx.dma(g, xnb[i][:, :], bass.AP(sp_xn, i * 128 * S + 512 * ch, [[S, 128], [1, 512]]),
                                sem="gd")

                @block.sync
                def _(sync, ch=ch):
                    ctx.wait(sync, "dve")
                    ctx.dma(sync, r2bc[:, :], bass.AP(rows_dram, 3 * S + 512 * ch, [[0, 128], [1, 512]]))
                    for tt in range(4):
                        ctx.dma(sync, r2col[:, tt:tt + 1],
                                bass.AP(rows_dram, 3 * S + 512 * ch + 128 * tt, [[1, 128], [1, 1]]))

                @block.tensor
                def _(t, ch=ch):
                    ctx.wait(t, "gd", "ld", "dve", "act")
                    for i in range(NT):
                        mm = t.matmul(p_a[0][:, :], kwt[i][:, :], xnb[i][:, :],
                                      start=(i == 0), stop=(i == NT - 1))
                    ctx.tick(mm, "pe")

                @block.vector
                def _(v, ch=ch):
                    ctx.wait(v, "pe", "ld")
                    v.tensor_tensor(tmpA[:, :], p_a[0][:, :], r2bc[:, :], ALU.mult)
                    v.drain()
                    ctx.tick(v.tensor_scalar(kstg[:, :], tmpA[:, :], ucol[:, 0:1], None, ALU.add), "dve")

                @block.sync
                def _(sync, ch=ch):
                    ctx.wait(sync, "dve")
                    ctx.dma(sync, kraw_hbm.ap()[:, 512 * ch:512 * (ch + 1)], kstg[:, :])

                for tt in range(4):
                    kt = 4 * ch + tt

                    @block.tensor
                    def _(t, tt=tt, ch=ch):
                        ctx.wait(t, "act", "dve")
                        for i in range(NT):
                            mm = t.matmul(p_b[:, 0:128], xnb[i][:, 128 * tt:128 * (tt + 1)], vwt[i][:, :],
                                          start=(i == 0), stop=(i == NT - 1))
                        ctx.tick(mm, "pe")

                    @block.scalar
                    def _(sc, tt=tt, kt=kt):
                        ctx.wait(sc, "pe", "ld", "dve")
                        sc.activation(vstg[:, 0:64], p_b[:, 0:64], AF.Identity,
                                      scale=r2col[:, tt:tt + 1])
                        sc.activation(vstg[:, 65:129], p_b[:, 64:128], AF.Identity,
                                      scale=r2col[:, tt:tt + 1])
                        ctx.tick(sc.activation(vstg[:, 64:65], zero6[0:128, 0:1], AF.Identity,
                                               bias=1.0), "act")

                    @block.vector
                    def _(v, kt=kt):
                        ctx.wait(v, "act", "gd")
                        ctx.tick(v.tensor_tensor(vstg[:, 0:129], vstg[:, 0:129],
                                                 u0v_bc[:, :], ALU.add), "dve")

                    @block.sync
                    def _(sync, kt=kt):
                        ctx.wait(sync, "dve")
                        ctx.dma(sync, vmoa_hbm.ap()[10 + kt], vstg[:, :])

            @block.sync
            def _(sync):
                for sl in list(range(10)) + list(range(74, 84)):
                    ctx.dma(sync, vmoa_hbm.ap()[sl], zero6[:, 0:144])
                for qg in range(2):
                    ctx.dma(sync, bass.AP(kmix_hbm, qg * 128 * (2560 + S), [[2560 + S, 128], [1, 1280]]),
                            zero6[:, :])
                    ctx.dma(sync, bass.AP(kmix_hbm, qg * 128 * (2560 + S) + 1280 + S, [[2560 + S, 128], [1, 1280]]),
                            zero6[:, :])

            E["moa_env"] = dict(sb6=sbp, qt=qt, xnb=xnb, gates_sb=gates_sb, gT=gT,
                                u0q_c=u0q_c, u0g_bc=u0g_bc, tmpA=tmpA,
                                p_a=p_a, p_b=p_b, agwt=agwt, qwt=qwt, lgt=lgt)
            _moa_q_gates(nc, ctx, block, E, cfg_sb, ones_col)

        E["moa_env2"] = dict(qt=qt, gT=gT, gates_sb=gates_sb, relTb=relTb, tmpA=tmpA, sb6=sb6)
        _moa_attn(nc, ctx, block, E, cfg_sb, ones_col, big)


def _moa_q_gates(nc, ctx, block, E, cfg_sb, ones_col):
    M = E["moa_env"]
    rows_dram, xnown_dram = E["rows_dram"], E["xnown_dram"]
    sp_xn = E["sp_xn"]
    xnb, qt, qwt, agwt = M["xnb"], M["qt"], M["qwt"], M["agwt"]
    gates_sb, gT = M["gates_sb"], M["gT"]
    tmpA, p_a, p_b = M["tmpA"], M["p_a"], M["p_b"]
    u0q_c, u0g_bc, lgt = M["u0q_c"], M["u0g_bc"], M["lgt"]
    sb6 = M["sb6"]
    r2bco = sb6("r2bco", [128, 512])
    r2colo = sb6("r2colo", [128, 4])
    eqs = sb6("eqs", [128, 256])
    dcols = sb6("dcols", [128, 48])

    @block.scalar
    def _(sc):
        ctx.wait(sc, "ld", "dve", "act", "pe", "gd")
        rA = sc.alloc_register("moxA")
        rB = sc.alloc_register("moxB")
        sc.reg_load(rA, cfg_sb[0:1, 0:1])
        sc.reg_load(rB, cfg_sb[0:1, 1:2])
        for k, rr in ((0, rA), (1024, rB)):
            ctx.dma(sc, bass.AP(xnown_dram, k, [[2048, 1024], [1, 1024]]),
                    bass.AP(sp_xn, rr, [[S, 1024], [1, 1024]]), sem="sd")
        r3 = sc.alloc_register("mor3")
        for k, rr in ((0, rA), (1024, rB)):
            sc.reg_alu(r3, rr, 3 * S, ALU.add)
            ctx.dma(sc, bass.AP(rows_dram, 5 * S + k, [[S, 1], [1, 1024]]),
                    bass.AP(rows_dram, r3, [[S, 1], [1, 1024]]), sem="sd")
        sc.wait_ge(ctx.sems["sd"], ctx.cnt["sd"])

    for ch in range(4):
        @block.gpsimd
        def _(g, ch=ch):
            ctx.wait(g, "pe", "dve", "sd")
            for i in range(NT):
                ctx.dma(g, xnb[i][:, :], bass.AP(xnown_dram, i * 128 * 2048 + 512 * ch,
                                                 [[2048, 128], [1, 512]]), sem="gd")

        @block.sync
        def _(sync, ch=ch):
            ctx.wait(sync, "dve", "sd")
            ctx.dma(sync, r2bco[:, :], bass.AP(rows_dram, 5 * S + 512 * ch, [[0, 128], [1, 512]]))
            for tt4 in range(4):
                ctx.dma(sync, r2colo[:, tt4:tt4 + 1],
                        bass.AP(rows_dram, 5 * S + 512 * ch + 128 * tt4, [[1, 128], [1, 1]]))

        for e in range(NT):
            @block.tensor
            def _(t, e=e, ch=ch):
                ctx.wait(t, "gd", "ld", "dve", "act")
                for i in range(NT):
                    mm = t.matmul(p_a[e % 2][:, :], qwt[e][:, 128 * i:128 * (i + 1)], xnb[i][:, :],
                                  start=(i == 0), stop=(i == NT - 1))
                ctx.tick(mm, "pe")

            @block.vector
            def _(v, e=e, ch=ch):
                ctx.wait(v, "pe")
                v.tensor_tensor(tmpA[:, :], p_a[e % 2][:, :], r2bco[:, :], ALU.mult)
                v.drain()
                ctx.tick(v.tensor_scalar(qt[e][:, 512 * ch:512 * (ch + 1)], tmpA[:, :],
                                         u0q_c[:, e:e + 1], None, ALU.add), "dve")

        for tt in range(4):
            ti = 4 * ch + tt

            @block.tensor
            def _(t, tt=tt, ti=ti):
                ctx.wait(t, "dve", "act")
                for i in range(NT):
                    mm = t.matmul(p_b[:, 0:8], xnb[i][:, 128 * tt:128 * (tt + 1)], agwt[i][:, :],
                                  start=(i == 0), stop=(i == NT - 1))
                ctx.tick(mm, "pe")

            @block.vector
            def _(v, tt=tt, ti=ti):
                ctx.wait(v, "pe")
                lg = lgt[:, 0:8]
                v.tensor_scalar(lg, p_b[:, 0:8], r2colo[:, tt:tt + 1], None, ALU.mult)
                v.drain()
                v.tensor_tensor(lg, lg, u0g_bc[:, :], ALU.add)
                v.drain()
                m1, m2, msk = lgt[:, 8:9], lgt[:, 9:10], lgt[:, 10:18]
                v.reduce_max(m1, lg, AX.X)
                v.drain()
                v.tensor_scalar(eqs[:, 16 * ti:16 * ti + 8], lg, m1, None, ALU.is_equal)
                v.drain()
                v.scalar_tensor_tensor(msk, eqs[:, 16 * ti:16 * ti + 8], -1e30, lg, ALU.mult, ALU.add)
                v.drain()
                v.reduce_max(m2, msk, AX.X)
                v.drain()
                v.tensor_scalar(eqs[:, 16 * ti + 8:16 * ti + 16], lg, m2, None, ALU.is_equal)
                ctx.tick(v.tensor_tensor(dcols[:, ti:ti + 1], m2, m1, ALU.subtract), "dve")

    @block.scalar
    def _(sc):
        ctx.wait(sc, "dve")
        ctx.tick(sc.activation(dcols[:, 16:32], dcols[:, 0:16], AF.Exp), "act")

    @block.vector
    def _(v):
        ctx.wait(v, "act")
        v.tensor_scalar(dcols[:, 32:48], dcols[:, 16:32], 1.0, None, ALU.add)
        v.drain()
        v.reciprocal(dcols[:, 32:48], dcols[:, 32:48])
        v.drain()
        v.tensor_tensor(dcols[:, 16:32], dcols[:, 16:32], dcols[:, 32:48], ALU.mult)
        v.drain()
        for ti in range(16):
            v.tensor_scalar(gates_sb[:, 8 * ti:8 * (ti + 1)], eqs[:, 16 * ti:16 * ti + 8],
                            dcols[:, 32 + ti:33 + ti], None, ALU.mult)
            v.drain()
            mm = v.scalar_tensor_tensor(gates_sb[:, 8 * ti:8 * (ti + 1)],
                                        eqs[:, 16 * ti + 8:16 * ti + 16],
                                        dcols[:, 16 + ti:17 + ti],
                                        gates_sb[:, 8 * ti:8 * (ti + 1)], ALU.mult, ALU.add)
            v.drain()
        ctx.tick(mm, "dve")

    gTf = sb6("gTf", [8, 2048])
    for ti in range(16):
        @block.tensor
        def _(t, ti=ti):
            ctx.wait(t, "dve")
            ctx.tick(t.matmul(p_b[0:8, 0:128], gates_sb[:, 8 * ti:8 * (ti + 1)],
                              E["identp_sb"][:, :], is_transpose=True, start=True, stop=True), "pe")

        @block.vector
        def _(v, ti=ti):
            ctx.wait(v, "pe")
            v.tensor_copy(gT[0:8, 128 * ti:128 * (ti + 1)], p_b[0:8, 0:128])
            ctx.tick(v.tensor_copy(gTf[0:8, 128 * ti:128 * (ti + 1)], p_b[0:8, 0:128]), "dve")

    @block.sync
    def _(sync):
        ctx.wait(sync, "dve")
        ctx.dma(sync, E["gmoa_dram"].ap(), gTf[:, :])


def _moa_attn(nc, ctx, block, E, cfg_sb, ones_col, big):
    M = E["moa_env2"]
    qt, gT, relTb, tmpA = M["qt"], M["gT"], M["relTb"], M["tmpA"]
    sb6 = M["sb6"]
    sm6, rows_dram = E["sm6_dram"], E["rows_dram"]
    kraw_hbm, kmix_hbm, vmoa_hbm, vpad_hbm = E["kraw_hbm"], E["kmix_hbm"], E["vmoa_hbm"], E["vpad_hbm"]
    xnown_dram, gmoa_dram, dbg = E["xnown_dram"], E["gmoa_dram"], E["dbg"]
    xdown_dram = E["xdown_dram"]
    identp = E["identp_sb"]

    kmix = sb6("kmix", [128, S], BF16)
    vrot = sb6("vrot", [128, 64 * 129], BF16)
    relw = sb6("relw", [128, 512], BF16)
    band = sb6("band", [128, 8 * 384], BF16)
    kband = sb6("kband", [128, 1280], BF16)
    vband = sb6("vband", [128, 10 * 129], BF16)
    rel0r = sb6("rel0r", [1, 128], BF16)
    rel128r = sb6("rel128r", [1, 128], BF16)
    indl_c = sb6("indl_c", [1, 512], BF16)
    indh_c = sb6("indh_c", [1, 512], BF16)
    onesrow = sb6("onesrow", [1, 128], BF16)
    qrel0 = sb6("qrel0", [1, 1024], BF16)
    qrel128 = sb6("qrel128", [1, 1024], BF16)
    Pm = [sb6(f"Pm{i}", [128, 1024], BF16) for i in range(2)]
    fwork = sb6("fwork", [128, 1024])
    rngbc = sb6("rngbc", [128, 1024])
    cAe = sb6("cAe", [65, 1024])
    cBe = sb6("cBe", [65, 1024])
    rrow = sb6("rrow", [1, 1024])
    grow = sb6("grow", [1, 1024])
    negce = sb6("negce", [128, 8])
    nrow = sb6("nrow", [1, 64])
    krf = sb6("krf", [128, 512])
    bnd32 = sb6("bnd32", [128, 384])
    identb = sb6("identb", [128, 128], BF16)
    gc = [sb6(f"gc{e}", [128, 1024], BF16) for e in range(NT)]
    owt = sb6("owt", [128, 1024], BF16)
    obT = sb6("obT", [8, 1024], BF16)

    with ExitStack() as s7:
        p_s = [s7.enter_context(nc.psum_tensor(f"p_s6{i}", [128, 1024], F32)) for i in range(2)]
        p_cA = s7.enter_context(nc.psum_tensor("p_cA", [65, 1024], F32))
        p_cB = s7.enter_context(nc.psum_tensor("p_cB", [65, 1024], F32))

        @block.sync
        def _(sync):
            ctx.wait(sync, "dve", "act", "pe", "gd", "sd")
            sync.wait_ge(ctx.sems["ld"], ctx.cnt["ld"])
            ctx.dma(sync, vrot[:, :].rearrange("p (a w) -> p a w", a=64, w=129),
                    bass.AP(vmoa_hbm, 10 * 128 * 144, [[144, 128], [128 * 144, 64], [1, 129]]))

        @block.gpsimd
        def _(g):
            ctx.wait(g, "dve", "act", "pe", "sd")
            g.wait_ge(ctx.sems["ld"], ctx.cnt["ld"])
            ctx.dma(g, rel0r[:, :], bass.AP(sm6, 256, [[1, 1], [1, 128]]), sem="gd")
            ctx.dma(g, rel128r[:, :], bass.AP(sm6, 384, [[1, 1], [1, 128]]), sem="gd")
            ctx.dma(g, obT[:, :], E["ob"].ap(), sem="gd")

        @block.vector
        def _(v):
            ctx.wait(v, "ld", "pe", "act", "gd", "sd")
            v.memset(onesrow[:, :], 1.0)
            v.tensor_copy(identb[:, :], identp[:, :])
            v.tensor_copy(relw[:, 192:321], relTb[:, 0:129])
            v.tensor_copy(relw[:, 0:192], relTb[:, 0:1].broadcast_to([128, 192]))
            ctx.tick(v.tensor_copy(relw[:, 321:512], relTb[:, 128:129].broadcast_to([128, 191])), "dve")

        def _sq_chunk(src_getter, col, first, width=512):
            @block.scalar
            def _(sc):
                ctx.wait(sc, "pe", "ld", "dve")
                ctx.tick(sc.activation(tmpA[:, 0:width], src_getter(), AF.Square), "act")

            @block.tensor
            def _(t):
                ctx.wait(t, "act")
                ctx.tick(t.matmul(p_s[0][0:1, 0:width], ones_col[:, 0:1], tmpA[:, 0:width],
                                  start=True, stop=True), "pe")

            @block.vector
            def _(v):
                ctx.wait(v, "pe")
                v.reduce_max(nrow[0:1, 63:64], p_s[0][0:1, 0:width], AX.X)
                v.drain()
                if first:
                    mm = v.tensor_copy(nrow[0:1, col:col + 1], nrow[0:1, 63:64])
                else:
                    mm = v.tensor_tensor(nrow[0:1, col:col + 1], nrow[0:1, col:col + 1],
                                         nrow[0:1, 63:64], ALU.max)
                ctx.tick(mm, "dve")

        for e in range(NT):
            for chq in range(4):
                _sq_chunk(lambda e=e, chq=chq: qt[e][:, 512 * chq:512 * (chq + 1)], e, chq == 0)
        for ch in range(16):
            @block.sync
            def _(sync, ch=ch):
                ctx.wait(sync, "act")
                ctx.dma(sync, Pm[1][:, 0:512], kraw_hbm.ap()[:, 512 * ch:512 * (ch + 1)])
            _sq_chunk(lambda: Pm[1][:, 0:512], 16, ch == 0)
        _sq_chunk(lambda: relTb[:, 0:129], 17, True, width=129)

        @block.scalar
        def _(sc):
            ctx.wait(sc, "dve")
            ctx.tick(sc.activation(nrow[0:1, 32:50], nrow[0:1, 0:18], AF.Sqrt), "act")

        @block.vector
        def _(v):
            ctx.wait(v, "act")
            v.tensor_tensor(nrow[0:1, 20:21], nrow[0:1, 48:49], nrow[0:1, 49:50], ALU.add)
            v.drain()
            v.tensor_scalar(nrow[0:1, 24:32], nrow[0:1, 32:40], nrow[0:1, 20:21], None, ALU.mult)
            v.drain()
            ctx.tick(v.tensor_scalar(nrow[0:1, 24:32], nrow[0:1, 24:32], -1.0, None, ALU.mult), "dve")

        @block.sync
        def _(sync):
            ctx.wait(sync, "dve")
            ctx.dma(sync, bass.AP(sm6, 512, [[1, 1], [1, 8]]), nrow[0:1, 24:32])
            sync.wait_ge(ctx.sems["ld"], ctx.cnt["ld"])
            ctx.dma(sync, negce[:, :], bass.AP(sm6, 512, [[0, 128], [1, 8]]))

        for qg in range(2):
            qgs = qg * 1024

            for ch in range(16):
                @block.gpsimd
                def _(g, ch=ch):
                    ctx.wait(g, "dve")
                    ctx.dma(g, krf[:, :], kraw_hbm.ap()[:, 512 * ch:512 * (ch + 1)], sem="gd")

                @block.sync
                def _(sync, ch=ch, qg=qg):
                    ctx.wait(sync, "pe")
                    ctx.dma(sync, indl_c[:, :], bass.AP(E["indl"], qg * S + 512 * ch, [[S, 1], [1, 512]]))
                    ctx.dma(sync, indh_c[:, :], bass.AP(E["indh"], qg * S + 512 * ch, [[S, 1], [1, 512]]))

                @block.tensor
                def _(t, ch=ch, qg=qg):
                    ctx.wait(t, "dve", "act", "gd", "ld")
                    t.matmul(p_s[0][:, 0:512], rel0r[0:1, :], indl_c[0:1, :],
                             start=True, stop=False)
                    ctx.tick(t.matmul(p_s[0][:, 0:512], rel128r[0:1, :], indh_c[0:1, :],
                                      start=False, stop=True), "pe")

                @block.vector
                def _(v, ch=ch):
                    ctx.wait(v, "pe", "gd")
                    ctx.tick(v.tensor_tensor(kmix[:, 512 * ch:512 * (ch + 1)], p_s[0][:, 0:512],
                                             krf[:, :], ALU.add), "dve")

            @block.sync
            def _(sync, qg=qg):
                ctx.wait(sync, "dve")
                ctx.dma(sync, bass.AP(kmix_hbm, qg * 128 * (2560 + S) + 1280, [[2560 + S, 128], [1, S]]),
                        kmix[:, :])

            @block.scalar
            def _(sc, qg=qg):
                ctx.wait(sc, "ld", "pe")
                rk = sc.alloc_register(f"mbk{qg}")
                sc.reg_load(rk, cfg_sb[0:1, 37 + qg:38 + qg])
                ctx.dma(sc, kband[:, :], bass.AP(kmix_hbm, rk, [[2560 + S, 128], [1, 1280]]), sem="sd")
                rv = sc.alloc_register(f"mbv{qg}")
                sc.reg_load(rv, cfg_sb[0:1, 35 + qg:36 + qg])
                ctx.dma(sc, vband[:, :].rearrange("p (a w) -> p a w", a=10, w=129),
                        bass.AP(vmoa_hbm, rv, [[144, 128], [128 * 144, 10], [1, 129]]), sem="sd")
                sc.wait_ge(ctx.sems["sd"], ctx.cnt["sd"])

            vp_pe, vp_dve, vp_ld = [], [], []
            for e in range(NT):
                for qtl in range(8):
                    i = len(vp_pe)

                    @block.tensor
                    def _(t, e=e, qtl=qtl, qg=qg, i=i):
                        if i == 0:
                            ctx.wait(t, "dve", "ld")
                        if i >= 2:
                            t.wait_ge(ctx.sems["dve"], vp_dve[i - 2])
                        ctx.tick(t.matmul(p_s[i % 2][:, 0:512],
                                          qt[e][:, qgs + 128 * qtl:qgs + 128 * (qtl + 1)],
                                          relw[:, :], start=True, stop=True), "pe")
                        vp_pe.append(ctx.cnt["pe"])

                    @block.vector
                    def _(v, e=e, qtl=qtl, i=i):
                        v.wait_ge(ctx.sems["pe"], vp_pe[i])
                        if i >= 2:
                            v.wait_ge(ctx.sems["ld"], vp_ld[i - 2])
                        ctx.tick(v.tensor_copy(Pm[i % 2][:, 0:512], p_s[i % 2][:, 0:512]), "dve")
                        vp_dve.append(ctx.cnt["dve"])

                    @block.sync
                    def _(sync, e=e, qtl=qtl, i=i):
                        sync.wait_ge(ctx.sems["dve"], vp_dve[i])
                        ctx.dma(sync, vpad_hbm.ap()[e][qtl], Pm[i % 2][:, 0:512])
                        vp_ld.append(ctx.cnt["ld"])

            for e in range(NT):
                @block.tensor
                def _(t, e=e, qg=qg):
                    ctx.wait(t, "dve", "ld", "act")
                    for hf in range(2):
                        t.matmul(p_s[0][0:1, 512 * hf:512 * (hf + 1)], relTb[:, 0:1],
                                 qt[e][:, qgs + 512 * hf:qgs + 512 * (hf + 1)], start=True, stop=True)
                        mm = t.matmul(p_s[1][0:1, 512 * hf:512 * (hf + 1)], relTb[:, 128:129],
                                      qt[e][:, qgs + 512 * hf:qgs + 512 * (hf + 1)], start=True, stop=True)
                    ctx.tick(mm, "pe")

                @block.vector
                def _(v, e=e):
                    ctx.wait(v, "pe")
                    v.tensor_copy(qrel0[:, :], p_s[0][0:1, 0:1024])
                    ctx.tick(v.tensor_copy(qrel128[:, :], p_s[1][0:1, 0:1024]), "dve")

                @block.sync
                def _(sync, e=e, qg=qg):
                    ctx.wait(sync, "pe")
                    for qtl in range(8):
                        ctx.dma(sync, band[:, 384 * qtl:384 * (qtl + 1)],
                                bass.AP(vpad_hbm, (e * 8 + qtl) * 65536 + 128, [[511, 128], [1, 384]]))

                acts_l, pes_l = [], []

                def _exp_ctx(j, e=e, qg=qg, acts_l=acts_l, pes_l=pes_l):
                    @block.scalar
                    def _(sc, j=j, e=e):
                        sc.wait_ge(ctx.sems["pe"], pes_l[j])
                        if j == 0:
                            ctx.wait(sc, "dve")
                        ctx.tick(sc.activation(Pm[j % 2][:, :], p_s[j % 2][:, :], AF.Exp,
                                               bias=negce[:, e:e + 1]), "act")
                        acts_l.append(ctx.cnt["act"])

                    @block.tensor
                    def _(t, j=j):
                        t.wait_ge(ctx.sems["act"], acts_l[j])
                        for hf in range(2):
                            t.matmul(p_cA[0:65, 512 * hf:512 * (hf + 1)],
                                     vrot[:, 129 * j:129 * j + 65],
                                     Pm[j % 2][:, 512 * hf:512 * (hf + 1)],
                                     start=(j == 0), stop=False)
                            mm = t.matmul(p_cB[0:65, 512 * hf:512 * (hf + 1)],
                                          vrot[:, 129 * j + 64:129 * j + 129],
                                          Pm[j % 2][:, 512 * hf:512 * (hf + 1)],
                                          start=(j == 0), stop=False)
                        ctx.tick(mm, "pe")

                for kt in range(64):
                    @block.tensor
                    def _(t, e=e, kt=kt, qg=qg):
                        if kt == 0:
                            ctx.wait(t, "dve", "ld", "act", "sd")
                        if kt >= 2:
                            t.wait_ge(ctx.sems["act"], acts_l[kt - 2])
                        t.matmul(p_s[kt % 2][:, 0:512], kmix[:, 128 * kt:128 * (kt + 1)],
                                 qt[e][:, qgs:qgs + 512], start=True, stop=True)
                        ctx.tick(t.matmul(p_s[kt % 2][:, 512:1024], kmix[:, 128 * kt:128 * (kt + 1)],
                                          qt[e][:, qgs + 512:qgs + 1024], start=True, stop=True), "pe")
                        pes_l.append(ctx.cnt["pe"])

                    if kt >= 1:
                        _exp_ctx(kt - 1)
                _exp_ctx(63)

                for bi in range(10):
                    qtls = list(range(max(0, bi - 2), min(8, bi + 1)))

                    @block.vector
                    def _(v, e=e, bi=bi, qtls=qtls):
                        ctx.wait(v, "pe")
                        if bi == 0:
                            ctx.wait(v, "ld", "sd")
                        for ii, qtl in enumerate(qtls):
                            off = 128 * (bi - qtl)
                            mm = v.tensor_copy(bnd32[:, 128 * ii:128 * (ii + 1)],
                                               band[:, 384 * qtl + off:384 * qtl + off + 128])
                        ctx.tick(mm, "dve")

                    @block.tensor
                    def _(t, e=e, bi=bi, qg=qg, qtls=qtls):
                        ctx.wait(t, "act", "dve")
                        if bi == 0:
                            ctx.wait(t, "ld", "sd")
                        t.matmul(p_s[0][:, 0:512], kband[:, 128 * bi:128 * (bi + 1)],
                                 qt[e][:, qgs:qgs + 512], start=True, stop=True)
                        last = t.matmul(p_s[0][:, 512:1024], kband[:, 128 * bi:128 * (bi + 1)],
                                        qt[e][:, qgs + 512:qgs + 1024], start=True, stop=True)
                        for ii, qtl in enumerate(qtls):
                            last = t.matmul(p_s[1][:, 128 * qtl:128 * (qtl + 1)],
                                            bnd32[:, 128 * ii:128 * (ii + 1)],
                                            identp[:, :], is_transpose=True, start=True, stop=True)
                        if bi <= 6:
                            lo = 128 * (bi + 1)
                            while lo < 1024:
                                hi = min(1024, lo + (512 - lo % 512))
                                last = t.matmul(p_s[1][0:128, lo:hi], onesrow[0:1, 0:128],
                                                qrel0[0:1, lo:hi], start=True, stop=True)
                                lo = hi
                        if bi >= 3:
                            lo = 0
                            c1 = 128 * (bi - 2)
                            while lo < c1:
                                hi = min(c1, lo + 512)
                                last = t.matmul(p_s[1][0:128, lo:hi], onesrow[0:1, 0:128],
                                                qrel128[0:1, lo:hi], start=True, stop=True)
                                lo = hi
                        ctx.tick(last, "pe")

                    @block.vector
                    def _(v, e=e, bi=bi):
                        ctx.wait(v, "pe")
                        v.tensor_copy(krf[:, :], p_s[1][:, 0:512])
                        v.tensor_copy(tmpA[:, :], p_s[1][:, 512:1024])
                        v.drain()
                        v.tensor_tensor(fwork[:, 0:512], p_s[0][:, 0:512], krf[:, :], ALU.add)
                        ctx.tick(v.tensor_tensor(fwork[:, 512:1024], p_s[0][:, 512:1024],
                                                 tmpA[:, :], ALU.add), "dve")

                    @block.scalar
                    def _(sc, e=e, bi=bi):
                        ctx.wait(sc, "dve", "pe")
                        sc.activation(Pm[0][:, :], fwork[:, :], AF.Exp, bias=negce[:, e:e + 1])
                        ctx.tick(sc.activation(Pm[1][:, :], p_s[0][:, :], AF.Exp,
                                               bias=negce[:, e:e + 1]), "act")

                    @block.vector
                    def _(v, e=e, bi=bi):
                        ctx.wait(v, "act")
                        ctx.tick(v.tensor_tensor(Pm[0][:, :], Pm[0][:, :], Pm[1][:, :], ALU.subtract), "dve")

                    @block.tensor
                    def _(t, e=e, bi=bi):
                        ctx.wait(t, "dve")
                        for hf in range(2):
                            t.matmul(p_cA[0:65, 512 * hf:512 * (hf + 1)],
                                     vband[:, 129 * bi:129 * bi + 65],
                                     Pm[0][:, 512 * hf:512 * (hf + 1)],
                                     start=False, stop=(bi == 9))
                            mm = t.matmul(p_cB[0:65, 512 * hf:512 * (hf + 1)],
                                          vband[:, 129 * bi + 64:129 * bi + 129],
                                          Pm[0][:, 512 * hf:512 * (hf + 1)],
                                          start=False, stop=(bi == 9))
                        ctx.tick(mm, "pe")

                @block.vector
                def _(v, e=e, qg=qg):
                    ctx.wait(v, "pe", "ld")
                    v.tensor_copy(cAe[:, :], p_cA[0:65, :])
                    v.tensor_copy(cBe[:, :], p_cB[0:65, :])
                    v.drain()
                    ctx.tick(v.reciprocal(rrow[:, :], cBe[0:1, 0:1024]), "dve")

                @block.sync
                def _(sync, e=e, qg=qg):
                    ctx.wait(sync, "dve")
                    ctx.dma(sync, bass.AP(xdown_dram, 0, [[2048, 64], [1, 1024]]), cAe[0:64, :])
                    ctx.dma(sync, bass.AP(xdown_dram, 64 * 2048, [[2048, 64], [1, 1024]]), cBe[1:65, :])
                    ctx.dma(sync, grow[:, :], bass.AP(gmoa_dram, 2048 * e + qgs, [[1, 1], [1, 1024]]))

                @block.vector
                def _(v, e=e):
                    ctx.wait(v, "ld")
                    ctx.tick(v.tensor_tensor(rrow[:, :], rrow[:, :], grow[:, :], ALU.mult), "dve")

                @block.sync
                def _(sync, e=e):
                    ctx.wait(sync, "dve")
                    ctx.dma(sync, bass.AP(sm6, 2048, [[1, 1], [1, 1024]]), rrow[:, :])
                    sync.wait_ge(ctx.sems["ld"], ctx.cnt["ld"])
                    ctx.dma(sync, rngbc[:, :], bass.AP(sm6, 2048, [[0, 128], [1, 1024]]))
                    ctx.dma(sync, fwork[:, :], bass.AP(xdown_dram, 0, [[2048, 128], [1, 1024]]))

                @block.vector
                def _(v, e=e):
                    ctx.wait(v, "ld")
                    ctx.tick(v.tensor_tensor(gc[e][:, :], fwork[:, :], rngbc[:, :], ALU.mult), "dve")

            for dt in range(NT):
                @block.gpsimd
                def _(g, dt=dt, qg=qg):
                    ctx.wait(g, "pe", "dve")
                    ctx.dma(g, owt[:, :].rearrange("p (a w) -> p a w", a=8, w=128),
                            bass.AP(E["ow"], 128 * dt, [[1024, 128], [128 * 1024, 8], [1, 128]]),
                            sem="gd")

                @block.tensor
                def _(t, dt=dt, qg=qg):
                    ctx.wait(t, "gd", "dve")
                    if dt == 0:
                        ctx.wait(t, "act")
                    last = None
                    for hf in range(2):
                        for e in range(NT):
                            t.matmul(p_s[0][:, 512 * hf:512 * (hf + 1)],
                                     owt[:, 128 * e:128 * (e + 1)],
                                     gc[e][:, 512 * hf:512 * (hf + 1)],
                                     start=(e == 0), stop=False)
                        last = t.matmul(p_s[0][:, 512 * hf:512 * (hf + 1)],
                                        obT[0:8, 128 * dt:128 * (dt + 1)],
                                        gT[0:8, qgs + 512 * hf:qgs + 512 * (hf + 1)],
                                        start=False, stop=True)
                    ctx.tick(last, "pe")

                @block.sync
                def _(sync, dt=dt, qg=qg):
                    ctx.wait(sync, "pe", "dve")
                    ctx.dma(sync, krf[:, :], bass.AP(xnown_dram, 2048 * 128 * dt + qgs, [[2048, 128], [1, 512]]))
                    ctx.dma(sync, tmpA[:, :], bass.AP(xnown_dram, 2048 * 128 * dt + qgs + 512,
                                                      [[2048, 128], [1, 512]]))

                @block.vector
                def _(v, dt=dt, qg=qg):
                    ctx.wait(v, "ld", "pe")
                    v.tensor_tensor(big[dt][:, qgs:qgs + 512], p_s[0][:, 0:512], krf[:, :], ALU.add)
                    ctx.tick(v.tensor_tensor(big[dt][:, qgs + 512:qgs + 1024], p_s[0][:, 512:1024],
                                             tmpA[:, :], ALU.add), "dve")

        if DEBUG:
            @block.sync
            def _(sync):
                ctx.wait(sync, "dve", "pe", "act")
                for i in range(NT):
                    ctx.dma(sync, dbg["x2"][128 * i:128 * (i + 1), :], big[i][:, 0:2048])
                ctx.dma(sync, dbg["qt"], qt[0][:, :])
                ctx.dma(sync, dbg["kmix"], kmix[:, 2048:3072])
                ctx.dma(sync, dbg["negce"], negce[:, :])
                ctx.dma(sync, dbg["gctx"], gc[0][:, :])


def _moe_final(nc, ctx, block, E, big, big_all, cfg_sb, ones_col):
    rows_dram, sm6 = E["rows_dram"], E["sm6_dram"]
    fg_dram, sblk_dram, xown_dram = E["fg_dram"], E["sblk_dram"], E["xown_dram"]
    dbg = E["dbg"]
    with ExitStack() as s8:
        sb8 = lambda n, s, d=F32: s8.enter_context(nc.sbuf_tensor(n, s, d))
        xn3 = [sb8(f"xn3_{i}", [128, CW], BF16) for i in range(NT)]
        w1s = [sb8(f"w1s{i}", [128, 4096], BF16) for i in range(2)]
        w2s = [sb8(f"w2s{i}", [128, 4096], BF16) for i in range(2)]
        fgT = sb8("fgT", [16, CW], BF16)
        fgr = sb8("fgr", [16, CW])
        fgwb = sb8("fgwb", [128, 128], BF16)
        fb2b = sb8("fb2b", [16, 1024], BF16)
        fcols = sb8("fcols", [128, 16])
        fb1c = sb8("fb1c", [128, 64])
        hgb = sb8("hgb", [128, 4 * 512], BF16)
        gbc = sb8("gbc", [128, 512])
        mu_bc = sb8("mu_bc", [128, 512])
        r3_bc = sb8("r3_bc", [128, 512])
        rowm = sb8("rowm", [1, CW])
        rowv = sb8("rowv", [1, CW])
        rowr = sb8("rowr", [1, CW])
        rowt = sb8("rowt", [1, CW])
        sq8 = sb8("sq8", [128, 512])
        hq = sb8("hq", [128, 512])
        eqs8 = sb8("eqs8", [128, 512])
        dcol8 = sb8("dcol8", [128, 80])
        outs = sb8("outs", [128, 512])
        sfin_t = sb8("sfin_t", [128, 512])
        xow_t = sb8("xow_t", [128, 512])

        p_r1 = s8.enter_context(nc.psum_tensor("p_r1", [1, 512], F32))
        p_r2 = s8.enter_context(nc.psum_tensor("p_r2", [1, 512], F32))
        p_g8 = s8.enter_context(nc.psum_tensor("p_g8", [128, 128], F32))
        p_h8 = [s8.enter_context(nc.psum_tensor(f"p_h8{i}", [128, 512], F32)) for i in range(2)]
        p_o8 = [s8.enter_context(nc.psum_tensor(f"p_o8{i}", [128, 512], F32)) for i in range(2)]

        for cc in range(4):
            for i in range(NT):
                @block.scalar
                def _(sc, i=i, cc=cc):
                    ctx.wait(sc, "pe", "dve")
                    ctx.tick(sc.activation(sq8[:, :], big[i][:, 512 * cc:512 * (cc + 1)], AF.Square), "act")

                @block.tensor
                def _(t, i=i, cc=cc):
                    ctx.wait(t, "act", "dve")
                    t.matmul(p_r1[:, :], ones_col[:, 0:1], big[i][:, 512 * cc:512 * (cc + 1)],
                             start=(i == 0), stop=(i == NT - 1))
                    ctx.tick(t.matmul(p_r2[:, :], ones_col[:, 0:1], sq8[:, :],
                                      start=(i == 0), stop=(i == NT - 1)), "pe")

            @block.vector
            def _(v, cc=cc):
                ctx.wait(v, "pe")
                sl = slice(512 * cc, 512 * (cc + 1))
                v.tensor_scalar(rowm[0:1, sl], p_r1[:, :], 1.0 / D, None, ALU.mult)
                v.drain()
                v.tensor_tensor(rowt[0:1, 0:512], rowm[0:1, sl], rowm[0:1, sl], ALU.mult)
                v.drain()
                v.tensor_scalar(rowv[0:1, sl], p_r2[:, :], 1.0 / D, EPS_LN, ALU.mult, ALU.add)
                v.drain()
                ctx.tick(v.tensor_tensor(rowv[0:1, sl], rowv[0:1, sl], rowt[0:1, 0:512],
                                         ALU.subtract), "dve")

            @block.scalar
            def _(sc, cc=cc):
                ctx.wait(sc, "dve")
                sl = slice(512 * cc, 512 * (cc + 1))
                ctx.tick(sc.activation(rowt[0:1, sl], rowv[0:1, sl], AF.Sqrt), "act")

            @block.vector
            def _(v, cc=cc):
                ctx.wait(v, "act")
                sl = slice(512 * cc, 512 * (cc + 1))
                v.reciprocal(rowr[0:1, sl], rowt[0:1, sl])
                v.drain()
                ctx.tick(_newton_rsqrt(v, rowr[0:1, sl], rowv[0:1, sl], rowt[0:1, sl]), "dve")

        @block.sync
        def _(sync):
            ctx.wait(sync, "dve")
            ctx.dma(sync, bass.AP(rows_dram, 4 * S, [[S, 1], [1, 2048]]), rowm[0:1, :])
            ctx.dma(sync, bass.AP(rows_dram, 4 * S + 2048, [[S, 1], [1, 2048]]), rowr[0:1, :])

        @block.scalar
        def _(sc):
            ctx.wait(sc, "ld", "dve", "act", "pe", "gd")
            rA = sc.alloc_register("xoA")
            rB = sc.alloc_register("xoB")
            sc.reg_load(rA, cfg_sb[0:1, 0:1])
            sc.reg_load(rB, cfg_sb[0:1, 1:2])
            for k, rr in ((0, rA), (1024, rB)):
                ctx.dma(sc, bass.AP(xown_dram, k, [[2048, 1024], [1, 1024]]),
                        bass.AP(E["x_ch"], rr, [[S, 1024], [1, 1024]]), sem="sd")
            sc.wait_ge(ctx.sems["sd"], ctx.cnt["sd"])

        @block.sync
        def _(sync):
            ctx.wait(sync, "dve", "act")
            for e16 in range(16):
                for ft4 in range(4):
                    ctx.dma(sync, fb1c[:, 4 * e16 + ft4:4 * e16 + ft4 + 1],
                            bass.AP(E["fb1"], 512 * e16 + 128 * ft4, [[1, 128], [1, 1]]))
            for i in range(NT):
                ctx.dma(sync, fcols[:, i:i + 1], bass.AP(E["fln2w"], 128 * i, [[1, 128], [1, 1]]))
                ctx.dma(sync, fcols[:, 8 + i:9 + i], bass.AP(E["fln2b"], 128 * i, [[1, 128], [1, 1]]))

        @block.gpsimd
        def _(g):
            ctx.wait(g, "dve", "act")
            ctx.dma(g, fgwb[:, :].rearrange("p (a w) -> p a w", a=8, w=16),
                    bass.AP(E["fgw"], 0, [[16, 128], [2048, 8], [1, 16]]), sem="gd")
            ctx.dma(g, fb2b[:, :], E["fb2"].ap(), sem="gd")

        for cc in range(4):
            @block.sync
            def _(sync, cc=cc):
                ctx.wait(sync, "dve")
                sync.wait_ge(ctx.sems["ld"], ctx.cnt["ld"])
                ctx.dma(sync, mu_bc[:, :], bass.AP(rows_dram, 4 * S + 512 * cc, [[0, 128], [1, 512]]))
                ctx.dma(sync, r3_bc[:, :], bass.AP(rows_dram, 4 * S + 2048 + 512 * cc, [[0, 128], [1, 512]]))

            for i in range(NT):
                @block.vector
                def _(v, i=i, cc=cc):
                    ctx.wait(v, "ld")
                    v.tensor_tensor(sq8[:, :], big[i][:, 512 * cc:512 * (cc + 1)], mu_bc[:, :], ALU.subtract)
                    v.drain()
                    v.tensor_tensor(sq8[:, :], sq8[:, :], r3_bc[:, :], ALU.mult)
                    v.drain()
                    ctx.tick(v.tensor_scalar(xn3[i][:, 512 * cc:512 * (cc + 1)], sq8[:, :],
                                             fcols[:, i:i + 1], fcols[:, 8 + i:9 + i],
                                             ALU.mult, ALU.add), "dve")

        for tt in range(16):
            @block.tensor
            def _(t, tt=tt):
                ctx.wait(t, "dve", "ld", "gd")
                for i in range(NT):
                    mm = t.matmul(p_g8[:, 0:16], xn3[i][:, 128 * tt:128 * (tt + 1)],
                                  fgwb[:, 16 * i:16 * (i + 1)], start=(i == 0), stop=(i == NT - 1))
                ctx.tick(mm, "pe")

            @block.vector
            def _(v, tt=tt):
                ctx.wait(v, "pe")
                m1, m2, msk = dcol8[:, 48:49], dcol8[:, 49:50], dcol8[:, 64:80]
                v.reduce_max(m1, p_g8[:, 0:16], AX.X)
                v.drain()
                v.tensor_scalar(eqs8[:, 32 * tt:32 * tt + 16], p_g8[:, 0:16], m1, None, ALU.is_equal)
                v.drain()
                v.scalar_tensor_tensor(msk, eqs8[:, 32 * tt:32 * tt + 16], -1e30, p_g8[:, 0:16],
                                       ALU.mult, ALU.add)
                v.drain()
                v.reduce_max(m2, msk, AX.X)
                v.drain()
                v.tensor_scalar(eqs8[:, 32 * tt + 16:32 * tt + 32], p_g8[:, 0:16], m2, None, ALU.is_equal)
                ctx.tick(v.tensor_tensor(dcol8[:, tt:tt + 1], m2, m1, ALU.subtract), "dve")

        @block.scalar
        def _(sc):
            ctx.wait(sc, "dve")
            ctx.tick(sc.activation(dcol8[:, 16:32], dcol8[:, 0:16], AF.Exp), "act")

        @block.vector
        def _(v):
            ctx.wait(v, "act")
            v.tensor_scalar(dcol8[:, 32:48], dcol8[:, 16:32], 1.0, None, ALU.add)
            v.drain()
            v.reciprocal(dcol8[:, 32:48], dcol8[:, 32:48])
            v.drain()
            v.tensor_tensor(dcol8[:, 16:32], dcol8[:, 16:32], dcol8[:, 32:48], ALU.mult)
            v.drain()
            for tt in range(16):
                v.tensor_scalar(eqs8[:, 32 * tt:32 * tt + 16], eqs8[:, 32 * tt:32 * tt + 16],
                                dcol8[:, 32 + tt:33 + tt], None, ALU.mult)
                v.drain()
                mm = v.scalar_tensor_tensor(eqs8[:, 32 * tt:32 * tt + 16],
                                            eqs8[:, 32 * tt + 16:32 * tt + 32],
                                            dcol8[:, 16 + tt:17 + tt],
                                            eqs8[:, 32 * tt:32 * tt + 16], ALU.mult, ALU.add)
                v.drain()
            ctx.tick(mm, "dve")

        for tt in range(16):
            @block.tensor
            def _(t, tt=tt):
                ctx.wait(t, "dve")
                ctx.tick(t.matmul(p_g8[0:16, 0:128], eqs8[:, 32 * tt:32 * tt + 16],
                                  E["identp_sb"][:, :], is_transpose=True, start=True, stop=True), "pe")

            @block.vector
            def _(v, tt=tt):
                ctx.wait(v, "pe")
                v.tensor_copy(fgT[0:16, 128 * tt:128 * (tt + 1)], p_g8[0:16, 0:128])
                ctx.tick(v.tensor_copy(fgr[0:16, 128 * tt:128 * (tt + 1)], p_g8[0:16, 0:128]), "dve")

        @block.sync
        def _(sync):
            ctx.wait(sync, "dve")
            ctx.dma(sync, fg_dram.ap(), fgr[:, :])
            sync.wait_ge(ctx.sems["ld"], ctx.cnt["ld"])

        gds, pes8 = {}, {}
        h_pe, h_act, h_tt = [], [], []
        o_pe, o_add = [], []
        gbc_ld = {}
        for e in range(16):
            @block.gpsimd
            def _(g, e=e):
                if e >= 2:
                    g.wait_ge(ctx.sems["pe"], pes8[e - 2])
                else:
                    ctx.wait(g, "pe")
                for ft in range(4):
                    ctx.dma(g, w1s[e % 2][:, 1024 * ft:1024 * (ft + 1)].rearrange(
                        "p (a w) -> p a w", a=8, w=128),
                        bass.AP(E["fw1"], e * 524288 + 128 * ft, [[512, 128], [65536, 8], [1, 128]]),
                        sem="gd")
                    ctx.dma(g, w2s[e % 2][:, 1024 * ft:1024 * (ft + 1)].rearrange(
                        "p (a w) -> p a w", a=8, w=128),
                        bass.AP(E["fw2"], e * 524288 + 131072 * ft, [[1024, 128], [128, 8], [1, 128]]),
                        sem="gd")
                gds[e] = ctx.cnt["gd"]

            for cc in range(4):
                @block.sync
                def _(sync, e=e, cc=cc):
                    ctx.wait(sync, "dve")
                    ctx.dma(sync, gbc[:, :], bass.AP(fg_dram, 2048 * e + 512 * cc, [[0, 128], [1, 512]]))
                    gbc_ld[(e, cc)] = ctx.cnt["ld"]

                for ft in range(4):
                    idx = len(h_pe)

                    @block.tensor
                    def _(t, e=e, cc=cc, ft=ft, idx=idx):
                        t.wait_ge(ctx.sems["gd"], gds[e])
                        if idx == 0:
                            ctx.wait(t, "dve", "ld", "act")
                        if idx >= 2:
                            t.wait_ge(ctx.sems["act"], h_act[idx - 2])
                        for i in range(NT):
                            mm = t.matmul(p_h8[ft % 2][:, :],
                                          w1s[e % 2][:, 1024 * ft + 128 * i:1024 * ft + 128 * (i + 1)],
                                          xn3[i][:, 512 * cc:512 * (cc + 1)],
                                          start=(i == 0), stop=(i == NT - 1))
                        ctx.tick(mm, "pe")
                        h_pe.append(ctx.cnt["pe"])

                    @block.scalar
                    def _(sc, e=e, cc=cc, ft=ft, idx=idx):
                        sc.wait_ge(ctx.sems["pe"], h_pe[idx])
                        if idx >= 1:
                            sc.wait_ge(ctx.sems["dve"], h_tt[idx - 1])
                        ctx.tick(sc.activation(hq[:, :], p_h8[ft % 2][:, :], AF.Relu,
                                               bias=fb1c[:, 4 * e + ft:4 * e + ft + 1]), "act")
                        h_act.append(ctx.cnt["act"])

                    @block.vector
                    def _(v, e=e, cc=cc, ft=ft, idx=idx):
                        v.wait_ge(ctx.sems["act"], h_act[idx])
                        v.wait_ge(ctx.sems["ld"], gbc_ld[(e, cc)])
                        ctx.tick(v.tensor_tensor(hgb[:, 512 * ft:512 * (ft + 1)], hq[:, :], gbc[:, :],
                                                 ALU.mult), "dve")
                        h_tt.append(ctx.cnt["dve"])

                for dt in range(NT):
                    odx = len(o_pe)

                    @block.tensor
                    def _(t, e=e, cc=cc, dt=dt, odx=odx):
                        t.wait_ge(ctx.sems["dve"], h_tt[-1])
                        if odx >= 2:
                            t.wait_ge(ctx.sems["dve"], o_add[odx - 2])
                        for ft in range(4):
                            mm = t.matmul(p_o8[dt % 2][:, :],
                                          w2s[e % 2][:, 1024 * ft + 128 * dt:1024 * ft + 128 * (dt + 1)],
                                          hgb[:, 512 * ft:512 * (ft + 1)], start=(ft == 0), stop=(ft == 3))
                        ctx.tick(mm, "pe")
                        o_pe.append(ctx.cnt["pe"])

                    @block.vector
                    def _(v, e=e, cc=cc, dt=dt, odx=odx):
                        v.wait_ge(ctx.sems["pe"], o_pe[odx])
                        ctx.tick(v.tensor_tensor(big[dt][:, 512 * cc:512 * (cc + 1)],
                                                 big[dt][:, 512 * cc:512 * (cc + 1)],
                                                 p_o8[dt % 2][:, :], ALU.add), "dve")
                        o_add.append(ctx.cnt["dve"])
            pes8[e] = ctx.cnt["pe"]

        for cc in range(4):
            for dt in range(NT):
                @block.tensor
                def _(t, cc=cc, dt=dt):
                    ctx.wait(t, "dve", "gd")
                    ctx.tick(t.matmul(p_o8[dt % 2][:, :], fb2b[0:16, 128 * dt:128 * (dt + 1)],
                                      fgT[0:16, 512 * cc:512 * (cc + 1)], start=True, stop=True), "pe")

                @block.vector
                def _(v, cc=cc, dt=dt):
                    ctx.wait(v, "pe")
                    ctx.tick(v.tensor_tensor(big[dt][:, 512 * cc:512 * (cc + 1)],
                                             big[dt][:, 512 * cc:512 * (cc + 1)],
                                             p_o8[dt % 2][:, :], ALU.add), "dve")

        for dt in range(NT):
            for cc in range(4):
                @block.scalar
                def _(sc, dt=dt, cc=cc):
                    ctx.wait(sc, "dve")
                    ctx.tick(sc.activation(sq8[:, :], big[dt][:, 512 * cc:512 * (cc + 1)],
                                           AF.Sigmoid), "act")

                @block.sync
                def _(sync, dt=dt, cc=cc):
                    ctx.wait(sync, "dve", "sd")
                    ctx.dma(sync, sfin_t[:, :], bass.AP(sblk_dram, 128 * 2048 * dt + 512 * cc,
                                                        [[2048, 128], [1, 512]]))
                    ctx.dma(sync, xow_t[:, :], bass.AP(xown_dram, 128 * 2048 * dt + 512 * cc,
                                                       [[2048, 128], [1, 512]]))

                @block.vector
                def _(v, dt=dt, cc=cc):
                    ctx.wait(v, "act", "ld", "st")
                    v.tensor_tensor(outs[:, :], big[dt][:, 512 * cc:512 * (cc + 1)], sq8[:, :], ALU.mult)
                    v.drain()
                    v.tensor_tensor(outs[:, :], outs[:, :], sfin_t[:, :], ALU.mult)
                    v.drain()
                    ctx.tick(v.tensor_tensor(outs[:, :], outs[:, :], xow_t[:, :], ALU.add), "dve")

                @block.sync
                def _(sync, dt=dt, cc=cc):
                    ctx.wait(sync, "dve")
                    ctx.dma(sync, E["out_ch"][128 * dt:128 * (dt + 1), 512 * cc:512 * (cc + 1)],
                            outs[:, :], sem="st")

        if DEBUG:
            @block.sync
            def _(sync):
                ctx.wait(sync, "dve", "act", "pe")
                ctx.dma(sync, dbg["xn3"], xn3[0][:, :])
                ctx.dma(sync, dbg["fgT"], fgT[:, :])


def kernel(**inp):
    xs = np.ascontiguousarray(np.asarray(inp["L0_attn_x"], np.float32).transpose(0, 2, 1))
    kk = np.arange(128).reshape(128, 1)
    cc = np.arange(2048).reshape(1, 2048)
    maskm_np = (cc >= kk + 1024).astype(ml_dtypes.bfloat16)
    relT_np = np.ascontiguousarray(np.asarray(inp["L0_ffn_rel_table"], np.float32).T)
    ts = np.arange(S)
    nc = build_program()
    in_maps = []
    for c in range(8):
        b, j = c // 4, c % 4
        q0A, q0B = 1024 * j, 1024 * (7 - j)
        cfg = np.zeros((1, 64), np.int32)
        cfg[0, 0] = q0A
        cfg[0, 1] = q0B
        blocks = ([(0, 1024 * r) for r in range(j)] +
                  [(1, 1024 * r) for r in range(7 - j)] +
                  [(0, 1024 * j), (1, 1024 * (7 - j))])
        pmsk = np.zeros((2, 9), np.float32)
        for pos, (st, k0) in enumerate(blocks):
            cfg[0, 8 + pos] = k0
            cfg[0, 17 + pos] = st * 1024
            cfg[0, 26 + pos] = (k0 // 128) * 128 * 1280
            pmsk[st, pos] = 1.0
        pmsk = pmsk.reshape(-1)
        cfg[0, 35] = (q0A // 128 + 9) * (128 * 144)
        cfg[0, 36] = (q0B // 128 + 9) * (128 * 144)
        cfg[0, 37] = 0 * 128 * (2560 + S) + q0A + 1152
        cfg[0, 38] = 1 * 128 * (2560 + S) + q0B + 1152
        indl_np = np.stack([(ts <= q0A - 129), (ts <= q0B - 129)]).astype(ml_dtypes.bfloat16)
        indh_np = np.stack([(ts >= q0A + 1152), (ts >= q0B + 1152)]).astype(ml_dtypes.bfloat16)
        in_maps.append(dict(
            x_ch=xs[b],
            dd=np.asarray(inp["L0_attn_delta_decay"], np.float32).reshape(1, D),
            cfg=cfg,
            phi=np.asarray(inp["L0_attn_phi_res"], np.float32),
            gwT=np.ascontiguousarray(np.asarray(inp["L0_attn_gate_w"], np.float32).T),
            gb=np.asarray(inp["L0_attn_gate_b"], np.float32).reshape(1, 24),
            ident=np.eye(128, dtype=np.float32),
            wqkvT=np.ascontiguousarray(np.asarray(inp["L0_attn_in_proj_w"], np.float32).T),
            bqkv=np.asarray(inp["L0_attn_in_proj_b"], np.float32).reshape(1, 3072),
            ln1w=np.asarray(inp["L0_attn_ln1_w"], np.float32).reshape(1, D),
            maskm=maskm_np,
            posmask=np.broadcast_to(pmsk.reshape(1, 18), (128, 18)).copy(),
            outwT=np.ascontiguousarray(np.asarray(inp["L0_attn_out_w"], np.float32).T),
            outb=np.asarray(inp["L0_attn_out_b"], np.float32).reshape(1, D),
            w1T=np.ascontiguousarray(np.asarray(inp["L0_attn_w1"], np.float32).T),
            b1=np.asarray(inp["L0_attn_b1"], np.float32).reshape(1, 4096),
            w2T=np.ascontiguousarray(np.asarray(inp["L0_attn_w2"], np.float32).T),
            b2=np.asarray(inp["L0_attn_b2"], np.float32).reshape(1, D),
            ln2w=np.asarray(inp["L0_attn_ln2_w"], np.float32).reshape(1, D),
            fln1w=np.asarray(inp["L0_ffn_ln1_w"], np.float32).reshape(1, D),
            fln1b=np.asarray(inp["L0_ffn_ln1_b"], np.float32).reshape(1, D),
            fln2w=np.asarray(inp["L0_ffn_ln2_w"], np.float32).reshape(1, D),
            fln2b=np.asarray(inp["L0_ffn_ln2_b"], np.float32).reshape(1, D),
            agw=np.asarray(inp["L0_ffn_attn_gate_w"], np.float32),
            qw=np.asarray(inp["L0_ffn_q_w"], np.float32),
            qb=np.asarray(inp["L0_ffn_q_b"], np.float32),
            kw=np.asarray(inp["L0_ffn_k_w"], np.float32),
            kb=np.asarray(inp["L0_ffn_k_b"], np.float32).reshape(1, 128),
            vw=np.asarray(inp["L0_ffn_v_w"], np.float32),
            vb=np.asarray(inp["L0_ffn_v_b"], np.float32).reshape(1, 128),
            ow=np.asarray(inp["L0_ffn_o_w"], np.float32),
            ob=np.asarray(inp["L0_ffn_o_b"], np.float32),
            relT=relT_np,
            indl=indl_np,
            indh=indh_np,
            fgw=np.asarray(inp["L0_ffn_ff_gate_w"], np.float32),
            fw1=np.asarray(inp["L0_ffn_ff_w1"], np.float32),
            fb1=np.asarray(inp["L0_ffn_ff_b1"], np.float32),
            fw2=np.asarray(inp["L0_ffn_ff_w2"], np.float32),
            fb2=np.asarray(inp["L0_ffn_ff_b2"], np.float32),
        ))
    res = run_bass_kernel_spmd(nc, in_maps, core_ids=list(range(8)))
    out = np.zeros((B, S, D), np.float32)
    for c in range(8):
        b, j = c // 4, c % 4
        oc = res.results[c]["out_ch"]
        out[b, 1024 * j:1024 * (j + 1)] = oc[:, 0:1024].T
        out[b, 1024 * (7 - j):1024 * (8 - j)] = oc[:, 1024:2048].T
    kernel._res = res
    return out


# version 09 (final)
# speedup vs baseline: 1.1880x; 1.0191x over previous
import numpy as np
import ml_dtypes
import concourse.bass as bass
import concourse.mybir as mybir
from concourse.bass_utils import run_bass_kernel_spmd
from contextlib import ExitStack

F32 = mybir.dt.float32
BF16 = mybir.dt.bfloat16
I32 = mybir.dt.int32
AF = mybir.ActivationFunctionType
ALU = mybir.AluOpType
AX = mybir.AxisListType
bfloat16 = ml_dtypes.bfloat16

B, S, D = 2, 8192, 1024
NCH, CW, NT = 4, 2048, 8
NH = 16
EPS_LN, EPS_RMS = 1e-5, 1e-6
DEBUG = False
STAGE = 6


class Ctx:
    def __init__(self, nc, stack):
        self.nc, self.stack = nc, stack
        self.sems, self.cnt = {}, {}

    def sem(self, name):
        h = self.stack.enter_context(self.nc.semaphore(name))
        self.sems[name] = h
        self.cnt[name] = 0
        return h

    def inc(self, name, by=1):
        self.cnt[name] += by
        return self.sems[name]

    def dma(self, eng, out, in_, sem="ld", **kw):
        i = eng.dma_start(out=out, in_=in_, **kw)
        i.then_inc(self.inc(sem, 16), 16)
        return i

    def wait(self, eng, *names):
        for name in names:
            if self.cnt[name]:
                eng.wait_ge(self.sems[name], self.cnt[name])

    def tick(self, inst, name):
        inst.then_inc(self.inc(name, 1), 1)
        return inst


def _newton_rsqrt(v, r_ap, var_ap, tmp_ap):
    v.tensor_tensor(tmp_ap, r_ap, r_ap, ALU.mult)
    v.tensor_tensor(tmp_ap, tmp_ap, var_ap, ALU.mult)
    v.tensor_scalar(tmp_ap, tmp_ap, -0.5, 1.5, ALU.mult, ALU.add)
    return v.tensor_tensor(r_ap, r_ap, tmp_ap, ALU.mult)


def build_program():
    nc = bass.Bass("TRN2", target_bir_lowering=False, debug=False)
    E = {}

    def din(name, shape, dt=F32):
        E[name] = nc.dram_tensor(name, shape, dt, kind="ExternalInput")
        return E[name]

    din("x_ch", [D, S])
    din("dd", [1, D])
    din("cfg", [1, 64], I32)
    din("phi", [8, 8])
    din("gwT", [D, 24])
    din("gb", [1, 24])
    din("ident", [128, 128])
    din("wqkvT", [D, 3072])
    din("bqkv", [1, 3072])
    din("ln1w", [1, D])
    din("maskm", [128, 2048], BF16)
    din("posmask", [128, 18])
    din("outwT", [D, D])
    din("outb", [1, D])
    din("w1T", [D, 4096])
    din("b1", [1, 4096])
    din("w2T", [4096, D])
    din("b2", [1, D])
    din("ln2w", [1, D])
    din("fln1w", [1, D])
    din("fln1b", [1, D])
    din("fln2w", [1, D])
    din("fln2b", [1, D])
    din("agw", [D, 8])
    din("qw", [8, D, 128])
    din("qb", [8, 128])
    din("kw", [D, 128])
    din("kb", [1, 128])
    din("vw", [D, 128])
    din("vb", [1, 128])
    din("ow", [8, 128, D])
    din("ob", [8, D])
    din("relT", [128, 129])
    din("indl", [2, S], BF16)
    din("indh", [2, S], BF16)
    din("fgw", [D, 16])
    din("fw1", [16, D, 512])
    din("fb1", [16, 512])
    din("fw2", [16, 512, D])
    din("fb2", [16, D])

    out_ch = nc.dram_tensor("out_ch", [D, 2048], F32, kind="ExternalOutput").ap()
    dbg = {}
    if DEBUG:
        for nm, shape, dt in [
            ("xd_own", [D, 2048], F32), ("rows", [6, S], F32), ("xstate", [128, 8], F32),
            ("sblk", [D, 2048], F32), ("kv", [1024, 512], BF16), ("q", [1024, 512], BF16),
            ("small", [16, 64], F32), ("vtok", [128, 1280], BF16),
            ("ctxn", [128, 2048], BF16), ("sums", [16, 2048], F32),
            ("acc", [4 * 65, 1024], F32), ("pstage", [128, 3072], BF16),
            ("x2", [D, 2048], F32), ("gT", [8, 2048], BF16), ("qt", [128, 2048], BF16),
            ("kmix", [128, 1024], BF16), ("negce", [128, 8], F32), ("gctx", [128, 1024], BF16),
            ("xn3", [128, 2048], BF16), ("fgT", [16, 2048], BF16),
            ("acc2", [65, 1024], F32), ("Pa", [128, 1024], BF16),
            ("ab0", [65, 1024], F32), ("ab1", [65, 1024], F32),
        ]:
            dbg[nm] = nc.dram_tensor("dbg_" + nm, shape, dt, kind="ExternalOutput").ap()

    sp_xn = nc.dram_tensor("sp_xn", [D, S], F32)
    sp_xd = nc.dram_tensor("sp_xd", [D, S], F32)
    rows_dram = nc.dram_tensor("rows_dram", [6, S], F32)
    small_dram = nc.dram_tensor("small_dram", [16, 64], F32)
    k_hbm = nc.dram_tensor("k_hbm", [1024, S], BF16)
    q_hbm = nc.dram_tensor("q_hbm", [1024, 2048], BF16)
    vtok_hbm = nc.dram_tensor("vtok_hbm", [64, 128, 16 * 80], BF16)
    sblk_dram = nc.dram_tensor("sblk_dram", [D, 2048], F32)
    sk_hbm = nc.dram_tensor("sk_hbm", [9, 1024, 1024], BF16)
    sq_hbm = nc.dram_tensor("sq_hbm", [9, 1024, 1024], BF16)
    sv_hbm = nc.dram_tensor("sv_hbm", [9, 8 * 128, 1280], BF16)
    ctxacc = nc.dram_tensor("ctxacc", [8, 2, 2, 65, 1024], F32)
    sums_dram = nc.dram_tensor("sums_dram", [16, 2048], F32)
    xdown_dram = nc.dram_tensor("xdown_dram", [D, 2048], F32)
    xnown_dram = nc.dram_tensor("xnown_dram", [D, 2048], F32)
    xown_dram = nc.dram_tensor("xown_dram", [D, 2048], F32)
    kraw_hbm = nc.dram_tensor("kraw_hbm", [128, S], BF16)
    kmix_hbm = nc.dram_tensor("kmix_hbm", [2, 128, 2560 + S], BF16)
    vmoa_hbm = nc.dram_tensor("vmoa_hbm", [84, 128, 144], BF16)
    vpad_hbm = nc.dram_tensor("vpad_hbm", [8, 8, 128, 512], BF16)
    sm6_dram = nc.dram_tensor("sm6_dram", [1, 4096], F32)
    gmoa_dram = nc.dram_tensor("gmoa_dram", [8, 2048], F32)
    fg_dram = nc.dram_tensor("fg_dram", [16, 2048], F32)
    E.update(sp_xn=sp_xn, sp_xd=sp_xd, rows_dram=rows_dram, small_dram=small_dram,
             k_hbm=k_hbm, q_hbm=q_hbm, vtok_hbm=vtok_hbm, sblk_dram=sblk_dram,
             sk_hbm=sk_hbm, sq_hbm=sq_hbm, sv_hbm=sv_hbm, ctxacc=ctxacc,
             sums_dram=sums_dram, xdown_dram=xdown_dram, xnown_dram=xnown_dram,
             xown_dram=xown_dram, kraw_hbm=kraw_hbm, kmix_hbm=kmix_hbm, vmoa_hbm=vmoa_hbm,
             vpad_hbm=vpad_hbm, sm6_dram=sm6_dram, gmoa_dram=gmoa_dram, fg_dram=fg_dram,
             out_ch=out_ch, dbg=dbg)

    with ExitStack() as stack:
        ctx = Ctx(nc, stack)
        for sname in ["ld", "gd", "sd", "pe", "act", "dve", "st"]:
            ctx.sem(sname)
        sb = lambda n, s, d: stack.enter_context(nc.sbuf_tensor(n, s, d))

        big_all = sb("big_all", [128, NT * (CW + 1)], F32)
        big = [big_all[:, i * (CW + 1):(i + 1) * (CW + 1)] for i in range(NT)]
        ones_col = sb("ones_col", [128, 2], F32)
        onesb = sb("onesb", [128, 2], BF16)
        xstate = sb("xstate", [128, 8], F32)
        cfg_sb = sb("cfg_sb", [1, 64], I32)
        smalls = sb("smalls", [128, 64], F32)
        negc_bc = sb("negc_bc", [128, 16], F32)
        identp = sb("identp", [128, 128], F32)
        E_identp_loaded = [False]
        SD, SBETA, SCAR = 0, 8, 16

        block = stack.enter_context(nc.Block())

        E["identp_sb"] = identp

        @block.sync
        def _(sync):
            for i in range(NT):
                ctx.dma(sync, smalls[:, SD + i:SD + i + 1], bass.AP(E["dd"], 128 * i, [[1, 128], [1, 1]]))
            ctx.dma(sync, cfg_sb[:, :], E["cfg"].ap())
            ctx.dma(sync, identp[:, :], E["ident"].ap())

        @block.vector
        def _(v):
            v.memset(xstate[:, :], 0.0)
            v.memset(smalls[:, SCAR:SCAR + 8], 0.0)
            v.memset(ones_col[:, :], 1.0)
            v.memset(onesb[:, :], 1.0)
            for i in range(NT - 1):
                v.memset(big[i][:, 0:1], 0.0)
            ctx.tick(v.memset(big[NT - 1][:, 0:1], 0.0), "dve")

        @block.scalar
        def _(sc):
            ctx.wait(sc, "ld")
            ctx.tick(sc.activation(smalls[:, SBETA:SBETA + 8], smalls[:, SD:SD + 8], AF.Sigmoid), "act")

        _prep(nc, ctx, block, E, big, ones_col, xstate, smalls, SBETA, SCAR)

        if STAGE >= 2:
            _gates_mhc(nc, ctx, block, E, big, big_all, xstate, cfg_sb)
            _qkv(nc, ctx, block, E, big, cfg_sb, onesb, negc_bc)
        if STAGE >= 3:
            with ExitStack() as s34:
                ctxn = [s34.enter_context(nc.sbuf_tensor(f"ctxn{i}", [128, CW], BF16))
                        for i in range(NT)]
                E["ctxn_tiles"] = ctxn
                _mha(nc, ctx, block, E, big, big_all, cfg_sb, negc_bc)
                if STAGE >= 4:
                    E["ones_col_ref"] = ones_col
                    _attnffn(nc, ctx, block, E, big, big_all, cfg_sb)
        if STAGE >= 5:
            _moa(nc, ctx, block, E, big, big_all, cfg_sb, ones_col)
        if STAGE >= 6:
            _moe_final(nc, ctx, block, E, big, big_all, cfg_sb, ones_col)

        @block.gpsimd
        def _(g):
            for sname in ["ld", "gd", "sd", "pe", "act", "dve", "st"]:
                ctx.wait(g, sname)
            if DEBUG:
                rA = g.alloc_register("rA")
                rB = g.alloc_register("rB")
                g.reg_load(rA, cfg_sb[0:1, 0:1])
                g.reg_load(rB, cfg_sb[0:1, 1:2])
                for k, rr in ((0, rA), (1024, rB)):
                    ctx.dma(g, dbg["xd_own"].rearrange("(i p) t -> p i t", i=NT, p=128)[:, :, k:k + 1024],
                            bass.AP(E["sp_xd"], rr, [[S, 128], [128 * S, NT], [1, 1024]]), sem="gd")
                g.wait_ge(ctx.sems["gd"], ctx.cnt["gd"])
                ctx.dma(g, dbg["rows"], E["rows_dram"].ap(), sem="gd")
                ctx.dma(g, dbg["xstate"], xstate[:, :], sem="gd")
                if STAGE >= 2:
                    ctx.dma(g, dbg["sblk"], E["sblk_dram"].ap(), sem="gd")
                    ctx.dma(g, dbg["kv"], E["k_hbm"].ap()[:, 1024:1536], sem="gd")
                    ctx.dma(g, dbg["q"], E["q_hbm"].ap()[:, 512:1024], sem="gd")
                    ctx.dma(g, dbg["small"], E["small_dram"].ap(), sem="gd")
                    ctx.dma(g, dbg["vtok"], E["vtok_hbm"].ap()[7], sem="gd")
                if STAGE >= 3:
                    ctx.dma(g, dbg["acc"], E["ctxacc"].ap().rearrange("a b c d e -> (a b c d) e")[0:4 * 65, :], sem="gd")
                    ctx.dma(g, dbg["pstage"][:, 0:1024], E["sk_hbm"].ap()[1][0:128, :], sem="gd")
                    ctx.dma(g, dbg["pstage"][:, 1024:2048], E["sq_hbm"].ap()[7][0:128, :], sem="gd")
                    ctx.dma(g, dbg["pstage"][:, 2048:3072],
                            bass.AP(E["sv_hbm"], 1 * 8 * 128 * 1280, [[1280, 128], [128 * 1280, 8], [1, 128]]), sem="gd")
                if STAGE >= 5:
                    ctx.dma(g, dbg["gT"], E["gmoa_dram"].ap(), sem="gd")
            if STAGE < 6:
                ctx.dma(g, E["out_ch"], E["sp_xd"].ap()[:, 0:2048], sem="gd")
            g.wait_ge(ctx.sems["gd"], ctx.cnt["gd"])

    return nc


def _prep(nc, ctx, block, E, big, ones_col, xstate, smalls, SBETA, SCAR):
    sp_xn, sp_xd, rows_dram = E["sp_xn"], E["sp_xd"], E["rows_dram"]
    x_ch_h = E["x_ch"]
    MU, RR = 0, CW
    with ExitStack() as ps:
        psb = lambda n, s, d=F32: ps.enter_context(nc.sbuf_tensor(n, s, d))
        rows = psb("rows", [1, 4 * CW])
        rowtmp = psb("rowtmp", [1, 2 * CW])
        m_bc = psb("m_bc", [128, CW])
        r_bc = psb("r_bc", [128, CW])
        xsq = [psb(f"xsq{i}", [128, CW]) for i in range(2)]
        dtile = psb("dtile", [128, CW])
        ytile = [psb(f"ytile{i}", [128, CW]) for i in range(2)]
        ycols = psb("ycols", [128, NCH * NT])
        p_s1 = ps.enter_context(nc.psum_tensor("p_s1", [1, CW], F32))
        p_s2 = ps.enter_context(nc.psum_tensor("p_s2", [1, CW], F32))

        for c in range(NCH):
            t0 = c * CW

            @block.sync
            def _(sync, t0=t0):
                ctx.wait(sync, "ld", "dve", "pe")
                for i in range(NT):
                    ctx.dma(sync, big[i][:, 1:CW + 1], x_ch_h.ap()[128 * i:128 * (i + 1), t0:t0 + CW])

            for i in range(NT):
                @block.scalar
                def _(sc, i=i):
                    if i == 0:
                        ctx.wait(sc, "ld", "dve")
                    ctx.wait(sc, "pe")
                    ctx.tick(sc.activation(xsq[i % 2][:, :], big[i][:, 1:CW + 1], AF.Square), "act")

                @block.tensor
                def _(t, i=i):
                    if i == 0:
                        ctx.wait(t, "ld", "dve")
                    for s4 in range(4):
                        t.matmul(p_s1[:, 512 * s4:512 * (s4 + 1)], ones_col[:, 0:1],
                                 big[i][:, 1 + 512 * s4:1 + 512 * (s4 + 1)],
                                 start=(i == 0), stop=(i == NT - 1))
                    ctx.wait(t, "act")
                    for s4 in range(4):
                        mm = t.matmul(p_s2[:, 512 * s4:512 * (s4 + 1)], ones_col[:, 0:1],
                                      xsq[i % 2][:, 512 * s4:512 * (s4 + 1)],
                                      start=(i == 0), stop=(i == NT - 1))
                    ctx.tick(mm, "pe")

            @block.vector
            def _(v):
                ctx.wait(v, "pe")
                v.tensor_scalar(rows[:, MU:MU + CW], p_s1[:, :], 1.0 / D, None, ALU.mult)
                v.tensor_scalar(rowtmp[:, 0:CW], p_s2[:, :], 1.0 / D, None, ALU.mult)
                v.drain()
                v.tensor_tensor(rowtmp[:, CW:2 * CW], rows[:, MU:MU + CW], rows[:, MU:MU + CW], ALU.mult)
                v.drain()
                v.tensor_tensor(rowtmp[:, 0:CW], rowtmp[:, 0:CW], rowtmp[:, CW:2 * CW], ALU.subtract)
                ctx.tick(v.tensor_scalar(rowtmp[:, 0:CW], rowtmp[:, 0:CW], EPS_LN, None, ALU.add), "dve")

            @block.scalar
            def _(sc):
                ctx.wait(sc, "dve")
                ctx.tick(sc.activation(rowtmp[:, CW:2 * CW], rowtmp[:, 0:CW], AF.Sqrt), "act")

            @block.vector
            def _(v):
                ctx.wait(v, "act")
                v.reciprocal(rows[:, RR:RR + CW], rowtmp[:, CW:2 * CW])
                v.drain()
                ctx.tick(_newton_rsqrt(v, rows[:, RR:RR + CW], rowtmp[:, 0:CW], rowtmp[:, CW:2 * CW]), "dve")

            @block.sync
            def _(sync, t0=t0):
                ctx.wait(sync, "dve")
                ctx.dma(sync, bass.AP(rows_dram, 0 * S + t0, [[S, 1], [1, CW]]), rows[:, MU:MU + CW])
                ctx.dma(sync, bass.AP(rows_dram, 1 * S + t0, [[S, 1], [1, CW]]), rows[:, RR:RR + CW])
                sync.wait_ge(ctx.sems["ld"], ctx.cnt["ld"])
                ctx.dma(sync, m_bc[:, :], bass.AP(rows_dram, 0 * S + t0, [[0, 128], [1, CW]]))
                ctx.dma(sync, r_bc[:, :], bass.AP(rows_dram, 1 * S + t0, [[0, 128], [1, CW]]))

            for i in range(NT):
                @block.vector
                def _(v, c=c, i=i):
                    ctx.wait(v, "ld")
                    v.tensor_tensor(big[i][:, 1:CW + 1], big[i][:, 1:CW + 1], m_bc[:, :], ALU.subtract)
                    ctx.tick(v.tensor_tensor(big[i][:, 1:CW + 1], big[i][:, 1:CW + 1], r_bc[:, :], ALU.mult), "dve")
                    v.drain()
                    v.tensor_tensor(dtile[:, :], big[i][:, 1:CW + 1], big[i][:, 0:CW], ALU.subtract)
                    v.drain()
                    v.tensor_tensor_scan(ytile[i % 2][:, :],
                                         smalls[:, SBETA + i:SBETA + i + 1].broadcast_to([128, CW]),
                                         dtile[:, :], smalls[:, SCAR + i:SCAR + i + 1], ALU.mult, ALU.add)
                    v.drain()
                    v.tensor_copy(smalls[:, SCAR + i:SCAR + i + 1], ytile[i % 2][:, CW - 1:CW])
                    ctx.tick(v.reduce_sum(ycols[:, c * NT + i:c * NT + i + 1], ytile[i % 2][:, :], AX.X), "dve")

                @block.scalar
                def _(sc, i=i):
                    ctx.wait(sc, "dve", "pe")
                    ctx.tick(sc.activation(xsq[0][:, :], big[i][:, 1:CW + 1], AF.Square), "act")
                    ctx.tick(sc.activation(xsq[1][:, :], ytile[i % 2][:, :], AF.Square), "act")

                @block.tensor
                def _(t, i=i):
                    ctx.wait(t, "act")
                    for s4 in range(4):
                        t.matmul(p_s2[:, 512 * s4:512 * (s4 + 1)], ones_col[:, 0:1],
                                 xsq[0][:, 512 * s4:512 * (s4 + 1)],
                                 start=(i == 0), stop=(i == NT - 1))
                    for s4 in range(4):
                        mm = t.matmul(p_s1[:, 512 * s4:512 * (s4 + 1)], ones_col[:, 0:1],
                                      xsq[1][:, 512 * s4:512 * (s4 + 1)],
                                      start=(i == 0), stop=(i == NT - 1))
                    ctx.tick(mm, "pe")

                @block.sync
                def _(sync, i=i, t0=t0):
                    ctx.wait(sync, "dve")
                    ctx.dma(sync, bass.AP(sp_xn, i * 128 * S + t0, [[S, 128], [1, CW]]), big[i][:, 1:CW + 1])
                    ctx.dma(sync, bass.AP(sp_xd, i * 128 * S + t0, [[S, 128], [1, CW]]), ytile[i % 2][:, :])

            @block.vector
            def _(v):
                ctx.wait(v, "ld")
                for i in range(NT - 1):
                    v.tensor_copy(big[i][:, 0:1], big[i][:, CW:CW + 1])
                ctx.tick(v.tensor_copy(big[NT - 1][:, 0:1], big[NT - 1][:, CW:CW + 1]), "dve")

            for which in (0, 1):
                @block.vector
                def _(v, which=which):
                    if which == 0:
                        ctx.wait(v, "pe")
                        v.tensor_scalar(rowtmp[:, 0:CW], p_s1[:, :], 1.0 / D, None, ALU.mult)
                        v.drain()
                        ctx.tick(v.tensor_scalar(rowtmp[:, 0:CW], rowtmp[:, 0:CW],
                                                 EPS_RMS, None, ALU.add), "dve")
                    else:
                        v.tensor_scalar(rowtmp[:, 0:CW], p_s2[:, :], 1.0 / D, None, ALU.mult)
                        v.drain()
                        ctx.tick(v.tensor_scalar(rowtmp[:, 0:CW], rowtmp[:, 0:CW],
                                                 EPS_LN, None, ALU.add), "dve")

                @block.scalar
                def _(sc, which=which):
                    ctx.wait(sc, "dve")
                    ctx.tick(sc.activation(rowtmp[:, CW:2 * CW], rowtmp[:, 0:CW], AF.Sqrt), "act")

                @block.vector
                def _(v, which=which):
                    ctx.wait(v, "act")
                    RX = 2 * CW if which == 0 else 3 * CW
                    v.reciprocal(rows[:, RX:RX + CW], rowtmp[:, CW:2 * CW])
                    v.drain()
                    ctx.tick(_newton_rsqrt(v, rows[:, RX:RX + CW], rowtmp[:, 0:CW],
                                           rowtmp[:, CW:2 * CW]), "dve")

            @block.sync
            def _(sync, t0=t0):
                ctx.wait(sync, "dve")
                ctx.dma(sync, bass.AP(rows_dram, 2 * S + t0, [[S, 1], [1, CW]]), rows[:, 2 * CW:3 * CW])
                ctx.dma(sync, bass.AP(rows_dram, 3 * S + t0, [[S, 1], [1, CW]]), rows[:, 3 * CW:4 * CW])

        @block.vector
        def _(v):
            for c2 in range(NCH):
                v.drain()
                v.tensor_tensor(xstate[:, 0:NT], xstate[:, 0:NT], ycols[:, c2 * NT:(c2 + 1) * NT], ALU.add)
            v.drain()
            ctx.tick(v.tensor_copy(xstate[:, 0:NT], xstate[:, 0:NT]), "dve")


def _gates_mhc(nc, ctx, block, E, big, big_all, xstate, cfg_sb):
    sp_xd, small_dram, sblk_dram = E["sp_xd"], E["small_dram"], E["sblk_dram"]
    with ExitStack() as s2:
        sb2 = lambda n, s, d=F32: s2.enter_context(nc.sbuf_tensor(n, s, d))
        kH = sb2("kH", [8, 8])
        kHT = sb2("kHT", [8, 8])
        hrow = sb2("hrow", [8, 1])
        ident = sb2("identt", [128, 128])
        gwT_sb = sb2("gwT_sb", [128, 8 * 24])
        gcol = sb2("gcol", [24, 1])
        gbcol = sb2("gbcol", [24, 1])
        gres = sb2("gres", [8, 2])
        a_bc = sb2("a_bc", [128, 64])
        mt = [sb2(f"mt{n}", [128, 1024], BF16) for n in range(NT)]
        xdb_all = sb2("xdb_all", [128, NT * CW], BF16)
        xdb = [xdb_all[:, i * CW:(i + 1) * CW] for i in range(NT)]
        evt = sb2("evt", [128, 512])
        p_sk = s2.enter_context(nc.psum_tensor("p_sk", [8, 8], F32))
        p_g = s2.enter_context(nc.psum_tensor("p_g", [24, 1], F32))
        p_m = [s2.enter_context(nc.psum_tensor(f"p_m{i}", [128, 512], F32)) for i in range(2)]

        @block.sync
        def _(sync):
            ctx.dma(sync, kH[:, :], E["phi"].ap())
            ctx.dma(sync, ident[:, :], E["ident"].ap())
            ctx.dma(sync, gwT_sb[:, :], bass.AP(E["gwT"], 0, [[24, 128], [128 * 24, 8], [1, 24]]))
            ctx.dma(sync, gbcol[:, :], bass.AP(E["gb"], 0, [[1, 24], [1, 1]]))

        @block.scalar
        def _(sc):
            ctx.wait(sc, "ld")
            ctx.tick(sc.activation(kH[:, :], kH[:, :], AF.Exp), "act")

        for it in range(15):
            @block.vector
            def _(v, it=it):
                if it == 0:
                    ctx.wait(v, "act")
                else:
                    ctx.wait(v, "pe")
                    v.tensor_copy(kH[:, :], p_sk[:, :])
                    v.drain()
                v.reduce_sum(hrow[:, :], kH[:, :], AX.X)
                v.drain()
                v.reciprocal(hrow[:, :], hrow[:, :])
                v.drain()
                ctx.tick(v.tensor_scalar(kH[:, :], kH[:, :], hrow[:, :], None, ALU.mult), "dve")

            @block.tensor
            def _(t):
                ctx.wait(t, "dve")
                ctx.tick(t.matmul(p_sk[:, :], kH[:, :], ident[0:8, 0:8], is_transpose=True,
                                  start=True, stop=True), "pe")

            @block.vector
            def _(v):
                ctx.wait(v, "pe")
                v.tensor_copy(kHT[:, :], p_sk[:, :])
                v.drain()
                v.reduce_sum(hrow[:, :], kHT[:, :], AX.X)
                v.drain()
                v.reciprocal(hrow[:, :], hrow[:, :])
                v.drain()
                ctx.tick(v.tensor_scalar(kHT[:, :], kHT[:, :], hrow[:, :], None, ALU.mult), "dve")

        @block.tensor
        def _(t):
            ctx.wait(t, "dve")
            for i in range(NT):
                mm = t.matmul(p_g[:, :], gwT_sb[:, 24 * i:24 * (i + 1)], xstate[:, i:i + 1],
                              start=(i == 0), stop=(i == NT - 1))
            ctx.tick(mm, "pe")

        @block.vector
        def _(v):
            ctx.wait(v, "pe")
            ctx.tick(v.tensor_scalar(gcol[:, :], p_g[:, :], 1.0 / S, gbcol[:, :], ALU.mult, ALU.add), "dve")

        @block.sync
        def _(sync):
            ctx.wait(sync, "dve")
            ctx.dma(sync, bass.AP(small_dram, 10 * 64, [[1, 24], [1, 1]]), gcol[:, :])
            sync.wait_ge(ctx.sems["ld"], ctx.cnt["ld"])
            ctx.dma(sync, gres[:, 0:1], bass.AP(small_dram, 10 * 64 + 16, [[1, 8], [1, 1]]))

        @block.scalar
        def _(sc):
            ctx.wait(sc, "ld")
            ctx.tick(sc.activation(gres[:, 1:2], gres[:, 0:1], AF.Sigmoid), "act")

        @block.vector
        def _(v):
            ctx.wait(v, "act")
            ctx.tick(v.tensor_scalar(kHT[:, :], kHT[:, :], gres[:, 1:2], None, ALU.mult), "dve")

        @block.sync
        def _(sync):
            ctx.wait(sync, "dve")
            ctx.dma(sync, bass.AP(small_dram, 0, [[64, 8], [1, 8]]), kHT[:, :])
            sync.wait_ge(ctx.sems["ld"], ctx.cnt["ld"])
            ctx.dma(sync, a_bc[:, :], bass.AP(small_dram, 0, [[0, 128], [64, 8], [1, 8]]))

        @block.gpsimd
        def _(g):
            ctx.wait(g, "ld", "dve", "pe", "act")
            rA = g.alloc_register("r2A")
            rB = g.alloc_register("r2B")
            g.reg_load(rA, cfg_sb[0:1, 0:1])
            g.reg_load(rB, cfg_sb[0:1, 1:2])
            for k, rr in ((0, rA), (1024, rB)):
                ctx.dma(g, big_all[:, :].rearrange("p (i w) -> p i w", i=NT, w=CW + 1)[:, :, k:k + 1024],
                        bass.AP(sp_xd, rr, [[S, 128], [128 * S, NT], [1, 1024]]), sem="gd")
                ctx.dma(g, xdb_all[:, :].rearrange("p (i w) -> p i w", i=NT, w=CW)[:, :, k:k + 1024],
                        bass.AP(sp_xd, rr, [[S, 128], [128 * S, NT], [1, 1024]]), sem="gd")
            g.wait_ge(ctx.sems["gd"], ctx.cnt["gd"])

        @block.vector
        def _(v):
            ctx.wait(v, "ld")
            for n in range(NT):
                for m in range(NT):
                    mm = v.tensor_scalar(mt[n][:, 128 * m:128 * (m + 1)], ident[:, :],
                                         a_bc[:, 8 * n + m:8 * n + m + 1], None, ALU.mult)
            ctx.tick(mm, "dve")

        for m in range(NT):
            for ch in range(4):
                pp = (m * 4 + ch) % 2

                @block.tensor
                def _(t, m=m, ch=ch, pp=pp):
                    ctx.wait(t, "dve", "gd")
                    if m * 4 + ch >= 2:
                        ctx.wait(t, "dve")
                    for n in range(NT):
                        mm = t.matmul(p_m[pp][:, :], mt[n][:, 128 * m:128 * (m + 1)],
                                      xdb[n][:, 512 * ch:512 * (ch + 1)],
                                      start=(n == 0), stop=(n == NT - 1))
                    ctx.tick(mm, "pe")

                @block.vector
                def _(v, m=m, ch=ch, pp=pp):
                    ctx.wait(v, "pe", "ld")
                    ctx.tick(v.tensor_tensor(evt[:, :], p_m[pp][:, :],
                                             big[m][:, 512 * ch:512 * (ch + 1)], ALU.add), "dve")

                @block.sync
                def _(sync, m=m, ch=ch):
                    ctx.wait(sync, "dve")
                    ctx.dma(sync, bass.AP(sblk_dram, m * 128 * 2048 + 512 * ch, [[2048, 128], [1, 512]]),
                            evt[:, :])

        @block.sync
        def _(sync):
            sync.wait_ge(ctx.sems["ld"], ctx.cnt["ld"])


def _qkv(nc, ctx, block, E, big, cfg_sb, onesb, negc_bc):
    sp_xd, rows_dram, small_dram = E["sp_xd"], E["rows_dram"], E["small_dram"]
    k_hbm, q_hbm, vtok_hbm = E["k_hbm"], E["q_hbm"], E["vtok_hbm"]
    with ExitStack() as s3:
        sb3 = lambda n, s, d=F32: s3.enter_context(nc.sbuf_tensor(n, s, d))
        wq = [sb3(f"wq{i}", [128, 3072], BF16) for i in range(NT)]
        rhs = [sb3(f"rhs{i}", [128, 512], BF16) for i in range(NT)]
        xdf_all = sb3("xdf_all", [128, NT * 512])
        xdf = [xdf_all[:, i * 512:(i + 1) * 512] for i in range(NT)]
        r1bc = sb3("r1bc", [128, 512])
        stg = [sb3(f"stg{i}", [128, 512], BF16) for i in range(2)]
        sq = sb3("sq", [128, 512], BF16)
        vstg = [sb3(f"vstg{i}", [128, 1280], BF16) for i in range(2)]
        vb_bc = sb3("vb_bc", [128, 1024])
        ln1c = sb3("ln1c", [128, 8])
        bcol = sb3("bcol", [128, 24])
        h2 = sb3("h2", [128, 2], BF16)
        knq = sb3("knq", [2, 32])
        p_kv = [s3.enter_context(nc.psum_tensor(f"p_kv{i}", [128, 512], F32)) for i in range(2)]
        p_vt = [s3.enter_context(nc.psum_tensor(f"p_vt{i}", [128, 512], F32)) for i in range(2)]
        p_n = s3.enter_context(nc.psum_tensor("p_n", [2, 512], F32))

        @block.sync
        def _(sync):
            for i in range(NT):
                ctx.dma(sync, ln1c[:, i:i + 1], bass.AP(E["ln1w"], 128 * i, [[1, 128], [1, 1]]))
            for o in range(24):
                ctx.dma(sync, bcol[:, o:o + 1], bass.AP(E["bqkv"], 128 * o, [[1, 128], [1, 1]]))
            ctx.dma(sync, vb_bc[:, :], bass.AP(E["bqkv"], 2048, [[0, 128], [1, 1024]]))

        @block.gpsimd
        def _(g):
            ctx.wait(g, "ld")
            for i in range(NT):
                ctx.dma(g, wq[i][:, :], E["wqkvT"].ap()[128 * i:128 * (i + 1), :], sem="gd")

        @block.vector
        def _(v):
            ctx.wait(v, "ld", "gd")
            v.memset(h2[:, :], 0.0)
            v.memset(h2[0:64, 0:1], 1.0)
            v.memset(h2[64:128, 1:2], 1.0)
            v.memset(knq[:, 0:16], 0.0)
            v.memset(vstg[0][:, :], 1.0)
            v.memset(vstg[1][:, :], 1.0)
            v.tensor_scalar(bcol[:, 0:8], bcol[:, 0:8], 0.125, None, ALU.mult)
            for i in range(NT):
                mm = v.tensor_scalar(wq[i][:, :], wq[i][:, :], ln1c[:, i:i + 1], None, ALU.mult)
            ctx.tick(mm, "dve")

        nloop = [("kv", ch) for ch in range(16)] + [("q", ch) for ch in range(4)]

        for kind, ch in nloop:
            if kind == "kv":
                @block.sync
                def _(sync, ch=ch):
                    ctx.wait(sync, "pe", "dve")
                    ctx.dma(sync, r1bc[:, :], bass.AP(rows_dram, 2 * S + 512 * ch, [[0, 128], [1, 512]]))
                    for i in range(NT):
                        ctx.dma(sync, xdf[i][:, :],
                                bass.AP(sp_xd, i * 128 * S + 512 * ch, [[S, 128], [1, 512]]))
            else:
                @block.gpsimd
                def _(g, ch=ch):
                    ctx.wait(g, "pe", "dve")
                    rr = g.alloc_register(f"rq{ch}")
                    g.reg_load(rr, cfg_sb[0:1, 0:1] if ch < 2 else cfg_sb[0:1, 1:2])
                    ro = g.alloc_register(f"rqo{ch}")
                    g.reg_alu(ro, rr, (ch % 2) * 512, ALU.add)
                    rbase = g.alloc_register(f"rqb{ch}")
                    g.reg_alu(rbase, ro, 2 * S, ALU.add)
                    ctx.dma(g, r1bc[:, :], bass.AP(rows_dram, rbase, [[0, 128], [1, 512]]), sem="gd")
                    ctx.dma(g, xdf_all[:, :].rearrange("p (i w) -> p i w", i=NT, w=512),
                            bass.AP(sp_xd, ro, [[S, 128], [128 * S, NT], [1, 512]]), sem="gd")
                    g.wait_ge(ctx.sems["gd"], ctx.cnt["gd"])

            @block.vector
            def _(v, kind=kind):
                ctx.wait(v, "ld", "gd")
                for i in range(NT):
                    mm = v.tensor_tensor(rhs[i][:, :], xdf[i][:, :], r1bc[:, :], ALU.mult)
                ctx.tick(mm, "dve")

            ots = range(8, 16) if kind == "kv" else range(8)
            for ot in ots:
                pp = ot % 2

                @block.tensor
                def _(t, ot=ot, ch=ch, pp=pp, kind=kind):
                    ctx.wait(t, "dve", "act")
                    for i in range(NT):
                        mm = t.matmul(p_kv[pp][:, :], wq[i][:, 128 * ot:128 * (ot + 1)],
                                      rhs[i][:, :], start=(i == 0), stop=(i == NT - 1))
                    ctx.tick(mm, "pe")

                @block.vector
                def _(v, ot=ot, pp=pp, kind=kind):
                    ctx.wait(v, "pe", "ld")
                    ctx.tick(v.tensor_scalar(stg[pp][:, :], p_kv[pp][:, :],
                                             (0.125 if kind == "q" else 1.0), bcol[:, ot:ot + 1],
                                             ALU.mult, ALU.add), "dve")

                @block.scalar
                def _(sc, ot=ot, pp=pp, kind=kind):
                    ctx.wait(sc, "dve")
                    ctx.tick(sc.activation(sq[:, :], stg[pp][:, :], AF.Square), "act")

                @block.tensor
                def _(t):
                    ctx.wait(t, "act", "dve")
                    ctx.tick(t.matmul(p_n[:, :], h2[:, :], sq[:, :], start=True, stop=True), "pe")

                @block.vector
                def _(v, ot=ot, kind=kind):
                    ctx.wait(v, "pe")
                    col = (ot - 8) if kind == "kv" else (8 + ot)
                    v.reduce_max(knq[:, 16:17], p_n[:, :], AX.X)
                    v.drain()
                    ctx.tick(v.tensor_tensor(knq[:, col:col + 1], knq[:, col:col + 1],
                                             knq[:, 16:17], ALU.max), "dve")

                @block.sync
                def _(sync, ot=ot, ch=ch, pp=pp, kind=kind):
                    ctx.wait(sync, "dve")
                    if kind == "kv":
                        ctx.dma(sync, k_hbm.ap()[128 * (ot - 8):128 * (ot - 7), 512 * ch:512 * (ch + 1)],
                                stg[pp][:, :])
                    else:
                        ctx.dma(sync, q_hbm.ap()[128 * ot:128 * (ot + 1), 512 * ch:512 * (ch + 1)],
                                stg[pp][:, :])

            if kind == "kv":
                for tt in range(4):
                    kt = 4 * ch + tt
                    vs = vstg[kt % 2]
                    for hf in range(2):
                        @block.tensor
                        def _(t, tt=tt, hf=hf):
                            ctx.wait(t, "dve", "act")
                            for i in range(NT):
                                mm = t.matmul(p_vt[hf][:, :], rhs[i][:, 128 * tt:128 * (tt + 1)],
                                              wq[i][:, 2048 + 512 * hf:2048 + 512 * (hf + 1)],
                                              start=(i == 0), stop=(i == NT - 1))
                            ctx.tick(mm, "pe")

                        @block.vector
                        def _(v, hf=hf, vs=vs):
                            ctx.wait(v, "pe", "ld")
                            ctx.tick(v.tensor_tensor(
                                vs[:, :].rearrange("p (h w) -> p h w", h=16, w=80)[:, 8 * hf:8 * (hf + 1), 0:64],
                                p_vt[hf][:, :].rearrange("p (h d) -> p h d", h=8, d=64),
                                vb_bc[:, 512 * hf:512 * (hf + 1)].rearrange("p (h d) -> p h d", h=8, d=64),
                                ALU.add), "act")

                    @block.sync
                    def _(sync, kt=kt, vs=vs):
                        ctx.wait(sync, "act")
                        ctx.dma(sync, bass.AP(vtok_hbm, kt * 128 * 1280, [[1280, 128], [1, 1280]]),
                                vs[:, :])

        @block.scalar
        def _(sc):
            ctx.wait(sc, "dve")
            ctx.tick(sc.activation(knq[:, 16:32], knq[:, 0:16], AF.Sqrt), "act")

        @block.vector
        def _(v):
            ctx.wait(v, "act")
            v.tensor_tensor(knq[:, 0:8], knq[:, 16:24], knq[:, 24:32], ALU.mult)
            v.drain()
            ctx.tick(v.tensor_scalar(knq[:, 0:8], knq[:, 0:8], -1.0, None, ALU.mult), "dve")

        @block.sync
        def _(sync):
            ctx.wait(sync, "dve")
            ctx.dma(sync, bass.AP(small_dram, 8 * 64, [[1, 2], [2, 8]]), knq[:, 0:8],
                    allow_slow_non_contiguous=True)
            sync.wait_ge(ctx.sems["ld"], ctx.cnt["ld"])
            ctx.dma(sync, negc_bc[:, :], bass.AP(small_dram, 8 * 64, [[0, 128], [1, 16]]))
            sync.wait_ge(ctx.sems["ld"], ctx.cnt["ld"])


def _mha(nc, ctx, block, E, big, big_all, cfg_sb, negc_bc):
    sk_hbm, sq_hbm, sv_hbm = E["sk_hbm"], E["sq_hbm"], E["sv_hbm"]
    ctxacc, sums_dram = E["ctxacc"], E["sums_dram"]
    k_hbm, q_hbm, vtok_hbm = E["k_hbm"], E["q_hbm"], E["vtok_hbm"]
    with ExitStack() as s4:
        sb4 = lambda n, s, d=F32: s4.enter_context(nc.sbuf_tensor(n, s, d))
        negcp = sb4("negcp", [128, 8], F32)
        skt = [sb4(f"skt{i}", [128, 1024], BF16) for i in range(2)]
        sqt = [sb4(f"sqt{i}", [128, 1024], BF16) for i in range(2)]
        svt = [sb4(f"svt{i}", [128, 1280], BF16) for i in range(2)]
        Pab = sb4("Pab", [128, 2048], BF16)
        mskm = sb4("mskm", [128, 2048], BF16)
        pmsk = sb4("pmsk", [128, 18])
        abuf = [sb4(f"abuf{i}", [65, 1024], F32) for i in range(4)]
        zeros = sb4("zeros", [128, 1024], F32)
        p_scb = s4.enter_context(nc.psum_tensor("p_scb", [128, 2048], F32))
        p_cx = [s4.enter_context(nc.psum_tensor(f"p_cx{i}", [128, 1024], F32)) for i in range(2)]

        @block.sync
        def _(sync):
            ctx.wait(sync, "ld", "act", "dve")
            ctx.dma(sync, mskm[:, :], E["maskm"].ap())
            ctx.dma(sync, pmsk[:, :], E["posmask"].ap())
            r = sync.alloc_register("skr")
            for pos in range(9):
                sync.reg_load(r, cfg_sb[0:1, 8 + pos:9 + pos])
                ctx.dma(sync, sk_hbm.ap()[pos], bass.AP(k_hbm, r, [[S, 1024], [1, 1024]]))
            for pos in range(9):
                sync.reg_load(r, cfg_sb[0:1, 26 + pos:27 + pos])
                ctx.dma(sync, bass.AP(sv_hbm, pos * 8 * 128 * 1280, [[1, 1], [1, 8 * 128 * 1280]]),
                    bass.AP(vtok_hbm, r, [[1, 1], [1, 8 * 128 * 1280]]))

        @block.scalar
        def _(sc):
            ctx.wait(sc, "ld", "act", "dve")
            r = sc.alloc_register("sqr")
            for pos in range(9):
                sc.reg_load(r, cfg_sb[0:1, 17 + pos:18 + pos])
                ctx.dma(sc, sq_hbm.ap()[pos], bass.AP(q_hbm, r, [[2048, 1024], [1, 1024]]), sem="sd")

        @block.gpsimd
        def _(g):
            ctx.wait(g, "ld", "act", "dve", "sd")
            pass

        @block.vector
        def _(v):
            ctx.wait(v, "ld")
            for p8 in range(8):
                v.tensor_tensor(negcp[:, p8:p8 + 1], negc_bc[:, 2 * p8:2 * p8 + 1],
                                negc_bc[:, 2 * p8 + 1:2 * p8 + 2], ALU.min)
            ctx.tick(v.memset(zeros[:, :], 0.0), "dve")

        @block.gpsimd
        def _(g):
            ctx.wait(g, "dve")
            for t in range(32):
                ctx.dma(g, ctxacc.ap().rearrange("a b c d e -> (a b c) d e")[t][0:65, :],
                        zeros[0:65, :], sem="gd")
            g.wait_ge(ctx.sems["gd"], ctx.cnt["gd"])

        for pos in range(9):
            for pair in range(8):
                it = pos * 8 + pair
                db = it % 2

                @block.sync
                def _(sync, pos=pos, pair=pair, db=db, it=it):
                    if it < 2:
                        ctx.wait(sync, "gd", "sd", "ld")
                    else:
                        ctx.wait(sync, "pe")
                    ctx.dma(sync, skt[db][:, :], sk_hbm.ap()[pos][128 * pair:128 * (pair + 1), :])
                    ctx.dma(sync, sqt[db][:, :], sq_hbm.ap()[pos][128 * pair:128 * (pair + 1), :])
                    ctx.dma(sync, svt[db][:, :].rearrange("p (a w) -> p a w", a=8, w=160),
                            bass.AP(sv_hbm, pos * 8 * 128 * 1280 + 160 * pair,
                                    [[1280, 128], [128 * 1280, 8], [1, 160]]))

                for kt in range(8):
                    @block.tensor
                    def _(t, pos=pos, pair=pair, db=db, kt=kt, it=it):
                        if kt == 0:
                            ctx.wait(t, "ld")
                            ctx.wait(t, "act", "dve")
                        else:
                            ctx.wait(t, "act")
                            if pos >= 7:
                                ctx.wait(t, "dve")
                        for hf in range(2):
                            t.matmul(p_scb[:, 512 * hf:512 * (hf + 1)],
                                     skt[db][0:64, 128 * kt:128 * (kt + 1)],
                                     sqt[db][0:64, 512 * hf:512 * (hf + 1)],
                                     start=True, stop=True, tile_position=(0, 0))
                            mm = t.matmul(p_scb[:, 1024 + 512 * hf:1024 + 512 * (hf + 1)],
                                          skt[db][64:128, 128 * kt:128 * (kt + 1)],
                                          sqt[db][64:128, 512 * hf:512 * (hf + 1)],
                                          start=True, stop=True, tile_position=(64, 0))
                        ctx.tick(mm, "pe")

                    @block.scalar
                    def _(sc, pair=pair, kt=kt):
                        ctx.wait(sc, "pe")
                        ctx.tick(sc.activation(Pab[:, :], p_scb[:, :], AF.Exp,
                                               bias=negcp[:, pair:pair + 1]), "act")

                    if pos >= 7:
                        @block.vector
                        def _(v, kt=kt):
                            ctx.wait(v, "act")
                            o = 1024 - 128 * kt
                            v.tensor_tensor(Pab[:, 0:1024], Pab[:, 0:1024], mskm[:, o:o + 1024], ALU.mult)
                            ctx.tick(v.tensor_tensor(Pab[:, 1024:2048], Pab[:, 1024:2048],
                                                     mskm[:, o:o + 1024], ALU.mult), "dve")

                    @block.tensor
                    def _(t, pos=pos, db=db, kt=kt):
                        ctx.wait(t, "act")
                        if pos >= 7:
                            ctx.wait(t, "dve")
                        for hd in range(2):
                            for hf in range(2):
                                mm = t.matmul(p_cx[hd][0:65, 512 * hf:512 * (hf + 1)],
                                              svt[db][:, 160 * kt + 80 * hd:160 * kt + 80 * hd + 65],
                                              Pab[:, 1024 * hd + 512 * hf:1024 * hd + 512 * (hf + 1)],
                                              start=(kt == 0), stop=(kt == 7))
                        ctx.tick(mm, "pe")

                @block.vector
                def _(v, pos=pos, pair=pair):
                    ctx.wait(v, "pe", "gd")
                    v.tensor_scalar(abuf[0][:, :], p_cx[0][0:65, :], pmsk[0:65, pos:pos + 1], None, ALU.mult)
                    v.tensor_scalar(abuf[2][:, :], p_cx[1][0:65, :], pmsk[0:65, pos:pos + 1], None, ALU.mult)
                    v.tensor_scalar(abuf[1][:, :], p_cx[0][0:65, :], pmsk[0:65, 9 + pos:10 + pos], None, ALU.mult)
                    ctx.tick(v.tensor_scalar(abuf[3][:, :], p_cx[1][0:65, :], pmsk[0:65, 9 + pos:10 + pos],
                                             None, ALU.mult), "dve")

                if DEBUG and pos == 7 and pair == 0:
                    @block.sync
                    def _(sync):
                        ctx.wait(sync, "act", "dve")
                        ctx.dma(sync, E["dbg"]["Pa"], Pab[:, 0:1024])
                        ctx.dma(sync, E["dbg"]["ab0"], abuf[0][0:65, :])
                        ctx.dma(sync, E["dbg"]["ab1"], abuf[1][0:65, :])

                @block.gpsimd
                def _(g, pos=pos, pair=pair):
                    ctx.wait(g, "dve", "gd", "ld")
                    for hd in range(2):
                        for st in range(2):
                            ctx.dma(g, ctxacc.ap()[pair][hd][st][0:65, :], abuf[2 * hd + st][0:65, :],
                                    accum_op=ALU.add, sem="gd")

        ctxn = E["ctxn_tiles"]
        ctxf = sb4("ctxf", [128, 2048])
        rnbc = sb4("rnbc", [128, 2048])
        sums16 = sb4("sums16", [16, 2048])

        @block.gpsimd
        def _(g):
            ctx.wait(g, "gd")
            for st in range(2):
                ctx.dma(g, sums16[:, 1024 * st:1024 * (st + 1)],
                        bass.AP(ctxacc, 64 * 1024 + st * 65 * 1024, [[2 * 65 * 1024, 16], [1, 1024]]),
                        sem="gd")

        @block.vector
        def _(v):
            ctx.wait(v, "gd")
            ctx.tick(v.reciprocal(sums16[:, :], sums16[:, :]), "dve")

        @block.sync
        def _(sync):
            ctx.wait(sync, "dve")
            ctx.dma(sync, sums_dram.ap(), sums16[:, :])
            sync.wait_ge(ctx.sems["ld"], ctx.cnt["ld"])

        for pair in range(8):
            @block.sync
            def _(sync, pair=pair):
                ctx.wait(sync, "dve")
                ctx.dma(sync, rnbc[:, :],
                        bass.AP(sums_dram, 2 * pair * 2048, [[2048, 2], [0, 64], [1, 2048]]))

            @block.gpsimd
            def _(g, pair=pair):
                ctx.wait(g, "gd", "dve")
                for hd in range(2):
                    ctx.dma(g, ctxf[64 * hd:64 * (hd + 1), :].rearrange("p (s q) -> p s q", s=2, q=1024),
                            bass.AP(ctxacc, (pair * 2 + hd) * 2 * 65 * 1024,
                                    [[1024, 64], [65 * 1024, 2], [1, 1024]]), sem="gd")

            @block.vector
            def _(v, pair=pair):
                ctx.wait(v, "ld", "gd")
                ctx.tick(v.tensor_tensor(ctxn[pair][:, :], ctxf[:, :], rnbc[:, :], ALU.mult), "dve")

        if DEBUG:
            @block.sync
            def _(sync):
                ctx.wait(sync, "dve", "gd")
                ctx.dma(sync, E["dbg"]["acc2"], ctxacc.ap()[0][0][0][0:65, :])

        if DEBUG:
            @block.sync
            def _(sync):
                ctx.wait(sync, "dve")
                ctx.dma(sync, E["dbg"]["ctxn"], ctxn[0][:, :])
                ctx.dma(sync, E["dbg"]["sums"], sums16[:, :])


def _attnffn(nc, ctx, block, E, big, big_all, cfg_sb):
    sblk_dram, rows_dram = E["sblk_dram"], E["rows_dram"]
    ctxn = E["ctxn_tiles"]
    with ExitStack() as s5:
        sb5 = lambda n, s, d=F32: s5.enter_context(nc.sbuf_tensor(n, s, d))
        outw = [sb5(f"outw{i}", [128, 1024], BF16) for i in range(NT)]
        obcol = sb5("obcol", [128, 8])
        b2col = sb5("b2col", [128, 8])
        ln2c = sb5("ln2c", [128, 8])
        b1col = sb5("b1col", [128, 32])
        E["b1col_sb"] = b1col
        sbo = [sb5(f"sbo{i}", [128, 512]) for i in range(NT)]
        p_o = [s5.enter_context(nc.psum_tensor(f"p_o{i}", [128, 512], F32)) for i in range(2)]
        p_st = s5.enter_context(nc.psum_tensor("p_st", [1, 512], F32))

        @block.gpsimd
        def _(g):
            ctx.wait(g, "dve")
            for i in range(NT):
                ctx.dma(g, outw[i][:, :], E["outwT"].ap()[128 * i:128 * (i + 1), :], sem="gd")

        @block.sync
        def _(sync):
            for o in range(NT):
                ctx.dma(sync, obcol[:, o:o + 1], bass.AP(E["outb"], 128 * o, [[1, 128], [1, 1]]))
                ctx.dma(sync, b2col[:, o:o + 1], bass.AP(E["b2"], 128 * o, [[1, 128], [1, 1]]))
                ctx.dma(sync, ln2c[:, o:o + 1], bass.AP(E["ln2w"], 128 * o, [[1, 128], [1, 1]]))
            for o in range(32):
                ctx.dma(sync, b1col[:, o:o + 1], bass.AP(E["b1"], 128 * o, [[1, 128], [1, 1]]))

        for dt in range(NT):
            for ch in range(4):
                pp = (dt * 4 + ch) % 2

                @block.sync
                def _(sync, dt=dt, ch=ch):
                    if dt == 0 and ch == 0:
                        pass
                    ctx.wait(sync, "dve", "ld")
                    ctx.dma(sync, sbo[dt][:, :],
                            bass.AP(sblk_dram, dt * 128 * 2048 + 512 * ch, [[2048, 128], [1, 512]]))

                @block.tensor
                def _(t, dt=dt, ch=ch, pp=pp):
                    if dt == 0 and ch == 0:
                        ctx.wait(t, "gd", "dve")
                    else:
                        ctx.wait(t, "act", "dve")
                    for i in range(NT):
                        mm = t.matmul(p_o[pp][:, :], outw[i][:, 128 * dt:128 * (dt + 1)],
                                      ctxn[i][:, 512 * ch:512 * (ch + 1)],
                                      start=(i == 0), stop=(i == NT - 1))
                    ctx.tick(mm, "pe")

                @block.vector
                def _(v, dt=dt, ch=ch, pp=pp):
                    ctx.wait(v, "pe", "ld")
                    sl = slice(512 * ch, 512 * (ch + 1))
                    v.scalar_tensor_tensor(sbo[dt][:, :], p_o[pp][:, :], obcol[:, dt:dt + 1],
                                           sbo[dt][:, :], ALU.add, ALU.add)
                    v.drain()
                    ctx.tick(v.scalar_tensor_tensor(big[dt][:, sl], p_o[pp][:, :], obcol[:, dt:dt + 1],
                                                    big[dt][:, sl], ALU.add, ALU.add), "dve")

                @block.sync
                def _(sync, dt=dt, ch=ch):
                    ctx.wait(sync, "dve")
                    ctx.dma(sync, bass.AP(sblk_dram, dt * 128 * 2048 + 512 * ch, [[2048, 128], [1, 512]]),
                            sbo[dt][:, :])

        nrm = [sb5(f"nrm{i}", [128, 512], BF16) for i in range(NT)]
        htile = sb5("htile", [128, 32 * 512], BF16)
        w1g = [sb5(f"w1g{i}", [128, 8 * 128], BF16) for i in range(NT)]
        w2g = sb5("w2g", [128, 4096], BF16)
        rowf = sb5("rowf", [1, 3 * 512])
        r2bc = sb5("r2bc", [128, 512])
        sq5 = sb5("sq5", [128, 512])

        for ch in range(4):
            for i in range(NT):
                @block.scalar
                def _(sc, i=i, ch=ch):
                    ctx.wait(sc, "dve") if i == 0 else None
                    ctx.wait(sc, "pe")
                    ctx.tick(sc.activation(sq5[:, :], big[i][:, 512 * ch:512 * (ch + 1)], AF.Square), "act")

                @block.tensor
                def _(t, i=i):
                    ctx.wait(t, "act")
                    mm = t.matmul(p_st[:, :], E["ones_col_ref"][:, 0:1], sq5[:, :],
                                  start=(i == 0), stop=(i == NT - 1))
                    ctx.tick(mm, "pe")

            @block.vector
            def _(v):
                ctx.wait(v, "pe")
                v.tensor_scalar(rowf[:, 0:512], p_st[:, :], 1.0 / D, EPS_RMS, ALU.mult, ALU.add)
                ctx.tick(v.tensor_copy(rowf[:, 512:1024], rowf[:, 0:512]), "dve")

            @block.scalar
            def _(sc):
                ctx.wait(sc, "dve")
                ctx.tick(sc.activation(rowf[:, 1024:1536], rowf[:, 0:512], AF.Sqrt), "act")

            @block.vector
            def _(v):
                ctx.wait(v, "act")
                v.reciprocal(rowf[:, 512:1024], rowf[:, 1024:1536])
                v.drain()
                ctx.tick(_newton_rsqrt(v, rowf[:, 512:1024], rowf[:, 0:512], rowf[:, 1024:1536]), "dve")

            @block.sync
            def _(sync, ch=ch):
                ctx.wait(sync, "dve")
                ctx.dma(sync, bass.AP(rows_dram, 4 * S, [[S, 1], [1, 512]]), rowf[:, 512:1024])
                sync.wait_ge(ctx.sems["ld"], ctx.cnt["ld"])
                ctx.dma(sync, r2bc[:, :], bass.AP(rows_dram, 4 * S, [[0, 128], [1, 512]]))

            for i in range(NT):
                @block.vector
                def _(v, i=i, ch=ch):
                    if i == 0:
                        ctx.wait(v, "ld")
                    ctx.tick(v.scalar_tensor_tensor(nrm[i][:, :], big[i][:, 512 * ch:512 * (ch + 1)],
                                                    ln2c[:, i:i + 1], r2bc[:, :], ALU.mult, ALU.mult), "dve")

            for hg in range(4):
                @block.gpsimd
                def _(g, hg=hg, ch=ch):
                    ctx.wait(g, "pe")
                    for i in range(NT):
                        ctx.dma(g, w1g[i][:, :], E["w1T"].ap()[128 * i:128 * (i + 1),
                                                               1024 * hg:1024 * (hg + 1)], sem="gd")

                for hl in range(8):
                    ht = hg * 8 + hl
                    pp = ht % 2

                    @block.tensor
                    def _(t, hg=hg, hl=hl, pp=pp, ht=ht):
                        ctx.wait(t, "gd", "dve")
                        if ht >= 2:
                            ctx.wait(t, "act")
                        for i in range(NT):
                            mm = t.matmul(p_o[pp][:, :], w1g[i][:, 128 * hl:128 * (hl + 1)],
                                          nrm[i][:, :], start=(i == 0), stop=(i == NT - 1))
                        ctx.tick(mm, "pe")

                    @block.scalar
                    def _(sc, ht=ht, pp=pp):
                        ctx.wait(sc, "pe")
                        ctx.tick(sc.activation(htile[:, 512 * ht:512 * (ht + 1)], p_o[pp][:, :],
                                               AF.Gelu, bias=b1col[:, ht:ht + 1]), "act")

            for dt in range(NT):
                pp = dt % 2

                @block.gpsimd
                def _(g, dt=dt, ch=ch):
                    ctx.wait(g, "pe")
                    ctx.dma(g, w2g[:, :],
                            bass.AP(E["w2T"], 128 * dt, [[1024, 128], [128 * 1024, 32], [1, 128]]),
                            sem="gd")

                @block.sync
                def _(sync, dt=dt, ch=ch):
                    ctx.wait(sync, "dve", "ld")
                    ctx.dma(sync, sbo[dt][:, :],
                            bass.AP(sblk_dram, dt * 128 * 2048 + 512 * ch, [[2048, 128], [1, 512]]))

                @block.tensor
                def _(t, dt=dt, pp=pp):
                    ctx.wait(t, "gd", "act", "dve")
                    for ht in range(32):
                        mm = t.matmul(p_o[pp][:, :], w2g[:, 128 * ht:128 * (ht + 1)],
                                      htile[:, 512 * ht:512 * (ht + 1)],
                                      start=(ht == 0), stop=(ht == 31))
                    ctx.tick(mm, "pe")

                @block.vector
                def _(v, dt=dt, ch=ch, pp=pp):
                    ctx.wait(v, "pe", "ld")
                    ctx.tick(v.scalar_tensor_tensor(sbo[dt][:, :], p_o[pp][:, :], b2col[:, dt:dt + 1],
                                                    sbo[dt][:, :], ALU.add, ALU.add), "dve")

                @block.sync
                def _(sync, dt=dt, ch=ch):
                    ctx.wait(sync, "dve")
                    ctx.dma(sync, bass.AP(sblk_dram, dt * 128 * 2048 + 512 * ch, [[2048, 128], [1, 512]]),
                            sbo[dt][:, :])



QS = 0.08838834764831845


def _moa(nc, ctx, block, E, big, big_all, cfg_sb, ones_col):
    sp_xn, rows_dram, sm6 = E["sp_xn"], E["rows_dram"], E["sm6_dram"]
    kraw_hbm, kmix_hbm, vmoa_hbm = E["kraw_hbm"], E["kmix_hbm"], E["vmoa_hbm"]
    with ExitStack() as s6:
        sb6 = lambda n, s, d=F32: s6.enter_context(nc.sbuf_tensor(n, s, d))
        qt = [sb6(f"qt{e}", [128, 2048], BF16) for e in range(NT)]
        gates_sb = sb6("gates_sb", [128, 128])
        gT = sb6("gT", [8, 2048], BF16)
        relTb = sb6("relTb", [128, 129], BF16)
        tmpA = sb6("tmpA", [128, 512])

        with ExitStack() as sp6:
            sbp = lambda n, s, d=F32: sp6.enter_context(nc.sbuf_tensor(n, s, d))
            kwt = [sbp(f"kwt{i}", [128, 128], BF16) for i in range(NT)]
            vwt = [sbp(f"vwt{i}", [128, 128], BF16) for i in range(NT)]
            agwt = [sbp(f"agwt{i}", [128, 8], BF16) for i in range(NT)]
            qwt = [sbp(f"qwt{e}", [128, 8 * 128], BF16) for e in range(NT)]
            xnb = [sbp(f"xnb{i}", [128, 512], BF16) for i in range(NT)]
            relT_sb = sbp("relT_sb", [128, 129])
            cols6 = sbp("cols6", [128, 32])
            fl1b_bf = sbp("fl1b_bf", [128, 8], BF16)
            ucol = sbp("ucol", [128, 8])
            u0q_c = sbp("u0q_c", [128, 8])
            u0v_bc = sbp("u0v_bc", [128, 129], BF16)
            u0g_bc = sbp("u0g_bc", [128, 8])
            qbcol = sbp("qbcol", [128, 8])
            r2bc = sbp("r2bc6", [128, 512])
            r2col = sbp("r2col", [128, 4])
            kstg = sbp("kstg", [128, 512], BF16)
            vstg = sbp("vstg", [128, 144], BF16)
            zero6 = sbp("zero6", [128, 1280], BF16)
            lgt = sbp("lgt", [128, 32])
            p_a = [sp6.enter_context(nc.psum_tensor(f"p_a{i}", [128, 512], F32)) for i in range(2)]
            p_b = sp6.enter_context(nc.psum_tensor("p_b", [128, 144], F32))

            CFL1W, CFL1B, CKB, CVB, CFL1W8 = 0, 8, 16, 17, 18

            @block.sync
            def _(sync):
                ctx.wait(sync, "dve", "act", "pe")
                for i in range(NT):
                    ctx.dma(sync, cols6[:, CFL1W + i:CFL1W + i + 1], bass.AP(E["fln1w"], 128 * i, [[1, 128], [1, 1]]))
                    ctx.dma(sync, cols6[:, CFL1B + i:CFL1B + i + 1], bass.AP(E["fln1b"], 128 * i, [[1, 128], [1, 1]]))
                ctx.dma(sync, cols6[:, CKB:CKB + 1], bass.AP(E["kb"], 0, [[1, 128], [1, 1]]))
                ctx.dma(sync, cols6[:, CVB:CVB + 1], bass.AP(E["vb"], 0, [[1, 128], [1, 1]]))
                for e in range(NT):
                    ctx.dma(sync, qbcol[:, e:e + 1], bass.AP(E["qb"], 128 * e, [[1, 128], [1, 1]]))
                ctx.dma(sync, relT_sb[:, 0:129], E["relT"].ap())

            @block.gpsimd
            def _(g):
                ctx.wait(g, "ld", "dve", "act", "pe")
                for i in range(NT):
                    ctx.dma(g, kwt[i][:, :], E["kw"].ap()[128 * i:128 * (i + 1), :], sem="gd")
                    ctx.dma(g, vwt[i][:, :], E["vw"].ap()[128 * i:128 * (i + 1), :], sem="gd")
                    ctx.dma(g, agwt[i][:, :], E["agw"].ap()[128 * i:128 * (i + 1), :], sem="gd")
                for e in range(NT):
                    for i in range(NT):
                        ctx.dma(g, qwt[e][:, 128 * i:128 * (i + 1)],
                                E["qw"].ap()[e][128 * i:128 * (i + 1), :], sem="gd")

            @block.vector
            def _(v):
                ctx.wait(v, "ld", "gd")
                v.memset(zero6[:, :], 0.0)
                v.memset(vstg[:, :], 0.0)
                v.tensor_scalar(cols6[:, CFL1W8:CFL1W8 + 8], cols6[:, CFL1W:CFL1W + 8], QS, None, ALU.mult)
                v.tensor_copy(fl1b_bf[:, :], cols6[:, CFL1B:CFL1B + 8])
                ctx.tick(v.tensor_copy(relTb[:, 0:129], relT_sb[:, 0:129]), "dve")

            @block.tensor
            def _(t):
                ctx.wait(t, "dve", "gd")
                for i in range(NT):
                    mk = t.matmul(p_a[0][:, 0:1], kwt[i][:, :], fl1b_bf[:, i:i + 1],
                                  start=(i == 0), stop=(i == NT - 1))
                ctx.tick(mk, "pe")
                for i in range(NT):
                    mv = t.matmul(p_a[1][:, 0:1], vwt[i][:, :], fl1b_bf[:, i:i + 1],
                                  start=(i == 0), stop=(i == NT - 1))
                ctx.tick(mv, "pe")
                for e in range(NT):
                    for i in range(NT):
                        mq = t.matmul(p_b[:, e:e + 1], qwt[e][:, 128 * i:128 * (i + 1)], fl1b_bf[:, i:i + 1],
                                      start=(i == 0), stop=(i == NT - 1))
                ctx.tick(mq, "pe")
                for i in range(NT):
                    mg = t.matmul(p_b[0:8, 128:129], agwt[i][:, :], fl1b_bf[:, i:i + 1],
                                  start=(i == 0), stop=(i == NT - 1))
                ctx.tick(mg, "pe")

            @block.vector
            def _(v):
                ctx.wait(v, "pe")
                v.tensor_scalar(ucol[:, 0:1], p_a[0][:, 0:1], cols6[:, CKB:CKB + 1], None, ALU.add)
                v.tensor_scalar(ucol[:, 1:2], p_a[1][:, 0:1], cols6[:, CVB:CVB + 1], None, ALU.add)
                v.memset(ucol[:, 4:5], 0.0)
                v.tensor_tensor(u0q_c[:, :], p_b[:, 0:8], qbcol[:, :], ALU.add)
                v.drain()
                v.tensor_scalar(u0q_c[:, :], u0q_c[:, :], QS, None, ALU.mult)
                ctx.tick(v.tensor_copy(ucol[0:8, 3:4], p_b[0:8, 128:129]), "dve")

            @block.sync
            def _(sync):
                ctx.wait(sync, "dve")
                ctx.dma(sync, bass.AP(sm6, 0, [[1, 64], [1, 1]]), ucol[0:64, 1:2])
                ctx.dma(sync, bass.AP(sm6, 65, [[1, 64], [1, 1]]), ucol[64:128, 1:2])
                ctx.dma(sync, bass.AP(sm6, 64, [[1, 1], [1, 1]]), ucol[0:1, 4:5])
                ctx.dma(sync, bass.AP(sm6, 192, [[1, 8], [1, 1]]), ucol[0:8, 3:4])
                ctx.dma(sync, bass.AP(sm6, 256, [[1, 128], [1, 1]]), relT_sb[:, 0:1])
                ctx.dma(sync, bass.AP(sm6, 384, [[1, 128], [1, 1]]), relT_sb[:, 128:129])
                sync.wait_ge(ctx.sems["ld"], ctx.cnt["ld"])
                ctx.dma(sync, u0g_bc[:, :], bass.AP(sm6, 192, [[0, 128], [1, 8]]))

            @block.gpsimd
            def _(g):
                ctx.wait(g, "ld")
                ctx.dma(g, u0v_bc[:, :], bass.AP(sm6, 0, [[0, 128], [1, 129]]), sem="gd")

            @block.vector
            def _(v):
                ctx.wait(v, "ld", "dve")
                for i in range(NT):
                    v.tensor_scalar(kwt[i][:, :], kwt[i][:, :], cols6[:, CFL1W + i:CFL1W + i + 1], None, ALU.mult)
                    v.tensor_scalar(vwt[i][:, :], vwt[i][:, :], cols6[:, CFL1W + i:CFL1W + i + 1], None, ALU.mult)
                    v.tensor_scalar(agwt[i][:, :], agwt[i][:, :], cols6[:, CFL1W + i:CFL1W + i + 1], None, ALU.mult)
                    for e in range(NT):
                        mm = v.tensor_scalar(qwt[e][:, 128 * i:128 * (i + 1)], qwt[e][:, 128 * i:128 * (i + 1)],
                                             cols6[:, CFL1W8 + i:CFL1W8 + i + 1], None, ALU.mult)
                ctx.tick(mm, "dve")

            for ch in range(16):
                @block.gpsimd
                def _(g, ch=ch):
                    ctx.wait(g, "pe", "dve")
                    for i in range(NT):
                        ctx.dma(g, xnb[i][:, :], bass.AP(sp_xn, i * 128 * S + 512 * ch, [[S, 128], [1, 512]]),
                                sem="gd")

                @block.sync
                def _(sync, ch=ch):
                    ctx.wait(sync, "dve")
                    ctx.dma(sync, r2bc[:, :], bass.AP(rows_dram, 3 * S + 512 * ch, [[0, 128], [1, 512]]))
                    for tt in range(4):
                        ctx.dma(sync, r2col[:, tt:tt + 1],
                                bass.AP(rows_dram, 3 * S + 512 * ch + 128 * tt, [[1, 128], [1, 1]]))

                @block.tensor
                def _(t, ch=ch):
                    ctx.wait(t, "gd", "ld", "dve", "act")
                    for i in range(NT):
                        mm = t.matmul(p_a[0][:, :], kwt[i][:, :], xnb[i][:, :],
                                      start=(i == 0), stop=(i == NT - 1))
                    ctx.tick(mm, "pe")

                @block.vector
                def _(v, ch=ch):
                    ctx.wait(v, "pe", "ld")
                    v.tensor_tensor(tmpA[:, :], p_a[0][:, :], r2bc[:, :], ALU.mult)
                    v.drain()
                    ctx.tick(v.tensor_scalar(kstg[:, :], tmpA[:, :], ucol[:, 0:1], None, ALU.add), "dve")

                @block.sync
                def _(sync, ch=ch):
                    ctx.wait(sync, "dve")
                    ctx.dma(sync, kraw_hbm.ap()[:, 512 * ch:512 * (ch + 1)], kstg[:, :])

                for tt in range(4):
                    kt = 4 * ch + tt

                    @block.tensor
                    def _(t, tt=tt, ch=ch):
                        ctx.wait(t, "act", "dve")
                        for i in range(NT):
                            mm = t.matmul(p_b[:, 0:128], xnb[i][:, 128 * tt:128 * (tt + 1)], vwt[i][:, :],
                                          start=(i == 0), stop=(i == NT - 1))
                        ctx.tick(mm, "pe")

                    @block.scalar
                    def _(sc, tt=tt, kt=kt):
                        ctx.wait(sc, "pe", "ld", "dve")
                        sc.activation(vstg[:, 0:64], p_b[:, 0:64], AF.Identity,
                                      scale=r2col[:, tt:tt + 1])
                        sc.activation(vstg[:, 65:129], p_b[:, 64:128], AF.Identity,
                                      scale=r2col[:, tt:tt + 1])
                        ctx.tick(sc.activation(vstg[:, 64:65], zero6[0:128, 0:1], AF.Identity,
                                               bias=1.0), "act")

                    @block.vector
                    def _(v, kt=kt):
                        ctx.wait(v, "act", "gd")
                        ctx.tick(v.tensor_tensor(vstg[:, 0:129], vstg[:, 0:129],
                                                 u0v_bc[:, :], ALU.add), "dve")

                    @block.sync
                    def _(sync, kt=kt):
                        ctx.wait(sync, "dve")
                        ctx.dma(sync, vmoa_hbm.ap()[10 + kt], vstg[:, :])

            @block.sync
            def _(sync):
                for sl in list(range(10)) + list(range(74, 84)):
                    ctx.dma(sync, vmoa_hbm.ap()[sl], zero6[:, 0:144])
                for qg in range(2):
                    ctx.dma(sync, bass.AP(kmix_hbm, qg * 128 * (2560 + S), [[2560 + S, 128], [1, 1280]]),
                            zero6[:, :])
                    ctx.dma(sync, bass.AP(kmix_hbm, qg * 128 * (2560 + S) + 1280 + S, [[2560 + S, 128], [1, 1280]]),
                            zero6[:, :])

            E["moa_env"] = dict(sb6=sbp, qt=qt, xnb=xnb, gates_sb=gates_sb, gT=gT,
                                u0q_c=u0q_c, u0g_bc=u0g_bc, tmpA=tmpA,
                                p_a=p_a, p_b=p_b, agwt=agwt, qwt=qwt, lgt=lgt)
            _moa_q_gates(nc, ctx, block, E, cfg_sb, ones_col)

        E["moa_env2"] = dict(qt=qt, gT=gT, gates_sb=gates_sb, relTb=relTb, tmpA=tmpA, sb6=sb6)
        _moa_attn(nc, ctx, block, E, cfg_sb, ones_col, big)


def _moa_q_gates(nc, ctx, block, E, cfg_sb, ones_col):
    M = E["moa_env"]
    rows_dram, xnown_dram = E["rows_dram"], E["xnown_dram"]
    sp_xn = E["sp_xn"]
    xnb, qt, qwt, agwt = M["xnb"], M["qt"], M["qwt"], M["agwt"]
    gates_sb, gT = M["gates_sb"], M["gT"]
    tmpA, p_a, p_b = M["tmpA"], M["p_a"], M["p_b"]
    u0q_c, u0g_bc, lgt = M["u0q_c"], M["u0g_bc"], M["lgt"]
    sb6 = M["sb6"]
    r2bco = sb6("r2bco", [128, 512])
    r2colo = sb6("r2colo", [128, 4])
    eqs = sb6("eqs", [128, 256])
    dcols = sb6("dcols", [128, 48])

    @block.scalar
    def _(sc):
        ctx.wait(sc, "ld", "dve", "act", "pe", "gd")
        rA = sc.alloc_register("moxA")
        rB = sc.alloc_register("moxB")
        sc.reg_load(rA, cfg_sb[0:1, 0:1])
        sc.reg_load(rB, cfg_sb[0:1, 1:2])
        for k, rr in ((0, rA), (1024, rB)):
            ctx.dma(sc, bass.AP(xnown_dram, k, [[2048, 1024], [1, 1024]]),
                    bass.AP(sp_xn, rr, [[S, 1024], [1, 1024]]), sem="sd")
        r3 = sc.alloc_register("mor3")
        for k, rr in ((0, rA), (1024, rB)):
            sc.reg_alu(r3, rr, 3 * S, ALU.add)
            ctx.dma(sc, bass.AP(rows_dram, 5 * S + k, [[S, 1], [1, 1024]]),
                    bass.AP(rows_dram, r3, [[S, 1], [1, 1024]]), sem="sd")
        sc.wait_ge(ctx.sems["sd"], ctx.cnt["sd"])

    for ch in range(4):
        @block.gpsimd
        def _(g, ch=ch):
            ctx.wait(g, "pe", "dve", "sd")
            for i in range(NT):
                ctx.dma(g, xnb[i][:, :], bass.AP(xnown_dram, i * 128 * 2048 + 512 * ch,
                                                 [[2048, 128], [1, 512]]), sem="gd")

        @block.sync
        def _(sync, ch=ch):
            ctx.wait(sync, "dve", "sd")
            ctx.dma(sync, r2bco[:, :], bass.AP(rows_dram, 5 * S + 512 * ch, [[0, 128], [1, 512]]))
            for tt4 in range(4):
                ctx.dma(sync, r2colo[:, tt4:tt4 + 1],
                        bass.AP(rows_dram, 5 * S + 512 * ch + 128 * tt4, [[1, 128], [1, 1]]))

        for e in range(NT):
            @block.tensor
            def _(t, e=e, ch=ch):
                ctx.wait(t, "gd", "ld", "dve", "act")
                for i in range(NT):
                    mm = t.matmul(p_a[e % 2][:, :], qwt[e][:, 128 * i:128 * (i + 1)], xnb[i][:, :],
                                  start=(i == 0), stop=(i == NT - 1))
                ctx.tick(mm, "pe")

            @block.vector
            def _(v, e=e, ch=ch):
                ctx.wait(v, "pe")
                v.tensor_tensor(tmpA[:, :], p_a[e % 2][:, :], r2bco[:, :], ALU.mult)
                v.drain()
                ctx.tick(v.tensor_scalar(qt[e][:, 512 * ch:512 * (ch + 1)], tmpA[:, :],
                                         u0q_c[:, e:e + 1], None, ALU.add), "dve")

        for tt in range(4):
            ti = 4 * ch + tt

            @block.tensor
            def _(t, tt=tt, ti=ti):
                ctx.wait(t, "dve", "act")
                for i in range(NT):
                    mm = t.matmul(p_b[:, 0:8], xnb[i][:, 128 * tt:128 * (tt + 1)], agwt[i][:, :],
                                  start=(i == 0), stop=(i == NT - 1))
                ctx.tick(mm, "pe")

            @block.vector
            def _(v, tt=tt, ti=ti):
                ctx.wait(v, "pe")
                lg = lgt[:, 0:8]
                v.tensor_scalar(lg, p_b[:, 0:8], r2colo[:, tt:tt + 1], None, ALU.mult)
                v.drain()
                v.tensor_tensor(lg, lg, u0g_bc[:, :], ALU.add)
                v.drain()
                m1, m2, msk = lgt[:, 8:9], lgt[:, 9:10], lgt[:, 10:18]
                v.reduce_max(m1, lg, AX.X)
                v.drain()
                v.tensor_scalar(eqs[:, 16 * ti:16 * ti + 8], lg, m1, None, ALU.is_equal)
                v.drain()
                v.scalar_tensor_tensor(msk, eqs[:, 16 * ti:16 * ti + 8], -1e30, lg, ALU.mult, ALU.add)
                v.drain()
                v.reduce_max(m2, msk, AX.X)
                v.drain()
                v.tensor_scalar(eqs[:, 16 * ti + 8:16 * ti + 16], lg, m2, None, ALU.is_equal)
                ctx.tick(v.tensor_tensor(dcols[:, ti:ti + 1], m2, m1, ALU.subtract), "dve")

    @block.scalar
    def _(sc):
        ctx.wait(sc, "dve")
        ctx.tick(sc.activation(dcols[:, 16:32], dcols[:, 0:16], AF.Exp), "act")

    @block.vector
    def _(v):
        ctx.wait(v, "act")
        v.tensor_scalar(dcols[:, 32:48], dcols[:, 16:32], 1.0, None, ALU.add)
        v.drain()
        v.reciprocal(dcols[:, 32:48], dcols[:, 32:48])
        v.drain()
        v.tensor_tensor(dcols[:, 16:32], dcols[:, 16:32], dcols[:, 32:48], ALU.mult)
        v.drain()
        for ti in range(16):
            v.tensor_scalar(gates_sb[:, 8 * ti:8 * (ti + 1)], eqs[:, 16 * ti:16 * ti + 8],
                            dcols[:, 32 + ti:33 + ti], None, ALU.mult)
            v.drain()
            mm = v.scalar_tensor_tensor(gates_sb[:, 8 * ti:8 * (ti + 1)],
                                        eqs[:, 16 * ti + 8:16 * ti + 16],
                                        dcols[:, 16 + ti:17 + ti],
                                        gates_sb[:, 8 * ti:8 * (ti + 1)], ALU.mult, ALU.add)
            v.drain()
        ctx.tick(mm, "dve")

    gTf = sb6("gTf", [8, 2048])
    for ti in range(16):
        @block.tensor
        def _(t, ti=ti):
            ctx.wait(t, "dve")
            ctx.tick(t.matmul(p_b[0:8, 0:128], gates_sb[:, 8 * ti:8 * (ti + 1)],
                              E["identp_sb"][:, :], is_transpose=True, start=True, stop=True), "pe")

        @block.vector
        def _(v, ti=ti):
            ctx.wait(v, "pe")
            v.tensor_copy(gT[0:8, 128 * ti:128 * (ti + 1)], p_b[0:8, 0:128])
            ctx.tick(v.tensor_copy(gTf[0:8, 128 * ti:128 * (ti + 1)], p_b[0:8, 0:128]), "dve")

    @block.sync
    def _(sync):
        ctx.wait(sync, "dve")
        ctx.dma(sync, E["gmoa_dram"].ap(), gTf[:, :])


def _moa_attn(nc, ctx, block, E, cfg_sb, ones_col, big):
    M = E["moa_env2"]
    qt, gT, relTb, tmpA = M["qt"], M["gT"], M["relTb"], M["tmpA"]
    sb6 = M["sb6"]
    sm6, rows_dram = E["sm6_dram"], E["rows_dram"]
    kraw_hbm, kmix_hbm, vmoa_hbm, vpad_hbm = E["kraw_hbm"], E["kmix_hbm"], E["vmoa_hbm"], E["vpad_hbm"]
    xnown_dram, gmoa_dram, dbg = E["xnown_dram"], E["gmoa_dram"], E["dbg"]
    xdown_dram = E["xdown_dram"]
    identp = E["identp_sb"]

    kmix = sb6("kmix", [128, S], BF16)
    vrot = sb6("vrot", [128, 64 * 129], BF16)
    relw = sb6("relw", [128, 512], BF16)
    band = sb6("band", [128, 8 * 384], BF16)
    kband = sb6("kband", [128, 1280], BF16)
    vband = sb6("vband", [128, 10 * 129], BF16)
    rel0r = sb6("rel0r", [1, 128], BF16)
    rel128r = sb6("rel128r", [1, 128], BF16)
    indl_c = sb6("indl_c", [1, 512], BF16)
    indh_c = sb6("indh_c", [1, 512], BF16)
    onesrow = sb6("onesrow", [1, 128], BF16)
    qrel0 = sb6("qrel0", [1, 1024], BF16)
    qrel128 = sb6("qrel128", [1, 1024], BF16)
    Pm = [sb6(f"Pm{i}", [128, 1024], BF16) for i in range(2)]
    fwork = sb6("fwork", [128, 1024])
    rngbc = sb6("rngbc", [128, 1024])
    cAe = sb6("cAe", [65, 1024])
    cBe = sb6("cBe", [65, 1024])
    rrow = sb6("rrow", [1, 1024])
    grow = sb6("grow", [1, 1024])
    negce = sb6("negce", [128, 8])
    nrow = sb6("nrow", [1, 64])
    krf = sb6("krf", [128, 512])
    bnd32 = sb6("bnd32", [128, 384])
    identb = sb6("identb", [128, 128], BF16)
    gc = [sb6(f"gc{e}", [128, 1024], BF16) for e in range(NT)]
    owt = sb6("owt", [128, 1024], BF16)
    obT = sb6("obT", [8, 1024], BF16)

    with ExitStack() as s7:
        p_s = [s7.enter_context(nc.psum_tensor(f"p_s6{i}", [128, 1024], F32)) for i in range(2)]
        p_cA = s7.enter_context(nc.psum_tensor("p_cA", [65, 1024], F32))
        p_cB = s7.enter_context(nc.psum_tensor("p_cB", [65, 1024], F32))

        @block.sync
        def _(sync):
            ctx.wait(sync, "dve", "act", "pe", "gd", "sd")
            sync.wait_ge(ctx.sems["ld"], ctx.cnt["ld"])
            ctx.dma(sync, vrot[:, :].rearrange("p (a w) -> p a w", a=64, w=129),
                    bass.AP(vmoa_hbm, 10 * 128 * 144, [[144, 128], [128 * 144, 64], [1, 129]]))

        @block.gpsimd
        def _(g):
            ctx.wait(g, "dve", "act", "pe", "sd")
            g.wait_ge(ctx.sems["ld"], ctx.cnt["ld"])
            ctx.dma(g, rel0r[:, :], bass.AP(sm6, 256, [[1, 1], [1, 128]]), sem="gd")
            ctx.dma(g, rel128r[:, :], bass.AP(sm6, 384, [[1, 1], [1, 128]]), sem="gd")
            ctx.dma(g, obT[:, :], E["ob"].ap(), sem="gd")

        @block.vector
        def _(v):
            ctx.wait(v, "ld", "pe", "act", "gd", "sd")
            v.memset(onesrow[:, :], 1.0)
            v.tensor_copy(identb[:, :], identp[:, :])
            v.tensor_copy(relw[:, 192:321], relTb[:, 0:129])
            v.tensor_copy(relw[:, 0:192], relTb[:, 0:1].broadcast_to([128, 192]))
            ctx.tick(v.tensor_copy(relw[:, 321:512], relTb[:, 128:129].broadcast_to([128, 191])), "dve")

        def _sq_chunk(src_getter, col, first, width=512):
            @block.scalar
            def _(sc):
                ctx.wait(sc, "pe", "ld", "dve")
                ctx.tick(sc.activation(tmpA[:, 0:width], src_getter(), AF.Square), "act")

            @block.tensor
            def _(t):
                ctx.wait(t, "act")
                ctx.tick(t.matmul(p_s[0][0:1, 0:width], ones_col[:, 0:1], tmpA[:, 0:width],
                                  start=True, stop=True), "pe")

            @block.vector
            def _(v):
                ctx.wait(v, "pe")
                v.reduce_max(nrow[0:1, 63:64], p_s[0][0:1, 0:width], AX.X)
                v.drain()
                if first:
                    mm = v.tensor_copy(nrow[0:1, col:col + 1], nrow[0:1, 63:64])
                else:
                    mm = v.tensor_tensor(nrow[0:1, col:col + 1], nrow[0:1, col:col + 1],
                                         nrow[0:1, 63:64], ALU.max)
                ctx.tick(mm, "dve")

        for e in range(NT):
            for chq in range(4):
                _sq_chunk(lambda e=e, chq=chq: qt[e][:, 512 * chq:512 * (chq + 1)], e, chq == 0)
        for ch in range(16):
            @block.sync
            def _(sync, ch=ch):
                ctx.wait(sync, "act")
                ctx.dma(sync, Pm[1][:, 0:512], kraw_hbm.ap()[:, 512 * ch:512 * (ch + 1)])
            _sq_chunk(lambda: Pm[1][:, 0:512], 16, ch == 0)
        _sq_chunk(lambda: relTb[:, 0:129], 17, True, width=129)

        @block.scalar
        def _(sc):
            ctx.wait(sc, "dve")
            ctx.tick(sc.activation(nrow[0:1, 32:50], nrow[0:1, 0:18], AF.Sqrt), "act")

        @block.vector
        def _(v):
            ctx.wait(v, "act")
            v.tensor_tensor(nrow[0:1, 20:21], nrow[0:1, 48:49], nrow[0:1, 49:50], ALU.add)
            v.drain()
            v.tensor_scalar(nrow[0:1, 24:32], nrow[0:1, 32:40], nrow[0:1, 20:21], None, ALU.mult)
            v.drain()
            ctx.tick(v.tensor_scalar(nrow[0:1, 24:32], nrow[0:1, 24:32], -1.0, None, ALU.mult), "dve")

        @block.sync
        def _(sync):
            ctx.wait(sync, "dve")
            ctx.dma(sync, bass.AP(sm6, 512, [[1, 1], [1, 8]]), nrow[0:1, 24:32])
            sync.wait_ge(ctx.sems["ld"], ctx.cnt["ld"])
            ctx.dma(sync, negce[:, :], bass.AP(sm6, 512, [[0, 128], [1, 8]]))

        for qg in range(2):
            qgs = qg * 1024

            for ch in range(16):
                @block.gpsimd
                def _(g, ch=ch):
                    ctx.wait(g, "dve")
                    ctx.dma(g, krf[:, :], kraw_hbm.ap()[:, 512 * ch:512 * (ch + 1)], sem="gd")

                @block.sync
                def _(sync, ch=ch, qg=qg):
                    ctx.wait(sync, "pe")
                    ctx.dma(sync, indl_c[:, :], bass.AP(E["indl"], qg * S + 512 * ch, [[S, 1], [1, 512]]))
                    ctx.dma(sync, indh_c[:, :], bass.AP(E["indh"], qg * S + 512 * ch, [[S, 1], [1, 512]]))

                @block.tensor
                def _(t, ch=ch, qg=qg):
                    ctx.wait(t, "dve", "act", "gd", "ld")
                    t.matmul(p_s[0][:, 0:512], rel0r[0:1, :], indl_c[0:1, :],
                             start=True, stop=False)
                    ctx.tick(t.matmul(p_s[0][:, 0:512], rel128r[0:1, :], indh_c[0:1, :],
                                      start=False, stop=True), "pe")

                @block.vector
                def _(v, ch=ch):
                    ctx.wait(v, "pe", "gd")
                    ctx.tick(v.tensor_tensor(kmix[:, 512 * ch:512 * (ch + 1)], p_s[0][:, 0:512],
                                             krf[:, :], ALU.add), "dve")

            @block.sync
            def _(sync, qg=qg):
                ctx.wait(sync, "dve")
                ctx.dma(sync, bass.AP(kmix_hbm, qg * 128 * (2560 + S) + 1280, [[2560 + S, 128], [1, S]]),
                        kmix[:, :])

            @block.scalar
            def _(sc, qg=qg):
                ctx.wait(sc, "ld", "pe")
                rk = sc.alloc_register(f"mbk{qg}")
                sc.reg_load(rk, cfg_sb[0:1, 37 + qg:38 + qg])
                ctx.dma(sc, kband[:, :], bass.AP(kmix_hbm, rk, [[2560 + S, 128], [1, 1280]]), sem="sd")
                rv = sc.alloc_register(f"mbv{qg}")
                sc.reg_load(rv, cfg_sb[0:1, 35 + qg:36 + qg])
                ctx.dma(sc, vband[:, :].rearrange("p (a w) -> p a w", a=10, w=129),
                        bass.AP(vmoa_hbm, rv, [[144, 128], [128 * 144, 10], [1, 129]]), sem="sd")
                sc.wait_ge(ctx.sems["sd"], ctx.cnt["sd"])

            vp_pe, vp_dve, vp_ld = [], [], []
            for e in range(NT):
                for qtl in range(8):
                    i = len(vp_pe)

                    @block.tensor
                    def _(t, e=e, qtl=qtl, qg=qg, i=i):
                        if i == 0:
                            ctx.wait(t, "dve", "ld")
                        if i >= 2:
                            t.wait_ge(ctx.sems["dve"], vp_dve[i - 2])
                        ctx.tick(t.matmul(p_s[i % 2][:, 0:512],
                                          qt[e][:, qgs + 128 * qtl:qgs + 128 * (qtl + 1)],
                                          relw[:, :], start=True, stop=True), "pe")
                        vp_pe.append(ctx.cnt["pe"])

                    @block.vector
                    def _(v, e=e, qtl=qtl, i=i):
                        v.wait_ge(ctx.sems["pe"], vp_pe[i])
                        if i >= 2:
                            v.wait_ge(ctx.sems["ld"], vp_ld[i - 2])
                        ctx.tick(v.tensor_copy(Pm[i % 2][:, 0:512], p_s[i % 2][:, 0:512]), "dve")
                        vp_dve.append(ctx.cnt["dve"])

                    @block.sync
                    def _(sync, e=e, qtl=qtl, i=i):
                        sync.wait_ge(ctx.sems["dve"], vp_dve[i])
                        ctx.dma(sync, vpad_hbm.ap()[e][qtl], Pm[i % 2][:, 0:512])
                        vp_ld.append(ctx.cnt["ld"])

            for e in range(NT):
                @block.tensor
                def _(t, e=e, qg=qg):
                    ctx.wait(t, "dve", "ld", "act")
                    for hf in range(2):
                        t.matmul(p_s[0][0:1, 512 * hf:512 * (hf + 1)], relTb[:, 0:1],
                                 qt[e][:, qgs + 512 * hf:qgs + 512 * (hf + 1)], start=True, stop=True)
                        mm = t.matmul(p_s[1][0:1, 512 * hf:512 * (hf + 1)], relTb[:, 128:129],
                                      qt[e][:, qgs + 512 * hf:qgs + 512 * (hf + 1)], start=True, stop=True)
                    ctx.tick(mm, "pe")

                @block.vector
                def _(v, e=e):
                    ctx.wait(v, "pe")
                    v.tensor_copy(qrel0[:, :], p_s[0][0:1, 0:1024])
                    ctx.tick(v.tensor_copy(qrel128[:, :], p_s[1][0:1, 0:1024]), "dve")

                @block.sync
                def _(sync, e=e, qg=qg):
                    ctx.wait(sync, "pe")
                    for qtl in range(8):
                        ctx.dma(sync, band[:, 384 * qtl:384 * (qtl + 1)],
                                bass.AP(vpad_hbm, (e * 8 + qtl) * 65536 + 128, [[511, 128], [1, 384]]))

                acts_l, pes_l = [], []

                def _exp_ctx(j, e=e, qg=qg, acts_l=acts_l, pes_l=pes_l):
                    @block.scalar
                    def _(sc, j=j, e=e):
                        sc.wait_ge(ctx.sems["pe"], pes_l[j])
                        if j == 0:
                            ctx.wait(sc, "dve")
                        ctx.tick(sc.activation(Pm[j % 2][:, :], p_s[j % 2][:, :], AF.Exp,
                                               bias=negce[:, e:e + 1]), "act")
                        acts_l.append(ctx.cnt["act"])

                    @block.tensor
                    def _(t, j=j):
                        t.wait_ge(ctx.sems["act"], acts_l[j])
                        for hf in range(2):
                            t.matmul(p_cA[0:65, 512 * hf:512 * (hf + 1)],
                                     vrot[:, 129 * j:129 * j + 65],
                                     Pm[j % 2][:, 512 * hf:512 * (hf + 1)],
                                     start=(j == 0), stop=False)
                            mm = t.matmul(p_cB[0:65, 512 * hf:512 * (hf + 1)],
                                          vrot[:, 129 * j + 64:129 * j + 129],
                                          Pm[j % 2][:, 512 * hf:512 * (hf + 1)],
                                          start=(j == 0), stop=False)
                        ctx.tick(mm, "pe")

                for kt in range(64):
                    @block.tensor
                    def _(t, e=e, kt=kt, qg=qg):
                        if kt == 0:
                            ctx.wait(t, "dve", "ld", "act", "sd")
                        if kt >= 2:
                            t.wait_ge(ctx.sems["act"], acts_l[kt - 2])
                        t.matmul(p_s[kt % 2][:, 0:512], kmix[:, 128 * kt:128 * (kt + 1)],
                                 qt[e][:, qgs:qgs + 512], start=True, stop=True)
                        ctx.tick(t.matmul(p_s[kt % 2][:, 512:1024], kmix[:, 128 * kt:128 * (kt + 1)],
                                          qt[e][:, qgs + 512:qgs + 1024], start=True, stop=True), "pe")
                        pes_l.append(ctx.cnt["pe"])

                    if kt >= 1:
                        _exp_ctx(kt - 1)
                _exp_ctx(63)

                for bi in range(10):
                    qtls = list(range(max(0, bi - 2), min(8, bi + 1)))

                    @block.vector
                    def _(v, e=e, bi=bi, qtls=qtls):
                        ctx.wait(v, "pe")
                        if bi == 0:
                            ctx.wait(v, "ld", "sd")
                        for ii, qtl in enumerate(qtls):
                            off = 128 * (bi - qtl)
                            mm = v.tensor_copy(bnd32[:, 128 * ii:128 * (ii + 1)],
                                               band[:, 384 * qtl + off:384 * qtl + off + 128])
                        ctx.tick(mm, "dve")

                    @block.tensor
                    def _(t, e=e, bi=bi, qg=qg, qtls=qtls):
                        ctx.wait(t, "act", "dve")
                        if bi == 0:
                            ctx.wait(t, "ld", "sd")
                        t.matmul(p_s[0][:, 0:512], kband[:, 128 * bi:128 * (bi + 1)],
                                 qt[e][:, qgs:qgs + 512], start=True, stop=True)
                        t.matmul(p_s[0][:, 512:1024], kband[:, 128 * bi:128 * (bi + 1)],
                                 qt[e][:, qgs + 512:qgs + 1024], start=True, stop=True)
                        for ii, qtl in enumerate(qtls):
                            t.matmul(p_s[1][:, 128 * qtl:128 * (qtl + 1)],
                                     bnd32[:, 128 * ii:128 * (ii + 1)],
                                     identp[:, :], is_transpose=True, start=True, stop=False)
                        if bi <= 6:
                            lo = 128 * (bi + 1)
                            while lo < 1024:
                                hi = min(1024, lo + (512 - lo % 512))
                                t.matmul(p_s[1][0:128, lo:hi], onesrow[0:1, 0:128],
                                         qrel0[0:1, lo:hi], start=True, stop=False)
                                lo = hi
                        if bi >= 3:
                            lo = 0
                            c1 = 128 * (bi - 2)
                            while lo < c1:
                                hi = min(c1, lo + 512)
                                t.matmul(p_s[1][0:128, lo:hi], onesrow[0:1, 0:128],
                                         qrel128[0:1, lo:hi], start=True, stop=False)
                                lo = hi
                        t.matmul(p_s[1][:, 0:512], kband[:, 128 * bi:128 * (bi + 1)],
                                 qt[e][:, qgs:qgs + 512], start=False, stop=True)
                        last = t.matmul(p_s[1][:, 512:1024], kband[:, 128 * bi:128 * (bi + 1)],
                                        qt[e][:, qgs + 512:qgs + 1024], start=False, stop=True)
                        ctx.tick(last, "pe")

                    @block.scalar
                    def _(sc, e=e, bi=bi):
                        ctx.wait(sc, "dve", "pe")
                        sc.activation(Pm[0][:, :], p_s[1][:, :], AF.Exp, bias=negce[:, e:e + 1])
                        ctx.tick(sc.activation(Pm[1][:, :], p_s[0][:, :], AF.Exp,
                                               bias=negce[:, e:e + 1]), "act")

                    @block.vector
                    def _(v, e=e, bi=bi):
                        ctx.wait(v, "act")
                        ctx.tick(v.tensor_tensor(Pm[0][:, :], Pm[0][:, :], Pm[1][:, :], ALU.subtract), "dve")

                    @block.tensor
                    def _(t, e=e, bi=bi):
                        ctx.wait(t, "dve")
                        for hf in range(2):
                            t.matmul(p_cA[0:65, 512 * hf:512 * (hf + 1)],
                                     vband[:, 129 * bi:129 * bi + 65],
                                     Pm[0][:, 512 * hf:512 * (hf + 1)],
                                     start=False, stop=(bi == 9))
                            mm = t.matmul(p_cB[0:65, 512 * hf:512 * (hf + 1)],
                                          vband[:, 129 * bi + 64:129 * bi + 129],
                                          Pm[0][:, 512 * hf:512 * (hf + 1)],
                                          start=False, stop=(bi == 9))
                        ctx.tick(mm, "pe")

                @block.vector
                def _(v, e=e, qg=qg):
                    ctx.wait(v, "pe", "ld")
                    v.tensor_copy(cAe[:, :], p_cA[0:65, :])
                    v.tensor_copy(cBe[:, :], p_cB[0:65, :])
                    v.drain()
                    ctx.tick(v.reciprocal(rrow[:, :], cBe[0:1, 0:1024]), "dve")

                @block.sync
                def _(sync, e=e, qg=qg):
                    ctx.wait(sync, "dve")
                    ctx.dma(sync, bass.AP(xdown_dram, 0, [[2048, 64], [1, 1024]]), cAe[0:64, :])
                    ctx.dma(sync, bass.AP(xdown_dram, 64 * 2048, [[2048, 64], [1, 1024]]), cBe[1:65, :])
                    ctx.dma(sync, grow[:, :], bass.AP(gmoa_dram, 2048 * e + qgs, [[1, 1], [1, 1024]]))

                @block.vector
                def _(v, e=e):
                    ctx.wait(v, "ld")
                    ctx.tick(v.tensor_tensor(rrow[:, :], rrow[:, :], grow[:, :], ALU.mult), "dve")

                @block.sync
                def _(sync, e=e):
                    ctx.wait(sync, "dve")
                    ctx.dma(sync, bass.AP(sm6, 2048, [[1, 1], [1, 1024]]), rrow[:, :])
                    sync.wait_ge(ctx.sems["ld"], ctx.cnt["ld"])
                    ctx.dma(sync, rngbc[:, :], bass.AP(sm6, 2048, [[0, 128], [1, 1024]]))
                    ctx.dma(sync, fwork[:, :], bass.AP(xdown_dram, 0, [[2048, 128], [1, 1024]]))

                @block.vector
                def _(v, e=e):
                    ctx.wait(v, "ld")
                    ctx.tick(v.tensor_tensor(gc[e][:, :], fwork[:, :], rngbc[:, :], ALU.mult), "dve")

            for dt in range(NT):
                @block.gpsimd
                def _(g, dt=dt, qg=qg):
                    ctx.wait(g, "pe", "dve")
                    ctx.dma(g, owt[:, :].rearrange("p (a w) -> p a w", a=8, w=128),
                            bass.AP(E["ow"], 128 * dt, [[1024, 128], [128 * 1024, 8], [1, 128]]),
                            sem="gd")

                @block.tensor
                def _(t, dt=dt, qg=qg):
                    ctx.wait(t, "gd", "dve")
                    if dt == 0:
                        ctx.wait(t, "act")
                    last = None
                    for hf in range(2):
                        for e in range(NT):
                            t.matmul(p_s[0][:, 512 * hf:512 * (hf + 1)],
                                     owt[:, 128 * e:128 * (e + 1)],
                                     gc[e][:, 512 * hf:512 * (hf + 1)],
                                     start=(e == 0), stop=False)
                        last = t.matmul(p_s[0][:, 512 * hf:512 * (hf + 1)],
                                        obT[0:8, 128 * dt:128 * (dt + 1)],
                                        gT[0:8, qgs + 512 * hf:qgs + 512 * (hf + 1)],
                                        start=False, stop=True)
                    ctx.tick(last, "pe")

                @block.sync
                def _(sync, dt=dt, qg=qg):
                    ctx.wait(sync, "pe", "dve")
                    ctx.dma(sync, krf[:, :], bass.AP(xnown_dram, 2048 * 128 * dt + qgs, [[2048, 128], [1, 512]]))
                    ctx.dma(sync, tmpA[:, :], bass.AP(xnown_dram, 2048 * 128 * dt + qgs + 512,
                                                      [[2048, 128], [1, 512]]))

                @block.vector
                def _(v, dt=dt, qg=qg):
                    ctx.wait(v, "ld", "pe")
                    v.tensor_tensor(big[dt][:, qgs:qgs + 512], p_s[0][:, 0:512], krf[:, :], ALU.add)
                    ctx.tick(v.tensor_tensor(big[dt][:, qgs + 512:qgs + 1024], p_s[0][:, 512:1024],
                                             tmpA[:, :], ALU.add), "dve")

        if DEBUG:
            @block.sync
            def _(sync):
                ctx.wait(sync, "dve", "pe", "act")
                for i in range(NT):
                    ctx.dma(sync, dbg["x2"][128 * i:128 * (i + 1), :], big[i][:, 0:2048])
                ctx.dma(sync, dbg["qt"], qt[0][:, :])
                ctx.dma(sync, dbg["kmix"], kmix[:, 2048:3072])
                ctx.dma(sync, dbg["negce"], negce[:, :])
                ctx.dma(sync, dbg["gctx"], gc[0][:, :])


def _moe_final(nc, ctx, block, E, big, big_all, cfg_sb, ones_col):
    rows_dram, sm6 = E["rows_dram"], E["sm6_dram"]
    fg_dram, sblk_dram, xown_dram = E["fg_dram"], E["sblk_dram"], E["xown_dram"]
    dbg = E["dbg"]
    with ExitStack() as s8:
        sb8 = lambda n, s, d=F32: s8.enter_context(nc.sbuf_tensor(n, s, d))
        xn3 = [sb8(f"xn3_{i}", [128, CW], BF16) for i in range(NT)]
        w1s = [sb8(f"w1s{i}", [128, 4096], BF16) for i in range(2)]
        w2s = [sb8(f"w2s{i}", [128, 4096], BF16) for i in range(2)]
        fgT = sb8("fgT", [16, CW], BF16)
        fgr = sb8("fgr", [16, CW])
        fgwb = sb8("fgwb", [128, 128], BF16)
        fb2b = sb8("fb2b", [16, 1024], BF16)
        fcols = sb8("fcols", [128, 16])
        fb1c = sb8("fb1c", [128, 64])
        hgb = sb8("hgb", [128, 4 * 512], BF16)
        gbc = sb8("gbc", [128, 512])
        mu_bc = sb8("mu_bc", [128, 512])
        r3_bc = sb8("r3_bc", [128, 512])
        rowm = sb8("rowm", [1, CW])
        rowv = sb8("rowv", [1, CW])
        rowr = sb8("rowr", [1, CW])
        rowt = sb8("rowt", [1, CW])
        sq8 = sb8("sq8", [128, 512])
        hq = sb8("hq", [128, 512])
        eqs8 = sb8("eqs8", [128, 512])
        dcol8 = sb8("dcol8", [128, 80])
        outs = sb8("outs", [128, 512])
        sfin_t = sb8("sfin_t", [128, 512])
        xow_t = sb8("xow_t", [128, 512])

        p_r1 = s8.enter_context(nc.psum_tensor("p_r1", [1, 512], F32))
        p_r2 = s8.enter_context(nc.psum_tensor("p_r2", [1, 512], F32))
        p_g8 = s8.enter_context(nc.psum_tensor("p_g8", [128, 128], F32))
        p_h8 = [s8.enter_context(nc.psum_tensor(f"p_h8{i}", [128, 512], F32)) for i in range(2)]
        p_o8 = [s8.enter_context(nc.psum_tensor(f"p_o8{i}", [128, 512], F32)) for i in range(2)]

        for cc in range(4):
            for i in range(NT):
                @block.scalar
                def _(sc, i=i, cc=cc):
                    ctx.wait(sc, "pe", "dve")
                    ctx.tick(sc.activation(sq8[:, :], big[i][:, 512 * cc:512 * (cc + 1)], AF.Square), "act")

                @block.tensor
                def _(t, i=i, cc=cc):
                    ctx.wait(t, "act", "dve")
                    t.matmul(p_r1[:, :], ones_col[:, 0:1], big[i][:, 512 * cc:512 * (cc + 1)],
                             start=(i == 0), stop=(i == NT - 1))
                    ctx.tick(t.matmul(p_r2[:, :], ones_col[:, 0:1], sq8[:, :],
                                      start=(i == 0), stop=(i == NT - 1)), "pe")

            @block.vector
            def _(v, cc=cc):
                ctx.wait(v, "pe")
                sl = slice(512 * cc, 512 * (cc + 1))
                v.tensor_scalar(rowm[0:1, sl], p_r1[:, :], 1.0 / D, None, ALU.mult)
                v.drain()
                v.tensor_tensor(rowt[0:1, 0:512], rowm[0:1, sl], rowm[0:1, sl], ALU.mult)
                v.drain()
                v.tensor_scalar(rowv[0:1, sl], p_r2[:, :], 1.0 / D, EPS_LN, ALU.mult, ALU.add)
                v.drain()
                ctx.tick(v.tensor_tensor(rowv[0:1, sl], rowv[0:1, sl], rowt[0:1, 0:512],
                                         ALU.subtract), "dve")

            @block.scalar
            def _(sc, cc=cc):
                ctx.wait(sc, "dve")
                sl = slice(512 * cc, 512 * (cc + 1))
                ctx.tick(sc.activation(rowt[0:1, sl], rowv[0:1, sl], AF.Sqrt), "act")

            @block.vector
            def _(v, cc=cc):
                ctx.wait(v, "act")
                sl = slice(512 * cc, 512 * (cc + 1))
                v.reciprocal(rowr[0:1, sl], rowt[0:1, sl])
                v.drain()
                ctx.tick(_newton_rsqrt(v, rowr[0:1, sl], rowv[0:1, sl], rowt[0:1, sl]), "dve")

        @block.sync
        def _(sync):
            ctx.wait(sync, "dve")
            ctx.dma(sync, bass.AP(rows_dram, 4 * S, [[S, 1], [1, 2048]]), rowm[0:1, :])
            ctx.dma(sync, bass.AP(rows_dram, 4 * S + 2048, [[S, 1], [1, 2048]]), rowr[0:1, :])

        @block.scalar
        def _(sc):
            ctx.wait(sc, "ld", "dve", "act", "pe", "gd")
            rA = sc.alloc_register("xoA")
            rB = sc.alloc_register("xoB")
            sc.reg_load(rA, cfg_sb[0:1, 0:1])
            sc.reg_load(rB, cfg_sb[0:1, 1:2])
            for k, rr in ((0, rA), (1024, rB)):
                ctx.dma(sc, bass.AP(xown_dram, k, [[2048, 1024], [1, 1024]]),
                        bass.AP(E["x_ch"], rr, [[S, 1024], [1, 1024]]), sem="sd")
            sc.wait_ge(ctx.sems["sd"], ctx.cnt["sd"])

        @block.sync
        def _(sync):
            ctx.wait(sync, "dve", "act")
            for e16 in range(16):
                for ft4 in range(4):
                    ctx.dma(sync, fb1c[:, 4 * e16 + ft4:4 * e16 + ft4 + 1],
                            bass.AP(E["fb1"], 512 * e16 + 128 * ft4, [[1, 128], [1, 1]]))
            for i in range(NT):
                ctx.dma(sync, fcols[:, i:i + 1], bass.AP(E["fln2w"], 128 * i, [[1, 128], [1, 1]]))
                ctx.dma(sync, fcols[:, 8 + i:9 + i], bass.AP(E["fln2b"], 128 * i, [[1, 128], [1, 1]]))

        @block.gpsimd
        def _(g):
            ctx.wait(g, "dve", "act")
            ctx.dma(g, fgwb[:, :].rearrange("p (a w) -> p a w", a=8, w=16),
                    bass.AP(E["fgw"], 0, [[16, 128], [2048, 8], [1, 16]]), sem="gd")
            ctx.dma(g, fb2b[:, :], E["fb2"].ap(), sem="gd")

        for cc in range(4):
            @block.sync
            def _(sync, cc=cc):
                ctx.wait(sync, "dve")
                sync.wait_ge(ctx.sems["ld"], ctx.cnt["ld"])
                ctx.dma(sync, mu_bc[:, :], bass.AP(rows_dram, 4 * S + 512 * cc, [[0, 128], [1, 512]]))
                ctx.dma(sync, r3_bc[:, :], bass.AP(rows_dram, 4 * S + 2048 + 512 * cc, [[0, 128], [1, 512]]))

            for i in range(NT):
                @block.vector
                def _(v, i=i, cc=cc):
                    ctx.wait(v, "ld")
                    v.tensor_tensor(sq8[:, :], big[i][:, 512 * cc:512 * (cc + 1)], mu_bc[:, :], ALU.subtract)
                    v.drain()
                    v.tensor_tensor(sq8[:, :], sq8[:, :], r3_bc[:, :], ALU.mult)
                    v.drain()
                    ctx.tick(v.tensor_scalar(xn3[i][:, 512 * cc:512 * (cc + 1)], sq8[:, :],
                                             fcols[:, i:i + 1], fcols[:, 8 + i:9 + i],
                                             ALU.mult, ALU.add), "dve")

        for tt in range(16):
            @block.tensor
            def _(t, tt=tt):
                ctx.wait(t, "dve", "ld", "gd")
                for i in range(NT):
                    mm = t.matmul(p_g8[:, 0:16], xn3[i][:, 128 * tt:128 * (tt + 1)],
                                  fgwb[:, 16 * i:16 * (i + 1)], start=(i == 0), stop=(i == NT - 1))
                ctx.tick(mm, "pe")

            @block.vector
            def _(v, tt=tt):
                ctx.wait(v, "pe")
                m1, m2, msk = dcol8[:, 48:49], dcol8[:, 49:50], dcol8[:, 64:80]
                v.reduce_max(m1, p_g8[:, 0:16], AX.X)
                v.drain()
                v.tensor_scalar(eqs8[:, 32 * tt:32 * tt + 16], p_g8[:, 0:16], m1, None, ALU.is_equal)
                v.drain()
                v.scalar_tensor_tensor(msk, eqs8[:, 32 * tt:32 * tt + 16], -1e30, p_g8[:, 0:16],
                                       ALU.mult, ALU.add)
                v.drain()
                v.reduce_max(m2, msk, AX.X)
                v.drain()
                v.tensor_scalar(eqs8[:, 32 * tt + 16:32 * tt + 32], p_g8[:, 0:16], m2, None, ALU.is_equal)
                ctx.tick(v.tensor_tensor(dcol8[:, tt:tt + 1], m2, m1, ALU.subtract), "dve")

        @block.scalar
        def _(sc):
            ctx.wait(sc, "dve")
            ctx.tick(sc.activation(dcol8[:, 16:32], dcol8[:, 0:16], AF.Exp), "act")

        @block.vector
        def _(v):
            ctx.wait(v, "act")
            v.tensor_scalar(dcol8[:, 32:48], dcol8[:, 16:32], 1.0, None, ALU.add)
            v.drain()
            v.reciprocal(dcol8[:, 32:48], dcol8[:, 32:48])
            v.drain()
            v.tensor_tensor(dcol8[:, 16:32], dcol8[:, 16:32], dcol8[:, 32:48], ALU.mult)
            v.drain()
            for tt in range(16):
                v.tensor_scalar(eqs8[:, 32 * tt:32 * tt + 16], eqs8[:, 32 * tt:32 * tt + 16],
                                dcol8[:, 32 + tt:33 + tt], None, ALU.mult)
                v.drain()
                mm = v.scalar_tensor_tensor(eqs8[:, 32 * tt:32 * tt + 16],
                                            eqs8[:, 32 * tt + 16:32 * tt + 32],
                                            dcol8[:, 16 + tt:17 + tt],
                                            eqs8[:, 32 * tt:32 * tt + 16], ALU.mult, ALU.add)
                v.drain()
            ctx.tick(mm, "dve")

        for tt in range(16):
            @block.tensor
            def _(t, tt=tt):
                ctx.wait(t, "dve")
                ctx.tick(t.matmul(p_g8[0:16, 0:128], eqs8[:, 32 * tt:32 * tt + 16],
                                  E["identp_sb"][:, :], is_transpose=True, start=True, stop=True), "pe")

            @block.vector
            def _(v, tt=tt):
                ctx.wait(v, "pe")
                v.tensor_copy(fgT[0:16, 128 * tt:128 * (tt + 1)], p_g8[0:16, 0:128])
                ctx.tick(v.tensor_copy(fgr[0:16, 128 * tt:128 * (tt + 1)], p_g8[0:16, 0:128]), "dve")

        @block.sync
        def _(sync):
            ctx.wait(sync, "dve")
            ctx.dma(sync, fg_dram.ap(), fgr[:, :])
            sync.wait_ge(ctx.sems["ld"], ctx.cnt["ld"])

        gds, pes8 = {}, {}
        h_pe, h_act, h_tt = [], [], []
        o_pe, o_add = [], []
        gbc_ld = {}
        for e in range(16):
            @block.gpsimd
            def _(g, e=e):
                if e >= 2:
                    g.wait_ge(ctx.sems["pe"], pes8[e - 2])
                else:
                    ctx.wait(g, "pe")
                for ft in range(4):
                    ctx.dma(g, w1s[e % 2][:, 1024 * ft:1024 * (ft + 1)].rearrange(
                        "p (a w) -> p a w", a=8, w=128),
                        bass.AP(E["fw1"], e * 524288 + 128 * ft, [[512, 128], [65536, 8], [1, 128]]),
                        sem="gd")
                    ctx.dma(g, w2s[e % 2][:, 1024 * ft:1024 * (ft + 1)].rearrange(
                        "p (a w) -> p a w", a=8, w=128),
                        bass.AP(E["fw2"], e * 524288 + 131072 * ft, [[1024, 128], [128, 8], [1, 128]]),
                        sem="gd")
                gds[e] = ctx.cnt["gd"]

            for cc in range(4):
                @block.sync
                def _(sync, e=e, cc=cc):
                    ctx.wait(sync, "dve")
                    ctx.dma(sync, gbc[:, :], bass.AP(fg_dram, 2048 * e + 512 * cc, [[0, 128], [1, 512]]))
                    gbc_ld[(e, cc)] = ctx.cnt["ld"]

                for ft in range(4):
                    idx = len(h_pe)

                    @block.tensor
                    def _(t, e=e, cc=cc, ft=ft, idx=idx):
                        t.wait_ge(ctx.sems["gd"], gds[e])
                        if idx == 0:
                            ctx.wait(t, "dve", "ld", "act")
                        if idx >= 2:
                            t.wait_ge(ctx.sems["act"], h_act[idx - 2])
                        for i in range(NT):
                            mm = t.matmul(p_h8[ft % 2][:, :],
                                          w1s[e % 2][:, 1024 * ft + 128 * i:1024 * ft + 128 * (i + 1)],
                                          xn3[i][:, 512 * cc:512 * (cc + 1)],
                                          start=(i == 0), stop=(i == NT - 1))
                        ctx.tick(mm, "pe")
                        h_pe.append(ctx.cnt["pe"])

                    @block.scalar
                    def _(sc, e=e, cc=cc, ft=ft, idx=idx):
                        sc.wait_ge(ctx.sems["pe"], h_pe[idx])
                        if idx >= 1:
                            sc.wait_ge(ctx.sems["dve"], h_tt[idx - 1])
                        ctx.tick(sc.activation(hq[:, :], p_h8[ft % 2][:, :], AF.Relu,
                                               bias=fb1c[:, 4 * e + ft:4 * e + ft + 1]), "act")
                        h_act.append(ctx.cnt["act"])

                    @block.vector
                    def _(v, e=e, cc=cc, ft=ft, idx=idx):
                        v.wait_ge(ctx.sems["act"], h_act[idx])
                        v.wait_ge(ctx.sems["ld"], gbc_ld[(e, cc)])
                        ctx.tick(v.tensor_tensor(hgb[:, 512 * ft:512 * (ft + 1)], hq[:, :], gbc[:, :],
                                                 ALU.mult), "dve")
                        h_tt.append(ctx.cnt["dve"])

                for dt in range(NT):
                    odx = len(o_pe)

                    @block.tensor
                    def _(t, e=e, cc=cc, dt=dt, odx=odx):
                        t.wait_ge(ctx.sems["dve"], h_tt[-1])
                        if odx >= 2:
                            t.wait_ge(ctx.sems["dve"], o_add[odx - 2])
                        for ft in range(4):
                            mm = t.matmul(p_o8[dt % 2][:, :],
                                          w2s[e % 2][:, 1024 * ft + 128 * dt:1024 * ft + 128 * (dt + 1)],
                                          hgb[:, 512 * ft:512 * (ft + 1)], start=(ft == 0), stop=(ft == 3))
                        ctx.tick(mm, "pe")
                        o_pe.append(ctx.cnt["pe"])

                    @block.vector
                    def _(v, e=e, cc=cc, dt=dt, odx=odx):
                        v.wait_ge(ctx.sems["pe"], o_pe[odx])
                        ctx.tick(v.tensor_tensor(big[dt][:, 512 * cc:512 * (cc + 1)],
                                                 big[dt][:, 512 * cc:512 * (cc + 1)],
                                                 p_o8[dt % 2][:, :], ALU.add), "dve")
                        o_add.append(ctx.cnt["dve"])
            pes8[e] = ctx.cnt["pe"]

        for cc in range(4):
            for dt in range(NT):
                @block.tensor
                def _(t, cc=cc, dt=dt):
                    ctx.wait(t, "dve", "gd")
                    ctx.tick(t.matmul(p_o8[dt % 2][:, :], fb2b[0:16, 128 * dt:128 * (dt + 1)],
                                      fgT[0:16, 512 * cc:512 * (cc + 1)], start=True, stop=True), "pe")

                @block.vector
                def _(v, cc=cc, dt=dt):
                    ctx.wait(v, "pe")
                    ctx.tick(v.tensor_tensor(big[dt][:, 512 * cc:512 * (cc + 1)],
                                             big[dt][:, 512 * cc:512 * (cc + 1)],
                                             p_o8[dt % 2][:, :], ALU.add), "dve")

        for dt in range(NT):
            for cc in range(4):
                @block.scalar
                def _(sc, dt=dt, cc=cc):
                    ctx.wait(sc, "dve")
                    ctx.tick(sc.activation(sq8[:, :], big[dt][:, 512 * cc:512 * (cc + 1)],
                                           AF.Sigmoid), "act")

                @block.sync
                def _(sync, dt=dt, cc=cc):
                    ctx.wait(sync, "dve", "sd")
                    ctx.dma(sync, sfin_t[:, :], bass.AP(sblk_dram, 128 * 2048 * dt + 512 * cc,
                                                        [[2048, 128], [1, 512]]))
                    ctx.dma(sync, xow_t[:, :], bass.AP(xown_dram, 128 * 2048 * dt + 512 * cc,
                                                       [[2048, 128], [1, 512]]))

                @block.vector
                def _(v, dt=dt, cc=cc):
                    ctx.wait(v, "act", "ld", "st")
                    v.tensor_tensor(outs[:, :], big[dt][:, 512 * cc:512 * (cc + 1)], sq8[:, :], ALU.mult)
                    v.drain()
                    v.tensor_tensor(outs[:, :], outs[:, :], sfin_t[:, :], ALU.mult)
                    v.drain()
                    ctx.tick(v.tensor_tensor(outs[:, :], outs[:, :], xow_t[:, :], ALU.add), "dve")

                @block.sync
                def _(sync, dt=dt, cc=cc):
                    ctx.wait(sync, "dve")
                    ctx.dma(sync, E["out_ch"][128 * dt:128 * (dt + 1), 512 * cc:512 * (cc + 1)],
                            outs[:, :], sem="st")

        if DEBUG:
            @block.sync
            def _(sync):
                ctx.wait(sync, "dve", "act", "pe")
                ctx.dma(sync, dbg["xn3"], xn3[0][:, :])
                ctx.dma(sync, dbg["fgT"], fgT[:, :])


def kernel(**inp):
    xs = np.ascontiguousarray(np.asarray(inp["L0_attn_x"], np.float32).transpose(0, 2, 1))
    kk = np.arange(128).reshape(128, 1)
    cc = np.arange(2048).reshape(1, 2048)
    maskm_np = (cc >= kk + 1024).astype(ml_dtypes.bfloat16)
    relT_np = np.ascontiguousarray(np.asarray(inp["L0_ffn_rel_table"], np.float32).T)
    ts = np.arange(S)
    nc = build_program()
    in_maps = []
    for c in range(8):
        b, j = c // 4, c % 4
        q0A, q0B = 1024 * j, 1024 * (7 - j)
        cfg = np.zeros((1, 64), np.int32)
        cfg[0, 0] = q0A
        cfg[0, 1] = q0B
        blocks = ([(0, 1024 * r) for r in range(j)] +
                  [(1, 1024 * r) for r in range(7 - j)] +
                  [(0, 1024 * j), (1, 1024 * (7 - j))])
        pmsk = np.zeros((2, 9), np.float32)
        for pos, (st, k0) in enumerate(blocks):
            cfg[0, 8 + pos] = k0
            cfg[0, 17 + pos] = st * 1024
            cfg[0, 26 + pos] = (k0 // 128) * 128 * 1280
            pmsk[st, pos] = 1.0
        pmsk = pmsk.reshape(-1)
        cfg[0, 35] = (q0A // 128 + 9) * (128 * 144)
        cfg[0, 36] = (q0B // 128 + 9) * (128 * 144)
        cfg[0, 37] = 0 * 128 * (2560 + S) + q0A + 1152
        cfg[0, 38] = 1 * 128 * (2560 + S) + q0B + 1152
        indl_np = np.stack([(ts <= q0A - 129), (ts <= q0B - 129)]).astype(ml_dtypes.bfloat16)
        indh_np = np.stack([(ts >= q0A + 1152), (ts >= q0B + 1152)]).astype(ml_dtypes.bfloat16)
        in_maps.append(dict(
            x_ch=xs[b],
            dd=np.asarray(inp["L0_attn_delta_decay"], np.float32).reshape(1, D),
            cfg=cfg,
            phi=np.asarray(inp["L0_attn_phi_res"], np.float32),
            gwT=np.ascontiguousarray(np.asarray(inp["L0_attn_gate_w"], np.float32).T),
            gb=np.asarray(inp["L0_attn_gate_b"], np.float32).reshape(1, 24),
            ident=np.eye(128, dtype=np.float32),
            wqkvT=np.ascontiguousarray(np.asarray(inp["L0_attn_in_proj_w"], np.float32).T),
            bqkv=np.asarray(inp["L0_attn_in_proj_b"], np.float32).reshape(1, 3072),
            ln1w=np.asarray(inp["L0_attn_ln1_w"], np.float32).reshape(1, D),
            maskm=maskm_np,
            posmask=np.broadcast_to(pmsk.reshape(1, 18), (128, 18)).copy(),
            outwT=np.ascontiguousarray(np.asarray(inp["L0_attn_out_w"], np.float32).T),
            outb=np.asarray(inp["L0_attn_out_b"], np.float32).reshape(1, D),
            w1T=np.ascontiguousarray(np.asarray(inp["L0_attn_w1"], np.float32).T),
            b1=np.asarray(inp["L0_attn_b1"], np.float32).reshape(1, 4096),
            w2T=np.ascontiguousarray(np.asarray(inp["L0_attn_w2"], np.float32).T),
            b2=np.asarray(inp["L0_attn_b2"], np.float32).reshape(1, D),
            ln2w=np.asarray(inp["L0_attn_ln2_w"], np.float32).reshape(1, D),
            fln1w=np.asarray(inp["L0_ffn_ln1_w"], np.float32).reshape(1, D),
            fln1b=np.asarray(inp["L0_ffn_ln1_b"], np.float32).reshape(1, D),
            fln2w=np.asarray(inp["L0_ffn_ln2_w"], np.float32).reshape(1, D),
            fln2b=np.asarray(inp["L0_ffn_ln2_b"], np.float32).reshape(1, D),
            agw=np.asarray(inp["L0_ffn_attn_gate_w"], np.float32),
            qw=np.asarray(inp["L0_ffn_q_w"], np.float32),
            qb=np.asarray(inp["L0_ffn_q_b"], np.float32),
            kw=np.asarray(inp["L0_ffn_k_w"], np.float32),
            kb=np.asarray(inp["L0_ffn_k_b"], np.float32).reshape(1, 128),
            vw=np.asarray(inp["L0_ffn_v_w"], np.float32),
            vb=np.asarray(inp["L0_ffn_v_b"], np.float32).reshape(1, 128),
            ow=np.asarray(inp["L0_ffn_o_w"], np.float32),
            ob=np.asarray(inp["L0_ffn_o_b"], np.float32),
            relT=relT_np,
            indl=indl_np,
            indh=indh_np,
            fgw=np.asarray(inp["L0_ffn_ff_gate_w"], np.float32),
            fw1=np.asarray(inp["L0_ffn_ff_w1"], np.float32),
            fb1=np.asarray(inp["L0_ffn_ff_b1"], np.float32),
            fw2=np.asarray(inp["L0_ffn_ff_w2"], np.float32),
            fb2=np.asarray(inp["L0_ffn_ff_b2"], np.float32),
        ))
    res = run_bass_kernel_spmd(nc, in_maps, core_ids=list(range(8)))
    out = np.zeros((B, S, D), np.float32)
    for c in range(8):
        b, j = c // 4, c % 4
        oc = res.results[c]["out_ch"]
        out[b, 1024 * j:1024 * (j + 1)] = oc[:, 0:1024].T
        out[b, 1024 * (7 - j):1024 * (8 - j)] = oc[:, 1024:2048].T
    kernel._res = res
    return out
